# version 02
# baseline (speedup 1.0000x reference)
import sys
sys.path.insert(0, "/opt/trn_rl_repo")
import numpy as np
import ml_dtypes
import concourse.bass as bass
import concourse.mybir as mybir
from concourse.bass_utils import run_bass_kernel_spmd

F32 = mybir.dt.float32
BF16 = mybir.dt.bfloat16
I32 = mybir.dt.int32
U16 = mybir.dt.uint16
AF = mybir.ActivationFunctionType
ALU = mybir.AluOpType
AX = mybir.AxisListType

DEBUG = False
ENGINES = ("sync", "tensor", "vector", "scalar", "gpsimd")
NEG = -1e30


class Ins:
    __slots__ = ("eng", "build", "deps", "dma", "cc", "sig", "sem", "val",
                 "tag", "gid", "pos")
    def __init__(self, eng, build, dma, cc, tag, gid):
        self.eng = eng; self.build = build; self.dma = dma; self.cc = cc
        self.tag = tag; self.gid = gid
        self.deps = []; self.sig = False; self.sem = None; self.val = None
        self.pos = None

def _bkey(x):
    if hasattr(x, "tensor"):
        return x.tensor.name
    return x.name

class Prog:
    N_DMA_SEMS = 6

    def __init__(self):
        self.instrs = []
        self.streams = {e: [] for e in ENGINES}
        self.last_write = {}
        self.readers = {}
        self.fence_deps = []
        self.relax_same = {"tensor"}

    def emit(self, eng, build, reads=(), writes=(), dma=False, cc=False,
             tag="", deps=()):
        ins = Ins(eng, build, dma, cc, tag, len(self.instrs))
        dd = {}
        for b in reads:
            w = self.last_write.get(_bkey(b))
            if w is not None:
                dd[w.gid] = w
        for b in writes:
            k = _bkey(b)
            w = self.last_write.get(k)
            if w is not None:
                dd[w.gid] = w
            for r in self.readers.get(k, ()):
                dd[r.gid] = r
        for d in deps:
            dd[d.gid] = d
        for d in self.fence_deps:
            dd[d.gid] = d
        ins.deps = [d for d in dd.values() if d is not ins]
        for b in reads:
            self.readers.setdefault(_bkey(b), []).append(ins)
        for b in writes:
            k = _bkey(b)
            self.last_write[k] = ins
            self.readers[k] = []
        self.instrs.append(ins)
        self.streams[eng].append(ins)
        ins.pos = len(self.streams[eng]) - 1
        return ins

    def fence(self):
        self.fence_deps = [s[-1] for s in self.streams.values() if s]
        for d in self.fence_deps:
            d.sig = True

    def _need_sem(self, p, d):
        if p.dma or p.cc or d.dma or d.cc:
            return True
        if p.eng != d.eng:
            return True
        return p.eng not in self.relax_same

    def run(self, nc):
        for ins in self.instrs:
            for p in ins.deps:
                if self._need_sem(p, ins):
                    p.sig = True
        csem = {e: nc.alloc_semaphore(f"c_{e}") for e in ENGINES}
        dsem = {e: [nc.alloc_semaphore(f"d_{e}{i}") for i in range(self.N_DMA_SEMS)]
                for e in ("sync", "scalar", "gpsimd", "vector", "tensor")}
        duse = {e: 0 for e in dsem}
        ccount = {e: 0 for e in ENGINES}
        for e in ENGINES:
            for ins in self.streams[e]:
                if ins.cc:
                    ins.sem = nc.alloc_semaphore(f"cc{ins.gid}")
                    ins.val = 1
                elif ins.dma:
                    i = duse[e] % self.N_DMA_SEMS
                    ins.sem = dsem[e][i]
                    ins.val = 16 * (duse[e] // self.N_DMA_SEMS + 1)
                    duse[e] += 1
                elif ins.sig:
                    ccount[e] += 1
                    ins.sem = csem[e]
                    ins.val = ccount[e]
        prog = self

        def mk_body(e):
            def body(eng):
                water = {}
                for ins in prog.streams[e]:
                    for p in ins.deps:
                        if not prog._need_sem(p, ins):
                            continue
                        assert p.sem is not None, (p.tag, p.eng, ins.tag)
                        k = id(p.sem)
                        if water.get(k, 0) >= p.val:
                            continue
                        eng.wait_ge(p.sem, p.val)
                        water[k] = p.val
                    bi = ins.build(eng)
                    if ins.sem is not None:
                        if ins.dma:
                            bi.then_inc(ins.sem, 16)
                        else:
                            bi.then_inc(ins.sem, 1)
            return body

        with nc.Block() as block:
            block.sync(mk_body("sync"))
            block.tensor(mk_body("tensor"))
            block.vector(mk_body("vector"))
            block.scalar(mk_body("scalar"))
            block.gpsimd(mk_body("gpsimd"))
        return



class Ctx:
    def __init__(self, nc, prog, scratch_bytes=0):
        self.nc = nc
        self.P = prog
        self.tiles = {}
        self.uid = 0
        part_sz = nc.SBUF_PARTITION_SIZE_BYTES
        self.scratch_base = part_sz - scratch_bytes
        self.scratch_off = self.scratch_base
        self._scratch_stack = []

    def sb(self, name, shape, dtype):
        h = self.nc.alloc_sbuf_tensor(name, shape, dtype)
        return h

    def finish_persistent(self):
        r = self.nc.bump_sbuf(32)
        assert r is not None, "sbuf exhausted"
        start = r[0] if isinstance(r, tuple) else r
        assert start <= self.scratch_base, \
            f"persistent region ({start}) collides with scratch base ({self.scratch_base})"

    def push_scratch(self):
        self._scratch_stack.append(self.scratch_off)

    def pop_scratch(self):
        self.scratch_off = self._scratch_stack.pop()

    def scratch(self, name, shape, dtype):
        left = self.nc.bytes_allocated(None) if False else None
        nbytes = int(np.prod(shape[1:])) * _dtb(dtype)
        off = (self.scratch_off + 31) & ~31
        assert off + nbytes <= self.nc.SBUF_PARTITION_SIZE_BYTES, \
            f"scratch overflow {name}: {off + nbytes}"
        h = self.nc.alloc_sbuf_tensor_at(f"{name}_{self.uid}", shape, dtype, offset=off)
        self.uid += 1
        self.scratch_off = off + nbytes
        return h

def _dtb(dt):
    return {F32: 4, I32: 4, BF16: 2, U16: 2}[dt]



def _isbuf(x):
    return hasattr(x, "tensor") or hasattr(x, "name")

def dma(C, eng, out, in_, tag="dma", deps=()):
    return C.P.emit(eng, lambda e, o=out, i=in_: e.dma_start(out=o, in_=i),
                    reads=[in_], writes=[out], dma=True, tag=tag, deps=deps)

def mm(C, out, lhsT, rhs, start, stop, tag="mm", deps=(), extra_reads=()):
    reads = [lhsT, rhs] + list(extra_reads)
    return C.P.emit("tensor",
                    lambda e, o=out, l=lhsT, r=rhs, st=start, sp=stop:
                        e.matmul(o, l, r, start=st, stop=sp),
                    reads=reads, writes=[out], tag=tag, deps=deps)

def act(C, out, in_, func, bias=0.0, scale=1.0, tag="act", deps=(), extra_reads=()):
    reads = [in_] + [x for x in (bias, scale) if _isbuf(x)] + list(extra_reads)
    return C.P.emit("scalar",
                    lambda e, o=out, i=in_, f=func, b=bias, s=scale:
                        e.activation(out=o, in_=i, func=f, bias=b, scale=s),
                    reads=reads, writes=[out], tag=tag, deps=deps)

def ve(C, name, *aps, eng="vector", tag="", deps=(), writes=1, **kw):
    outs = list(aps[:writes]); ins = list(aps[writes:])
    def build(e, n=name, o=outs, i=ins, kw=kw):
        return getattr(e, n)(*o, *i, **kw)
    return C.P.emit(eng, build, reads=[x for x in ins if _isbuf(x)],
                    writes=outs, tag=tag or name, deps=deps)

def tt(C, out, a, b, op, eng="vector", tag="tt", deps=()):
    return C.P.emit(eng, lambda e, o=out, x=a, y=b, p=op: e.tensor_tensor(out=o, in0=x, in1=y, op=p),
                    reads=[a, b], writes=[out], tag=tag, deps=deps)

def ts(C, out, in_, s1, op0, s2=None, op1=None, eng="vector", tag="ts", deps=(), extra_reads=()):
    def build(e, o=out, i=in_, s1=s1, s2=s2, op0=op0, op1=op1):
        if op1 is None:
            return e.tensor_scalar(out=o, in0=i, scalar1=s1, scalar2=None, op0=op0)
        return e.tensor_scalar(out=o, in0=i, scalar1=s1, scalar2=s2, op0=op0, op1=op1)
    reads = [in_] + [x for x in (s1, s2) if hasattr(x, "tensor")] + list(extra_reads)
    return C.P.emit(eng, build, reads=reads, writes=[out], tag=tag, deps=deps)

def cp(C, out, in_, eng="vector", tag="cp", deps=()):
    return C.P.emit(eng, lambda e, o=out, i=in_: e.tensor_copy(out=o, in_=i),
                    reads=[in_], writes=[out], tag=tag, deps=deps)

def memset(C, ap, v, eng="vector", tag="memset", deps=()):
    return C.P.emit(eng, lambda e, a=ap, vv=v: e.memset(a, vv),
                    reads=[], writes=[ap], tag=tag, deps=deps)


D = 2048
KT = 16
TOK = 1024
HALO = 256
TH = TOK + HALO
H0, HKV0, DH = 16, 4, 128
GRP = H0 // HKV0
WIN = 256


def declare_io(nc, upto):
    io = {}
    def inp(name, shape, dt=F32):
        io[name] = nc.dram_tensor(name, shape, dt, kind="ExternalInput").ap()
    inp("xinT", [D, TH])
    inp("cosT", [DH, TH]); inp("sinmT", [DH, TH])
    inp("bmask", [6, DH, 512])
    inp("temp", [1, 16])
    inp("hvalid", [DH, 2])
    inp("wqT", [D, D]); inp("wkvT", [D, 1024]); inp("woT", [D, D])
    io["xoutT"] = nc.dram_tensor("xoutT", [D, TOK], F32, kind="ExternalOutput").ap()
    if DEBUG:
        io["dbg_xnh"] = nc.dram_tensor("dbg_xnh", [D, TH], BF16, kind="ExternalOutput").ap()
        io["dbg_khat"] = nc.dram_tensor("dbg_khat", [DH, TH], BF16, kind="ExternalOutput").ap()
        io["dbg_vtok"] = nc.dram_tensor("dbg_vtok", [128, 512], BF16, kind="ExternalOutput").ap()
        io["dbg_ao"] = nc.dram_tensor("dbg_ao", [DH, TOK], BF16, kind="ExternalOutput").ap()
        io["dbg_q"] = nc.dram_tensor("dbg_q", [2 * DH, TOK], BF16, kind="ExternalOutput").ap()
        io["dbg_w"] = nc.dram_tensor("dbg_w", [DH, KT * 128], BF16, kind="ExternalOutput").ap()
        io["dbg_x2"] = nc.dram_tensor("dbg_x2", [2 * 128, TH], BF16, kind="ExternalOutput").ap()
        io["dbg_z"] = nc.dram_tensor("dbg_z", [2, 512], F32, kind="ExternalOutput").ap()
        io["dbg_p"] = nc.dram_tensor("dbg_p", [2 * DH, 512], BF16, kind="ExternalOutput").ap()
    return io


def build_consts(C, ident=False):
    nc = C.nc
    C.ones_bf_row = C.sb("ones_bf_row", [1, 128], BF16)
    C.ones_f_row = C.sb("ones_f_row", [1, 128], F32)
    C.ones_bf_col = C.sb("ones_bf_col", [128, 1], BF16)
    memset(C, C.ones_bf_row[:], 1.0)
    memset(C, C.ones_f_row[:], 1.0)
    memset(C, C.ones_bf_col[:], 1.0)
    if ident:
        C.ident_bf = C.sb("ident_bf", [128, 128], BF16)
        memset(C, C.ident_bf[:], 1.0)
        C.P.emit("gpsimd",
                 lambda e, o=C.ident_bf[:, :]: e.affine_select(
                     out=o, in_=o, pattern=[[-1, 128]], compare_op=ALU.is_equal,
                     fill=0.0, base=0, channel_multiplier=1),
                 reads=[C.ident_bf], writes=[C.ident_bf], tag="identsel")


def bcast_row(C, psum_ap, row_ap, n, bf=False, tag="bcast"):
    ones = C.ones_bf_row if bf else C.ones_f_row
    return mm(C, psum_ap, ones[:], row_ap, True, True, tag=tag)


def emit_ln(C, blocks, xnh, mrow0, rrow0, pb, tag="ln"):
    nc = C.nc
    tot = sum(w for (_, _, _, w) in blocks)
    C.push_scratch()
    xb = C.scratch(f"{tag}_xb", [128, 512], BF16)
    sq = C.scratch(f"{tag}_sq", [128, 512], BF16)
    mrow = mrow0[0:1, 0:tot]
    rrow = rrow0[0:1, 0:tot]
    mb = C.scratch(f"{tag}_mb", [128, 512], F32)
    rb = C.scratch(f"{tag}_rb", [128, 512], F32)

    chunks = []
    for (tiles, sc, dc, w) in blocks:
        off = 0
        while off < w:
            cw = min(512, w - off)
            chunks.append((tiles, sc + off, dc + off, cw))
            off += cw

    dst_pos = 0
    for (tiles, sc, dc, w) in chunks:
        psA = pb[0][:, 0:w]
        psB = pb[1][:, 0:w]
        for k in range(KT):
            xs = tiles[k][:, sc:sc + w]
            cp(C, xb[:, 0:w], xs, tag=f"{tag}_cast")
            tt(C, sq[:, 0:w], xb[:, 0:w], xb[:, 0:w], ALU.mult, tag=f"{tag}_sq")
            mm(C, psA[0:1, :], C.ones_bf_col[:], xb[:, 0:w], k == 0, k == KT - 1, tag=f"{tag}_sx")
            mm(C, psB[0:1, :], C.ones_bf_col[:], sq[:, 0:w], k == 0, k == KT - 1, tag=f"{tag}_sxx")
        m = mrow[0:1, dst_pos:dst_pos + w]
        r = rrow[0:1, dst_pos:dst_pos + w]
        ts(C, m, psA[0:1, :], 1.0 / D, ALU.mult, tag=f"{tag}_mean")
        ve(C, "scalar_tensor_tensor", r, psA[0:1, :], 1.0 / D, m, tag=f"{tag}_m2",
           op0=ALU.mult, op1=ALU.mult)
        ve(C, "scalar_tensor_tensor", r, psB[0:1, :], 1.0 / D, r, tag=f"{tag}_var",
           op0=ALU.mult, op1=ALU.subtract)
        ts(C, r, r, 1e-5, ALU.add, tag=f"{tag}_eps")
        act(C, r, r, AF.Sqrt, tag=f"{tag}_sqrt")
        ve(C, "reciprocal", r, r, tag=f"{tag}_recip")
        bcast_row(C, pb[2][:, 0:w], m, w, tag=f"{tag}_bm")
        bcast_row(C, pb[3][:, 0:w], r, w, tag=f"{tag}_br")
        cp(C, mb[:, 0:w], pb[2][:, 0:w], tag=f"{tag}_mbe")
        cp(C, rb[:, 0:w], pb[3][:, 0:w], tag=f"{tag}_rbe")
        for k in range(KT):
            xs = tiles[k][:, sc:sc + w]
            dst = xnh[k][:, dc:dc + w]
            tt(C, dst, xs, mb[:, 0:w], ALU.subtract, tag=f"{tag}_sub")
            tt(C, dst, dst, rb[:, 0:w], ALU.mult, tag=f"{tag}_mul")
        dst_pos += w
    C.pop_scratch()


def rope_norm(C, psums, out_bf, coss, sinm, ccols, rs_row, pbb, temp_col,
              pbsq, rp, sqb, tag="rn"):
    for ps, c0 in psums:
        w = ps.shape[1]
        cc = coss[:, ccols + c0: ccols + c0 + w]
        ss = sinm[:, ccols + c0: ccols + c0 + w]
        r = rp[:, 0:w]
        tt(C, r, ps, cc, ALU.mult, tag=f"{tag}_cos")
        ve(C, "scalar_tensor_tensor", sqb[0:64, 0:w], ps[64:128, :], 1.0,
           ss[0:64, :], op0=ALU.mult, op1=ALU.mult, tag=f"{tag}_rot1")
        ve(C, "scalar_tensor_tensor", sqb[64:128, 0:w], ps[0:64, :], 1.0,
           ss[64:128, :], op0=ALU.mult, op1=ALU.mult, tag=f"{tag}_rot2")
        tt(C, r, r, sqb[:, 0:w], ALU.add, tag=f"{tag}_add")
        ob = out_bf[:, c0:c0 + w]
        cp(C, ob, r, tag=f"{tag}_cast")
        tt(C, sqb[:, 0:w], ob, ob, ALU.mult, tag=f"{tag}_sq")
        mm(C, pbsq[0:1, 0:w], C.ones_bf_col[:], sqb[:, 0:w], True, True,
           tag=f"{tag}_ssq")
        rs = rs_row[0:1, 0:w]
        ts(C, rs, pbsq[0:1, 0:w], 1e-24, ALU.add, tag=f"{tag}_eps")
        act(C, rs, rs, AF.Sqrt, tag=f"{tag}_sqrt")
        ve(C, "reciprocal", rs, rs, tag=f"{tag}_recip")
        if temp_col is not None:
            tt(C, rs, rs, temp_col.broadcast_to([1, w]), ALU.mult, tag=f"{tag}_temp")
        bcast_row(C, pbb[:, 0:w], rs, w, tag=f"{tag}_bc")
        tt(C, ob, ob, pbb[:, 0:w], ALU.mult, tag=f"{tag}_nrm")


def load_xin(C, io):
    for k in range(KT):
        dma(C, "sync", C.xown[k][:, :], io["xinT"][k * 128:(k + 1) * 128, HALO:TH],
            tag=f"ldxo{k}")
        dma(C, "gpsimd", C.xhalo[k][:, :], io["xinT"][k * 128:(k + 1) * 128, 0:HALO],
            tag=f"ldxh{k}")


def emit_l0_attn(C, io, pb):
    nc = C.nc
    C.push_scratch()
    obuf = [C.scratch(f"ao{k}", [128, TOK], BF16) for k in range(H0)]
    coss = C.scratch("coss", [DH, TH], BF16)
    sinm = C.scratch("sinm", [DH, TH], BF16)
    bmask = C.scratch("bmask", [DH, 6 * 512], BF16)
    khat = [C.scratch(f"khat{g}", [DH, TH], BF16) for g in range(HKV0)]
    vtok = [C.scratch(f"vtok{t}", [128, 512], BF16) for t in range(TH // 128)]
    tmprow = C.scratch("tmprow", [1, 16], F32)
    hval = C.scratch("hval", [DH, 2], F32)
    rnrp = C.scratch("rnrp", [128, 512], F32)
    rnsq = C.scratch("rnsq", [128, 512], BF16)
    ntB = C.scratch("ntB", [128, 16], F32)
    rowpool = C.scratch("rowpool", [1, 2 * TH + 512], F32)
    rs_row = rowpool[0:1, 0:TH]
    zrow = rowpool[0:1, 2 * TH:2 * TH + 512]
    rzb = C.scratch("rzb", [128, 512], F32)

    dma(C, "gpsimd", coss[:, :], io["cosT"], tag="ldcos")
    dma(C, "gpsimd", sinm[:, :], io["sinmT"], tag="ldsin")
    dma(C, "gpsimd", bmask[:, :],
        io["bmask"].rearrange("m p n -> p m n"), tag="ldbm")
    dma(C, "sync", tmprow[:, :], io["temp"], tag="ldtemp")
    dma(C, "sync", hval[:, :], io["hvalid"], tag="ldhv")
    ntrow = C.scratch("ntrow", [1, 16], F32)
    ts(C, ntrow[0:1, :], tmprow[0:1, :], -1.0, ALU.mult, tag="negt")
    tt(C, ntrow[0:1, :], ntrow[0:1, :], tmprow[0:1, :], ALU.min, tag="negabs")
    bcast_row(C, pb[6][:, 0:16], ntrow[0:1, :], 16, tag="bcN")
    cp(C, ntB[:, :], pb[6][:, 0:16], tag="ntBe")

    C.push_scratch()
    emit_ln(C, [(C.xhalo, 0, 0, HALO), (C.xown, 0, HALO, TOK)], C.xnh,
            rowpool[0:1, 0:TH], rowpool[0:1, TH:2 * TH], pb, tag="ln0")
    C.pop_scratch()
    C.P.fence()

    C.push_scratch()
    wb = [C.scratch(f"wb{i}", [128, KT * 128], BF16) for i in range(2)]
    def load_wchunk(j, src, c0, tg):
        return dma(C, "gpsimd",
                   wb[j].ap().rearrange("p (kt n) -> p kt n", kt=KT),
                   src.rearrange("(kt p) n -> p kt n", p=128)[:, :, c0:c0 + 128],
                   tag=tg)

    cblk = ((0, 256), (256, 512), (768, 512))
    load_wchunk(0, io["wkvT"], 0, "ldwk0")
    for g in range(HKV0):
        if g + 1 < HKV0:
            load_wchunk((g + 1) % 2, io["wkvT"], (g + 1) * 128, f"ldwk{g+1}")
        pc = []
        for bi, (c0, w) in enumerate(cblk):
            ps = pb[bi][:, 0:w]
            for k in range(KT):
                lhsT = wb[g % 2].ap().rearrange("p (kt n) -> p kt n", kt=KT)[:, k, :]
                mm(C, ps, lhsT, C.xnh[k][:, c0:c0 + w], k == 0, k == KT - 1,
                   tag=f"kp{g}")
            pc.append((ps, c0))
        rope_norm(C, pc, khat[g][:, :], coss, sinm, 0, rs_row, pb[4], None,
                  pb[5], rnrp, rnsq, tag=f"kn{g}")
    for half in range(2):
        for j in range(2):
            load_wchunk(j, io["wkvT"], 512 + (2 * half + j) * 128, f"ldwv{half}{j}")
        for tc in range(TH // 128):
            ps = pb[2 + (tc % 2)][:, 0:256]
            for j in range(2):
                for k in range(KT):
                    mm(C, ps[:, j * 128:j * 128 + 128],
                       C.xnh[k][:, tc * 128:tc * 128 + 128],
                       wb[j].ap().rearrange("p (kt n) -> p kt n", kt=KT)[:, k, :],
                       k == 0, k == KT - 1, tag=f"vp{tc}")
            cp(C, vtok[tc][:, half * 256:half * 256 + 256], ps, tag=f"ve{tc}")
    C.pop_scratch()

    C.P.fence()
    if DEBUG:
        for k in range(KT):
            dma(C, "sync", io["dbg_xnh"][k * 128:(k + 1) * 128, :], C.xnh[k][:, :], tag="dbgx")
        dma(C, "sync", io["dbg_khat"], khat[0][:, :], tag="dbgk")
        dma(C, "sync", io["dbg_vtok"], vtok[3][:, :], tag="dbgv")
    C.push_scratch()
    qh = [C.scratch(f"qh{i}", [DH, TOK], BF16) for i in range(2)]
    prb = [C.scratch(f"prb{i}", [128, 512], BF16) for i in range(2)]
    wq = [C.scratch(f"wq{i}", [128, KT * 128], BF16) for i in range(2)]

    def load_wq(h):
        return dma(C, "gpsimd",
                   wq[h % 2].ap().rearrange("p (kt n) -> p kt n", kt=KT),
                   io["wqT"].rearrange("(kt p) n -> p kt n", p=128)[:, :, h * 128:(h + 1) * 128],
                   tag=f"ldwq{h}")

    for h in range(H0):
        load_wq(h)
        C.P.fence()
        pc = []
        for ci, (c0, w) in enumerate(((0, 512), (512, 512))):
            ps = pb[ci][:, 0:w]
            for k in range(KT):
                lhsT = wq[h % 2].ap().rearrange("p (kt n) -> p kt n", kt=KT)[:, k, :]
                mm(C, ps, lhsT, C.xnh[k][:, HALO + c0:HALO + c0 + w],
                   k == 0, k == KT - 1, tag=f"qp{h}")
            pc.append((ps, c0))

        rope_norm(C, pc, qh[h % 2][:, :], coss, sinm, HALO, rs_row, pb[4],
                  tmprow[0:1, h:h + 1], pb[5], rnrp, rnsq, tag=f"qn{h}")
        if DEBUG and h < 2:
            dma(C, "sync", io["dbg_q"][h * DH:(h + 1) * DH, :], qh[h % 2][:, :], tag="dbgq")
        g = h // GRP
        for qb in range(2):
            q_ap = qh[h % 2][:, qb * 512:(qb + 1) * 512]
            ops = pb[2][:, 0:512]
            zps = pb[3][0:1, 0:512]
            kcs = [4 * qb + o for o in range(-2, 4)]
            for i, kc in enumerate(kcs):
                kcol = kc * 128 + HALO
                first, last = i == 0, i == len(kcs) - 1
                sps = (pb[6] if i % 2 == 0 else pb[7])[:, 0:512]
                mm(C, sps, khat[g][:, kcol:kcol + 128], q_ap, True, True,
                   tag=f"sc{h}_{qb}")
                mi = kc - 4 * qb + 2
                tt(C, sps, sps, bmask[:, mi * 512:(mi + 1) * 512],
                   ALU.add, tag=f"msk{h}")
                act(C, prb[i % 2][:, :], sps, AF.Exp,
                    bias=ntB[:, h:h + 1], tag=f"exp{h}")
                if kc < 0:
                    ts(C, prb[i % 2][:, :], prb[i % 2][:, :], hval[:, kc + 2:kc + 3],
                       ALU.mult, tag=f"hv{h}")
                mm(C, zps, C.ones_bf_col[:], prb[i % 2][:, :], first, last,
                   tag=f"z{h}")
                mm(C, ops, vtok[kc + 2][:, g * 128:(g + 1) * 128],
                   prb[i % 2][:, :], first, last, tag=f"av{h}")
            if DEBUG and h < 2 and qb == 0:
                dma(C, "sync", io["dbg_z"][h:h + 1, :], zps, tag="dbgz") if False else None
                cp(C, rzb[:, :], zps.tensor.ap()[0:1, 0:512].broadcast_to([1, 512]), tag="x") if False else None
                cp(C, zrow, zps, tag=f"dbgzc{h}")
                dma(C, "sync", io["dbg_z"][h:h + 1, :], zrow, tag=f"dbgz{h}")

            ve(C, "reciprocal", zrow, zps, tag=f"rz{h}")
            bcast_row(C, pb[0][:, 0:512], zrow, 512, tag=f"bz{h}")
            cp(C, rzb[:, :], pb[0][:, 0:512], tag=f"rzb{h}")
            tt(C, obuf[h][:, qb * 512:(qb + 1) * 512], ops, rzb[:, :], ALU.mult,
               tag=f"oev{h}")
    C.pop_scratch()
    C.P.fence()
    if DEBUG:
        dma(C, "sync", io["dbg_ao"], obuf[0][:, :], tag="dbga")
        for k in range(H0):
            dma(C, "sync", io["dbg_xnh"][k * 128:(k + 1) * 128, 0:TOK], obuf[k][:, :], tag="dbgao2")


    C.push_scratch()
    wo = [C.scratch(f"wo{i}", [128, KT * 128], BF16) for i in range(2)]
    def load_wo(j):
        return dma(C, "gpsimd",
                   wo[j % 2].ap().rearrange("p (kt n) -> p kt n", kt=KT),
                   io["woT"].rearrange("(kt p) n -> p kt n", p=128)[:, :, j * 128:(j + 1) * 128],
                   tag=f"ldwo{j}")
    load_wo(0)
    for oc in range(KT):
        if oc + 1 < KT:
            load_wo(oc + 1)
        for c in range(2):
            ps = pb[(oc * 2 + c) % 4][:, 0:512]
            for i in range(H0):
                lhsT = wo[oc % 2].ap().rearrange("p (kt n) -> p kt n", kt=KT)[:, i, :]
                mm(C, ps, lhsT, obuf[i][:, c * 512:(c + 1) * 512],
                   i == 0, i == H0 - 1, tag=f"wo{oc}")
            tt(C, C.xown[oc][:, c * 512:(c + 1) * 512],
               C.xown[oc][:, c * 512:(c + 1) * 512], ps, ALU.add,
               tag=f"res{oc}")
    C.pop_scratch()
    C.pop_scratch()
    C.P.fence()



def build_l0attn_program():
    nc = bass.Bass("TRN2", target_bir_lowering=False, num_devices=8)
    io = declare_io(nc, "x1")
    prog = Prog()
    C = Ctx(nc, prog, scratch_bytes=95 * 1024)
    C.xown = [C.sb(f"xown{k}", [128, TOK], F32) for k in range(KT)]
    C.xnh = [C.sb(f"xnh{k}", [128, TH], BF16) for k in range(KT)]
    C.xhalo = [C.sb(f"xhalo{k}", [128, HALO], BF16) for k in range(KT)]
    pb = [nc.alloc_psum_tensor(f"pb{i}", [128, 512], F32) for i in range(8)]
    pb = [t.ap() for t in pb]
    build_consts(C)
    C.finish_persistent()
    load_xin(C, io)
    emit_l0_attn(C, io, pb)
    fins = []
    for k in range(KT):
        fins.append(dma(C, "sync", io["xoutT"][k * 128:(k + 1) * 128, :],
                        C.xown[k][:, :], tag=f"sto{k}"))
    prog.emit("sync", lambda e: e.nop(), deps=fins, tag="fin")
    prog.run(nc)
    return nc, io



def make_rope_tables(positions):
    dh = DH
    inv = 1.0 / (10000.0 ** (np.arange(0, dh, 2, dtype=np.float64) / dh))
    ang = positions.astype(np.float64)[None, :] * inv[:, None]
    cos = np.concatenate([np.cos(ang), np.cos(ang)], axis=0)
    sin = np.concatenate([np.sin(ang), np.sin(ang)], axis=0)
    sinm = sin.copy()
    sinm[0:64] *= -1.0
    return cos.astype(np.float32), sinm.astype(np.float32)


def make_band_masks():
    m = np.zeros((6, 128, 512), np.float32)
    ki = np.arange(128)[:, None]
    qi = np.arange(512)[None, :]
    for o in range(-2, 4):
        d = qi - ki - 128 * o
        ok = (d >= 0) & (d < WIN)
        m[o + 2] = np.where(ok, 0.0, NEG)
    return m


def stage_l0attn(inp):
    x = inp["L0_attn_x"]
    wq = inp["L0_attn_Wq"]; wkv = inp["L0_attn_Wkv"]; wo = inp["L0_attn_Wo"]
    temp = inp["L0_attn_qk_temp"].reshape(1, 16)
    wqT = np.ascontiguousarray(wq.T)
    wkvT = np.ascontiguousarray(wkv.T)
    woT = np.ascontiguousarray(wo.T)
    bm = make_band_masks()
    maps = []
    for c in range(8):
        b, hf = c // 2, c % 2
        lo = hf * TOK - HALO
        xs = np.zeros((TH, D), np.float32)
        s0 = max(lo, 0)
        xs[s0 - lo:, :] = x[b, s0:hf * TOK + TOK, :]
        cos, sinm = make_rope_tables(np.arange(lo, lo + TH))
        hv = np.full((128, 2), 0.0 if hf == 0 else 1.0, np.float32)
        maps.append({
            "hvalid": hv,
            "xinT": np.ascontiguousarray(xs.T),
            "cosT": cos, "sinmT": sinm,
            "bmask": bm, "temp": temp.astype(np.float32),
            "wqT": wqT, "wkvT": wkvT, "woT": woT,
        })
    return maps





E0 = 8
MH = 1408
MHS = 2816


def emit_moe_gather(C, io):
    nc = C.nc
    groups = [[0, 1, 2, 3, 4, 5, 6, 7]]
    from concourse.replica_groups import maybe_share_collective_output_space
    space = maybe_share_collective_output_space("AllGather", groups)
    out = {}
    specs = [
        ("wg", [2048, MH], [8 * 2048, MH]),
        ("wu", [2048, MH], [8 * 2048, MH]),
        ("wd", [1408, 2048], [8 * 1408, 2048]),
        ("swg", [256, MHS], [2048, MHS]),
        ("swu", [256, MHS], [2048, MHS]),
        ("swd", [352, 2048], [2816, 2048]),
    ]
    for name, sshape, fshape in specs:
        cin = nc.dram_tensor(f"cc_{name}_in", sshape, F32, kind="Internal").ap()
        cout = nc.dram_tensor(f"cc_{name}_out", fshape, F32, kind="Internal",
                              addr_space=space).ap()
        d = dma(C, "gpsimd", cin, io[f"sh_{name}"], tag=f"cp_{name}")
        C.P.emit("gpsimd",
                 lambda e, ci=cin, co=cout, g=groups:
                     e.collective_compute("AllGather", ALU.bypass,
                                          replica_groups=g,
                                          ins=[ci.opt()], outs=[co.opt()]),
                 reads=[cin], writes=[cout], cc=True, tag=f"ag_{name}")
        out[name] = cout
    return out


def emit_l0_moe(C, io, pb, gath):
    nc = C.nc
    C.push_scratch()
    hbuf = [C.scratch(f"hb{i}", [128, TOK], BF16) for i in range(22)]
    cbb = [C.scratch(f"cbb{e}", [128, TOK], BF16) for e in range(E0)]
    sgb = C.scratch("sgb", [128, TOK], BF16)
    gwT = C.scratch("gwT", [128, KT * 8], BF16)
    shgT = C.scratch("shgT", [128, KT * 1], BF16)
    iota8 = C.scratch("iota8", [128, 8], F32)
    cbT = C.scratch("cbT", [8, TOK], BF16)
    gsc = [C.scratch(f"gsc{i}", [128, 8], F32) for i in range(6)]
    gnm = C.scratch("gnm", [128, 1], F32)
    cb16 = C.scratch("cb16", [128, 8], BF16)

    C.push_scratch()
    mpool = C.scratch("mpool", [1, 2 * TOK], F32)
    emit_ln(C, [(C.xown, 0, 0, TOK)], C.xnh, mpool[0:1, 0:TOK],
            mpool[0:1, TOK:2 * TOK], pb, tag="ln1")
    C.pop_scratch()
    C.P.fence()
    rowg = C.scratch("rowg", [1, TOK], F32)
    wba = [C.scratch(f"mwa{i}", [128, KT * 128], BF16) for i in range(2)]
    wbb = [C.scratch(f"mwb{i}", [128, KT * 128], BF16) for i in range(2)]
    wbd = [C.scratch("mwd0", [128, 22 * 128], BF16)]

    xn = C.xnh

    dma(C, "gpsimd", gwT.ap().rearrange("p (kt n) -> p kt n", kt=KT),
        io["gate_wT"].rearrange("(kt p) n -> p kt n", p=128), tag="ldgw")
    dma(C, "gpsimd", shgT.ap()[:, 0:KT], io["shg"], tag="ldshg")
    C.P.emit("gpsimd", lambda e, o=iota8[:, :]: e.iota(o, pattern=[[1, 8]],
                                                       channel_multiplier=0,
                                                       allow_small_or_imprecise_dtypes=True),
             writes=[iota8], tag="iota8")

    for tc in range(8):
        lps = pb[0][:, 0:8]
        for k in range(KT):
            mm(C, lps, xn[k][:, tc * 128:(tc + 1) * 128],
               gwT.ap().rearrange("p (kt n) -> p kt n", kt=KT)[:, k, :],
               k == 0, k == KT - 1, tag=f"glog{tc}")
        lg, eL, t1, msk, e2, ew = gsc
        cp(C, lg[:, :], lps, tag=f"gcp{tc}")
        ve(C, "reduce_max", gnm[:, 0:1], lg[:, :], axis=AX.X, negate=True,
           tag=f"gm1{tc}")
        act(C, eL[:, :], lg[:, :], AF.Exp, bias=gnm[:, 0:1], tag=f"gexp{tc}")
        ts(C, msk[:, :], eL[:, :], 1.0, ALU.is_ge, tag=f"gc1{tc}")
        ve(C, "scalar_tensor_tensor", t1[:, :], msk[:, :], -1000.0, iota8[:, :],
           op0=ALU.mult, op1=ALU.add, tag=f"gt1{tc}")
        ve(C, "tensor_reduce", t1[:, 1:2], t1[:, :], axis=AX.X, op=ALU.min,
           tag=f"gi1{tc}")
        ts(C, t1[:, 1:2], t1[:, 1:2], 1000.0, ALU.add, tag=f"gi1b{tc}")
        ts(C, msk[:, :], iota8[:, :], t1[:, 1:2], ALU.is_equal, tag=f"gm1k{tc}")
        tt(C, e2[:, :], eL[:, :], msk[:, :], ALU.mult, tag=f"ge2a{tc}")
        tt(C, e2[:, :], eL[:, :], e2[:, :], ALU.subtract, tag=f"ge2{tc}")
        ve(C, "reduce_max", t1[:, 2:3], e2[:, :], axis=AX.X, tag=f"gm2{tc}")
        ts(C, ew[:, :], e2[:, :], t1[:, 2:3], ALU.is_ge, tag=f"gc2{tc}")
        ve(C, "scalar_tensor_tensor", ew[:, :],
           ew[:, :], -1000.0, iota8[:, :], op0=ALU.mult, op1=ALU.add, tag=f"gt2{tc}")
        ve(C, "tensor_reduce", t1[:, 3:4], ew[:, :], axis=AX.X, op=ALU.min,
           tag=f"gi2{tc}")
        ts(C, t1[:, 3:4], t1[:, 3:4], 1000.0, ALU.add, tag=f"gi2b{tc}")
        ts(C, ew[:, :], iota8[:, :], t1[:, 3:4], ALU.is_equal, tag=f"gm2k{tc}")
        tt(C, msk[:, :], msk[:, :], ew[:, :], ALU.add, tag=f"gms{tc}")
        tt(C, ew[:, :], msk[:, :], eL[:, :], ALU.mult, tag=f"gew{tc}")
        ve(C, "reduce_sum", t1[:, 4:5], ew[:, :], axis=AX.X, tag=f"gs12{tc}")
        ve(C, "reciprocal", t1[:, 5:6], t1[:, 4:5], tag=f"gr12{tc}")
        ts(C, e2[:, :], ew[:, :], t1[:, 5:6], ALU.mult, tag=f"gcmb{tc}")
        if DEBUG and tc == 1:
            dma(C, "sync", io["dbg_g"][:, 0:8], eL[:, :], tag="dbgeL")
            dma(C, "sync", io["dbg_g"][:, 8:16], msk[:, :], tag="dbgmsk")
            dma(C, "sync", io["dbg_g"][:, 16:24], ew[:, :], tag="dbgew")
            dma(C, "sync", io["dbg_g"][:, 24:32], t1[:, :], tag="dbgt1")
            dma(C, "sync", io["dbg_g"][:, 32:40], e2[:, :], tag="dbge2")
            dma(C, "sync", io["dbg_g"][:, 40:48], lg[:, :], tag="dbglg")
        cp(C, cb16[:, :], e2[:, :], tag=f"gcb{tc}")
        cbps = pb[7].bitcast(BF16)
        C.P.emit("tensor",
                 lambda e, o=cbps[0:8, 0:128], i=cb16[:, :],
                        idn=C.ident_bf[:, :]: e.transpose(o, i, idn),
                 reads=[cb16, C.ident_bf], writes=[pb[7]], tag=f"gtr{tc}")
        cp(C, cbT[0:8, tc * 128:(tc + 1) * 128], cbps[0:8, 0:128],
           tag=f"gcbT{tc}")
    sel8 = C.scratch("sel8", [8, 8 * 128], BF16)
    dma(C, "gpsimd", sel8[:, :], io["sel8"], tag="ldsel8")
    for e in range(E0):
        for c in range(2):
            mm(C, pb[2][:, 0:512], sel8[:, e * 128:(e + 1) * 128],
               cbT[0:8, c * 512:(c + 1) * 512], True, True, tag=f"cbb{e}")
            cp(C, cbb[e][:, c * 512:(c + 1) * 512], pb[2][:, 0:512],
               tag=f"cbbe{e}")

    for c in range(2):
        sps = pb[3][0:1, 0:512]
        for k in range(KT):
            mm(C, sps, shgT.ap()[:, k:k + 1],
               xn[k][:, c * 512:(c + 1) * 512], k == 0, k == KT - 1,
               tag=f"sg{c}")
        act(C, rowg[0:1, c * 512:(c + 1) * 512], sps, AF.Sigmoid, tag=f"sgs{c}")
        bcast_row(C, pb[3][:, 0:512], rowg[0:1, c * 512:(c + 1) * 512], 512,
                  tag=f"sgb{c}")
        cp(C, sgb[:, c * 512:(c + 1) * 512], pb[3][:, 0:512], tag=f"sgbe{c}")

    def stream_w(buf, src_rows, src_c0, w, tag, nkt=KT):
        return dma(C, "gpsimd",
                   buf.ap()[:, 0:nkt * w].rearrange("p (kt n) -> p kt n", kt=nkt),
                   src_rows.rearrange("(kt p) n -> p kt n", p=128)[:, :, src_c0:src_c0 + w],
                   tag=tag)

    if DEBUG:
        dma(C, "sync", io["dbg_cb"], cbT[:, :], tag="dbgcb")
        dma(C, "sync", io["dbg_sg"], rowg[0:1, :], tag="dbgsg")
    nsh = MHS // 128
    for hc in range(nsh):
        stream_w(wba[hc % 2], gath["swg"], hc * 128, 128, f"lswg{hc}")
        stream_w(wbb[hc % 2], gath["swu"], hc * 128, 128, f"lswu{hc}")
        for c in range(2):
            gps = pb[0 + (hc % 2)][:, 0:512]
            ups = pb[2 + (hc % 2)][:, 0:512]
            for k in range(KT):
                mm(C, gps, wba[hc % 2].ap().rearrange("p (kt n) -> p kt n", kt=KT)[:, k, :],
                   xn[k][:, c * 512:(c + 1) * 512], k == 0, k == KT - 1, tag=f"shg{hc}")
            for k in range(KT):
                mm(C, ups, wbb[hc % 2].ap().rearrange("p (kt n) -> p kt n", kt=KT)[:, k, :],
                   xn[k][:, c * 512:(c + 1) * 512], k == 0, k == KT - 1, tag=f"shu{hc}")
            sl = hbuf[hc][:, c * 512:(c + 1) * 512]
            act(C, sl, gps, AF.Silu, tag=f"shsl{hc}")
            tt(C, sl, sl, ups, ALU.mult, tag=f"shmu{hc}")
            tt(C, sl, sl, sgb[:, c * 512:(c + 1) * 512], ALU.mult, tag=f"shsg{hc}")
    for oc in range(KT):
        stream_w(wbd[0], gath["swd"], oc * 128, 128, f"lswd{oc}", nkt=nsh)
        for c in range(2):
            ps = pb[4 + (oc * 2 + c) % 2][:, 0:512]
            for hc in range(nsh):
                mm(C, ps, wbd[0].ap()[:, 0:nsh * 128].rearrange("p (kt n) -> p kt n", kt=nsh)[:, hc, :],
                   hbuf[hc][:, c * 512:(c + 1) * 512], hc == 0, hc == nsh - 1,
                   tag=f"shd{oc}")
            tt(C, C.xown[oc][:, c * 512:(c + 1) * 512],
               C.xown[oc][:, c * 512:(c + 1) * 512], ps, ALU.add, tag=f"shres{oc}")

    if DEBUG:
        dma(C, "sync", io["dbg_h0"], hbuf[0][:, :], tag="dbgh0")
    nh = MH // 128
    for e in range(E0):
        for hc in range(nh):
            stream_w(wba[hc % 2], gath["wg"][e * 2048:(e + 1) * 2048, :], hc * 128,
                     128, f"lwg{e}_{hc}")
            stream_w(wbb[hc % 2], gath["wu"][e * 2048:(e + 1) * 2048, :], hc * 128,
                     128, f"lwu{e}_{hc}")
            for c in range(2):
                gps = pb[0 + (hc % 2)][:, 0:512]
                ups = pb[2 + (hc % 2)][:, 0:512]
                for k in range(KT):
                    mm(C, gps, wba[hc % 2].ap().rearrange("p (kt n) -> p kt n", kt=KT)[:, k, :],
                       xn[k][:, c * 512:(c + 1) * 512], k == 0, k == KT - 1, tag=f"eg{e}")
                for k in range(KT):
                    mm(C, ups, wbb[hc % 2].ap().rearrange("p (kt n) -> p kt n", kt=KT)[:, k, :],
                       xn[k][:, c * 512:(c + 1) * 512], k == 0, k == KT - 1, tag=f"eu{e}")
                sl = hbuf[hc][:, c * 512:(c + 1) * 512]
                act(C, sl, gps, AF.Silu, tag=f"esl{e}")
                tt(C, sl, sl, ups, ALU.mult, tag=f"emu{e}")
                tt(C, sl, sl, cbb[e][:, c * 512:(c + 1) * 512], ALU.mult,
                   tag=f"ecb{e}")
        for oc in range(KT):
            stream_w(wbd[0], gath["wd"][e * 1408:(e + 1) * 1408, :], oc * 128,
                     128, f"lwd{e}_{oc}", nkt=nh)
            for c in range(2):
                ps = pb[4 + (oc * 2 + c) % 2][:, 0:512]
                for hc in range(nh):
                    mm(C, ps, wbd[0].ap()[:, 0:nh * 128].rearrange("p (kt n) -> p kt n", kt=nh)[:, hc, :],
                       hbuf[hc][:, c * 512:(c + 1) * 512], hc == 0, hc == nh - 1,
                       tag=f"ed{e}")
                tt(C, C.xown[oc][:, c * 512:(c + 1) * 512],
                   C.xown[oc][:, c * 512:(c + 1) * 512], ps, ALU.add,
                   tag=f"eres{e}_{oc}")
    C.pop_scratch()
    C.P.fence()


def build_moe_program():
    nc = bass.Bass("TRN2", target_bir_lowering=False, num_devices=8)
    io = {}
    def inp(name, shape, dt=F32):
        io[name] = nc.dram_tensor(name, shape, dt, kind="ExternalInput").ap()
    inp("xres", [D, TOK])
    inp("gate_wT", [D, 8])
    inp("sel8", [8, 8 * 128])
    inp("shg", [128, KT])
    inp("sh_wg", [2048, MH]); inp("sh_wu", [2048, MH]); inp("sh_wd", [1408, 2048])
    inp("sh_swg", [256, MHS]); inp("sh_swu", [256, MHS]); inp("sh_swd", [352, 2048])
    io["xoutT"] = nc.dram_tensor("xoutT", [D, TOK], F32, kind="ExternalOutput").ap()
    if DEBUG:
        io["dbg_cb"] = nc.dram_tensor("dbg_cb", [8, TOK], BF16, kind="ExternalOutput").ap()
        io["dbg_sg"] = nc.dram_tensor("dbg_sg", [1, TOK], F32, kind="ExternalOutput").ap()
        io["dbg_h0"] = nc.dram_tensor("dbg_h0", [128, TOK], BF16, kind="ExternalOutput").ap()
        io["dbg_g"] = nc.dram_tensor("dbg_g", [128, 48], F32, kind="ExternalOutput").ap()
    prog = Prog()
    C = Ctx(nc, prog, scratch_bytes=95 * 1024)
    C.xown = [C.sb(f"xown{k}", [128, TOK], F32) for k in range(KT)]
    C.xnh = [C.sb(f"xnh{k}", [128, TH], BF16) for k in range(KT)]
    C.xhalo = None
    pb = [nc.alloc_psum_tensor(f"pb{i}", [128, 512], F32) for i in range(8)]
    pb = [t.ap() for t in pb]
    build_consts(C, ident=True)
    C.finish_persistent()
    for k in range(KT):
        dma(C, "sync", C.xown[k][:, :], io["xres"][k * 128:(k + 1) * 128, :],
            tag=f"ldx{k}")
    gath = emit_moe_gather(C, io)
    emit_l0_moe(C, io, pb, gath)
    fins = []
    for k in range(KT):
        fins.append(dma(C, "sync", io["xoutT"][k * 128:(k + 1) * 128, :],
                        C.xown[k][:, :], tag=f"sto{k}"))
    prog.emit("sync", lambda e: e.nop(), deps=fins, tag="fin")
    prog.run(nc)
    return nc, io


def stage_moe(inp, x1):
    g = inp["L0_ffn_gate_w"]
    shg = np.ascontiguousarray(inp["L0_ffn_shgate_w"].reshape(KT, 128).T)
    wg = np.ascontiguousarray(inp["L0_ffn_w_gate"].reshape(8 * 2048, MH))
    wu = np.ascontiguousarray(inp["L0_ffn_w_up"].reshape(8 * 2048, MH))
    wd = np.ascontiguousarray(inp["L0_ffn_w_down"].reshape(8 * 1408, 2048))
    swg = inp["L0_ffn_shw_gate"]; swu = inp["L0_ffn_shw_up"]; swd = inp["L0_ffn_shw_down"]
    gwT = np.ascontiguousarray(g.T)
    maps = []
    for c in range(8):
        b, hf = c // 2, c % 2
        xs = x1[b, hf * TOK:(hf + 1) * TOK, :]
        sel8 = np.zeros((8, 8 * 128), np.float32)
        for e in range(8):
            sel8[e, e * 128:(e + 1) * 128] = 1.0
        maps.append({
            "xres": np.ascontiguousarray(xs.T), "sel8": sel8,
            "gate_wT": gwT, "shg": shg.astype(np.float32),
            "sh_wg": np.ascontiguousarray(wg[c * 2048:(c + 1) * 2048]),
            "sh_wu": np.ascontiguousarray(wu[c * 2048:(c + 1) * 2048]),
            "sh_wd": np.ascontiguousarray(wd[c * 1408:(c + 1) * 1408]),
            "sh_swg": np.ascontiguousarray(swg[c * 256:(c + 1) * 256]),
            "sh_swu": np.ascontiguousarray(swu[c * 256:(c + 1) * 256]),
            "sh_swd": np.ascontiguousarray(swd[c * 352:(c + 1) * 352]),
        })
    return maps





def rope_apply(C, ps, out_bf, coss, sinm, c0, w, rp, sqb, tag):
    cc = coss[:, c0:c0 + w]
    ss = sinm[:, c0:c0 + w]
    r = rp[:, 0:w]
    tt(C, r, ps, cc, ALU.mult, tag=f"{tag}_cos")
    ve(C, "scalar_tensor_tensor", sqb[0:64, 0:w], ps[64:128, :], 1.0,
       ss[0:64, :], op0=ALU.mult, op1=ALU.mult, tag=f"{tag}_rot1")
    ve(C, "scalar_tensor_tensor", sqb[64:128, 0:w], ps[0:64, :], 1.0,
       ss[64:128, :], op0=ALU.mult, op1=ALU.mult, tag=f"{tag}_rot2")
    tt(C, r, r, sqb[:, 0:w], ALU.add, tag=f"{tag}_add")
    cp(C, out_bf, r, tag=f"{tag}_cast")


def emit_l1_attn(C, io, pb):
    nc = C.nc
    INV_SQ_DH = 1.0 / float(np.sqrt(128.0))
    C.push_scratch()
    coss = C.scratch("coss1", [DH, TOK], BF16)
    sinm = C.scratch("sinm1", [DH, TOK], BF16)
    khat = [C.scratch(f"k1_{g}", [DH, 2 * TOK], BF16) for g in range(HKV0)]
    vtok = [C.scratch(f"v1_{t}", [128, 512], BF16) for t in range(2 * TOK // 128)]
    cmask = C.scratch("cmask1", [128, KT * 512], BF16)
    mkrow = C.scratch("mkrow", [1, 64], F32)
    rnrp = C.scratch("rnrp1", [128, 512], F32)
    rnsq = C.scratch("rnsq1", [128, 512], BF16)
    nbb = C.scratch("nbb1", [128, 512], F32)
    rowq = C.scratch("rowq1", [1, 1024], F32)
    zrow = rowq[0:1, 512:1024]
    rzb = C.scratch("rzb1", [128, 512], F32)
    prb = [C.scratch(f"prb1_{i}", [128, 512], BF16) for i in range(2)]
    wq = [C.scratch(f"wq1_{i}", [128, KT * 128], BF16) for i in range(2)]

    dma(C, "gpsimd", coss[:, :], io["cosT"][:, HALO:TH], tag="ldcos1")
    dma(C, "gpsimd", sinm[:, :], io["sinmT"][:, HALO:TH], tag="ldsin1")

    C.push_scratch()
    mpool = C.scratch("mpool1", [1, 2 * TOK], F32)
    emit_ln(C, [(C.xown, 0, 0, TOK)], C.xnh, mpool[0:1, 0:TOK],
            mpool[0:1, TOK:2 * TOK], pb, tag="ln2")
    C.pop_scratch()
    C.P.fence()
    xn = C.xnh

    C.push_scratch()
    kown = [C.scratch(f"ko{g}", [128, TOK], BF16) for g in range(HKV0)]
    def load_w1(buf, name, c0, tg):
        return dma(C, "gpsimd",
                   buf.ap().rearrange("p (kt n) -> p kt n", kt=KT),
                   io[name].rearrange("(kt p) n -> p kt n", p=128)[:, :, c0:c0 + 128],
                   tag=tg)
    load_w1(wq[0], "wk1", 0, "ldk1_0")
    for g in range(HKV0):
        if g + 1 < HKV0:
            load_w1(wq[(g + 1) % 2], "wk1", (g + 1) * 128, f"ldk1_{g+1}")
        for c in range(2):
            ps = pb[c][:, 0:512]
            for k in range(KT):
                mm(C, ps, wq[g % 2].ap().rearrange("p (kt n) -> p kt n", kt=KT)[:, k, :],
                   xn[k][:, c * 512:(c + 1) * 512], k == 0, k == KT - 1, tag=f"k1p{g}")
            rope_apply(C, ps, kown[g][:, c * 512:(c + 1) * 512], coss, sinm,
                       c * 512, 512, rnrp, rnsq, tag=f"k1r{g}")
    for half in range(2):
        for j in range(2):
            load_w1(wq[j], "wv1", (2 * half + j) * 128, f"ldv1_{half}{j}")
        for tc in range(TOK // 128):
            ps = pb[2 + (tc % 2)][:, 0:256]
            for j in range(2):
                for k in range(KT):
                    mm(C, ps[:, j * 128:j * 128 + 128],
                       xn[k][:, tc * 128:(tc + 1) * 128],
                       wq[j].ap().rearrange("p (kt n) -> p kt n", kt=KT)[:, k, :],
                       k == 0, k == KT - 1, tag=f"v1p{tc}")
            cp(C, vtok[tc][:, half * 256:half * 256 + 256], ps, tag=f"v1e{tc}")

    groups = [[0, 1], [2, 3], [4, 5], [6, 7]]
    cc_kown = nc.dram_tensor("cc_kown", [512, TOK], BF16, kind="Internal").ap()
    cc_kfull = nc.dram_tensor("cc_kfull", [1024, TOK], BF16, kind="Internal").ap()
    cc_vown = nc.dram_tensor("cc_vown", [TOK, 512], BF16, kind="Internal").ap()
    cc_vfull = nc.dram_tensor("cc_vfull", [2 * TOK, 512], BF16, kind="Internal").ap()
    for g in range(HKV0):
        dma(C, "sync", cc_kown[g * 128:(g + 1) * 128, :], kown[g][:, :], tag=f"kx{g}")
    for tc in range(TOK // 128):
        dma(C, "sync", cc_vown[tc * 128:(tc + 1) * 128, :], vtok[tc][:, :], tag=f"vx{tc}")
    C.pop_scratch()
    aoh = [C.scratch(f"aoh{i}", [128, 512], BF16) for i in range(H0)]
    C.q1buf = C.scratch("q1buf", [128, 512], BF16)
    C.P.emit("gpsimd",
             lambda e, ci=cc_kown, co=cc_kfull: e.collective_compute(
                 "AllGather", ALU.bypass, replica_groups=groups,
                 ins=[ci.opt()], outs=[co.opt()]),
             reads=[cc_kown], writes=[cc_kfull], cc=True, tag="agk")
    C.P.emit("gpsimd",
             lambda e, ci=cc_vown, co=cc_vfull: e.collective_compute(
                 "AllGather", ALU.bypass, replica_groups=groups,
                 ins=[ci.opt()], outs=[co.opt()]),
             reads=[cc_vown], writes=[cc_vfull], cc=True, tag="agv")
    for g in range(HKV0):
        for r in range(2):
            dma(C, "sync", khat[g][:, r * TOK:(r + 1) * TOK],
                cc_kfull[r * 512 + g * 128: r * 512 + (g + 1) * 128, :], tag=f"kl{g}")
    for tc in range(2 * TOK // 128):
        dma(C, "sync", vtok[tc][:, :], cc_vfull[tc * 128:(tc + 1) * 128, :],
            tag=f"vl{tc}")

    for g in range(HKV0):
        for c in range(4):
            sl = khat[g][:, c * 512:(c + 1) * 512]
            tt(C, rnsq[:, 0:512], sl, sl, ALU.mult, tag=f"mk{g}")
            mm(C, pb[5][0:1, 0:512], C.ones_bf_col[:], rnsq[:, 0:512], True, True,
               tag=f"mkm{g}")
            ve(C, "reduce_max", mkrow[0:1, g * 4 + c:g * 4 + c + 1],
               pb[5][0:1, 0:512], axis=AX.X, tag=f"mkr{g}")
        ve(C, "tensor_reduce", mkrow[0:1, 16 + g:17 + g], mkrow[0:1, g * 4:g * 4 + 4],
           axis=AX.X, op=ALU.max, tag=f"mkx{g}")
        act(C, mkrow[0:1, 16 + g:17 + g], mkrow[0:1, 16 + g:17 + g], AF.Sqrt,
            tag=f"mks{g}")
        ts(C, mkrow[0:1, 16 + g:17 + g], mkrow[0:1, 16 + g:17 + g], 1.01, ALU.mult,
           tag=f"mkc{g}")

    load_w1(wq[0], "wq1", 0, "ldq1_0")
    for qb in range(2):
        dma(C, "gpsimd", cmask[:, :],
            io["cmask"].rearrange("m p n -> p m n")[:, qb * KT:(qb + 1) * KT, :],
            tag=f"ldcm{qb}")
        for h in range(H0):
            nxt = h + 1 if h + 1 < H0 else (0 if qb == 0 else None)
            if nxt is not None:
                load_w1(wq[(h + 1) % 2], "wq1", nxt * 128, f"ldq1_{qb}_{nxt}")
            g = h // GRP
            qps = pb[0][:, 0:512]
            for k in range(KT):
                mm(C, qps, wq[h % 2].ap().rearrange("p (kt n) -> p kt n", kt=KT)[:, k, :],
                   xn[k][:, qb * 512:(qb + 1) * 512], k == 0, k == KT - 1,
                   tag=f"q1p{h}")
            qhb = prb[0]
            rope_apply(C, qps, C.q1buf[:, 0:512], coss, sinm, qb * 512, 512,
                       rnrp, rnsq, tag=f"q1r{h}")
            tt(C, rnsq[:, 0:512], C.q1buf[:, 0:512], C.q1buf[:, 0:512], ALU.mult,
               tag=f"q1sq{h}")
            mm(C, pb[5][0:1, 0:512], C.ones_bf_col[:], rnsq[:, 0:512], True, True,
               tag=f"q1ss{h}")
            ts(C, rowq[0:1, 0:512], pb[5][0:1, 0:512], 1e-12, ALU.add, tag=f"q1e{h}")
            act(C, rowq[0:1, 0:512], rowq[0:1, 0:512], AF.Sqrt, tag=f"q1s{h}")
            tt(C, rowq[0:1, 0:512], rowq[0:1, 0:512],
               mkrow[0:1, 16 + g:17 + g].broadcast_to([1, 512]), ALU.mult,
               tag=f"q1m{h}")
            ts(C, rowq[0:1, 0:512], rowq[0:1, 0:512], -INV_SQ_DH, ALU.mult,
               tag=f"q1n{h}")
            bcast_row(C, pb[6][:, 0:512], rowq[0:1, 0:512], 512, tag=f"q1b{h}")
            cp(C, nbb[:, :], pb[6][:, 0:512], tag=f"q1be{h}")
            zps = pb[3][0:1, 0:512]
            ops = pb[2][:, 0:512]
            for kc in range(KT):
                sps = (pb[6] if kc % 2 == 0 else pb[7])[:, 0:512]
                mm(C, sps, khat[g][:, kc * 128:(kc + 1) * 128], C.q1buf[:, 0:512],
                   True, True, tag=f"s1{h}_{qb}")
                ve(C, "scalar_tensor_tensor", rnrp[:, 0:512], sps, INV_SQ_DH,
                   nbb[:, :], op0=ALU.mult, op1=ALU.add, tag=f"s1b{h}")
                tt(C, rnrp[:, 0:512], rnrp[:, 0:512],
                   cmask[:, kc * 512:(kc + 1) * 512], ALU.add, tag=f"s1m{h}")
                act(C, prb[kc % 2][:, :], rnrp[:, 0:512], AF.Exp, tag=f"s1e{h}")
                mm(C, zps, C.ones_bf_col[:], prb[kc % 2][:, :], kc == 0, kc == KT - 1,
                   tag=f"z1{h}")
                mm(C, ops, vtok[kc][:, g * 128:(g + 1) * 128], prb[kc % 2][:, :],
                   kc == 0, kc == KT - 1, tag=f"av1{h}")
            ve(C, "reciprocal", zrow, zps, tag=f"rz1{h}")
            bcast_row(C, pb[4][:, 0:512], zrow, 512, tag=f"bz1{h}")
            cp(C, rzb[:, :], pb[4][:, 0:512], tag=f"rzb1{h}")
            tt(C, aoh[h][:, :], ops, rzb[:, :], ALU.mult, tag=f"oe1{h}")
        for oc in range(KT):
            load_w1(wq[oc % 2], "wo1", oc * 128, f"ldo1_{qb}_{oc}")
            ps = pb[(oc % 2)][:, 0:512]
            for i in range(H0):
                mm(C, ps, wq[oc % 2].ap().rearrange("p (kt n) -> p kt n", kt=KT)[:, i, :],
                   aoh[i][:, :], i == 0, i == H0 - 1, tag=f"wo1{oc}")
            tt(C, C.xown[oc][:, qb * 512:(qb + 1) * 512],
               C.xown[oc][:, qb * 512:(qb + 1) * 512], ps, ALU.add, tag=f"r1{oc}")
        if qb == 0:
            load_w1(wq[0], "wq1", 0, "ldq1_again")
    C.pop_scratch()
    C.P.fence()


def build_l1attn_program():
    nc = bass.Bass("TRN2", target_bir_lowering=False, num_devices=8)
    io = {}
    def inp(name, shape, dt=F32):
        io[name] = nc.dram_tensor(name, shape, dt, kind="ExternalInput").ap()
    inp("xres", [D, TOK])
    inp("cosT", [DH, TH]); inp("sinmT", [DH, TH])
    inp("cmask", [32, DH, 512])
    inp("wq1", [D, D]); inp("wk1", [D, 512]); inp("wv1", [D, 512]); inp("wo1", [D, D])
    io["xoutT"] = nc.dram_tensor("xoutT", [D, TOK], F32, kind="ExternalOutput").ap()
    prog = Prog()
    C = Ctx(nc, prog, scratch_bytes=95 * 1024)
    C.xown = [C.sb(f"xown{k}", [128, TOK], F32) for k in range(KT)]
    C.xnh = [C.sb(f"xnh{k}", [128, TH], BF16) for k in range(KT)]
    pb = [nc.alloc_psum_tensor(f"pb{i}", [128, 512], F32) for i in range(8)]
    pb = [t.ap() for t in pb]
    build_consts(C)
    C.finish_persistent()
    for k in range(KT):
        dma(C, "sync", C.xown[k][:, :], io["xres"][k * 128:(k + 1) * 128, :],
            tag=f"ldx{k}")
    emit_l1_attn(C, io, pb)
    fins = []
    for k in range(KT):
        fins.append(dma(C, "sync", io["xoutT"][k * 128:(k + 1) * 128, :],
                        C.xown[k][:, :], tag=f"sto{k}"))
    prog.emit("sync", lambda e: e.nop(), deps=fins, tag="fin")
    prog.run(nc)
    return nc, io


def stage_l1attn(inp, x2):
    maps = []
    for c in range(8):
        b, hf = c // 2, c % 2
        xs = x2[b, hf * TOK:(hf + 1) * TOK, :]
        lo = hf * TOK - HALO
        cos, sinm = make_rope_tables(np.arange(lo, lo + TH))
        cm = np.zeros((2, KT, DH, 512), np.float32)
        for qb in range(2):
            for kc in range(KT):
                kg = kc * 128 + np.arange(128)[:, None]
                qg = hf * TOK + qb * 512 + np.arange(512)[None, :]
                cm[qb, kc] = np.where(qg >= kg, 0.0, NEG)
        maps.append({
            "xres": np.ascontiguousarray(xs.T),
            "cosT": cos, "sinmT": sinm,
            "cmask": cm.reshape(32, DH, 512),
            "wq1": inp["L1_attn_Wq"], "wk1": inp["L1_attn_Wk"],
            "wv1": inp["L1_attn_Wv"], "wo1": inp["L1_attn_Wo"],
        })
    return maps





N1 = 16384
GELU_C = 0.7978845608028654
GELU_B = 0.044715


def emit_l1_dequant(C, io, pb, name, half_scale=False):
    nc = C.nc
    cc_in = nc.dram_tensor(f"cc_{name}_in", [2048, 2048], BF16, kind="Internal").ap()
    C.push_scratch()
    pk = C.scratch(f"dq_{name}_pk", [128, 1024], I32)
    sc = C.scratch(f"dq_{name}_sc", [128, 64], F32)
    ta = C.scratch(f"dq_{name}_ta", [128, 1024], I32)
    tb = C.scratch(f"dq_{name}_tb", [128, 1024], I32)
    td = C.scratch(f"dq_{name}_td", [128, 1024], I32)
    u16 = C.scratch(f"dq_{name}_u16", [128, 1024], U16)
    wcode = C.scratch(f"dq_{name}_wc", [128, 2048], BF16)
    scex = C.scratch(f"dq_{name}_se", [128, 2048], F32)
    wt = [C.scratch(f"dq_{name}_wt{i}", [128, 128], BF16) for i in range(2)]
    cbps = pb[7].bitcast(BF16)
    for t in range(KT):
        dma(C, "sync", pk[:, :], io[f"p{name}"][t * 128:(t + 1) * 128, :], tag=f"dqp{t}")
        dma(C, "sync", sc[:, :], io[f"p{name}_sc"][t * 128:(t + 1) * 128, :], tag=f"dqs{t}")
        for half in range(2):
            if half == 0:
                ts(C, ta[:, :], pk[:, :], 15, ALU.bitwise_and, tag=f"dq_c{t}")
            else:
                ts(C, ta[:, :], pk[:, :], 4, ALU.logical_shift_right, 15,
                   ALU.bitwise_and, tag=f"dq_c{t}")
            ts(C, tb[:, :], ta[:, :], 7, ALU.bitwise_and, tag=f"dq_m{t}")
            ts(C, td[:, :], tb[:, :], 0x40, ALU.mult, 0x3F00, ALU.add, tag=f"dq_t1{t}")
            ts(C, tb[:, :], tb[:, :], 0x3F00, ALU.mult, tag=f"dq_t2{t}")
            tt(C, td[:, :], td[:, :], tb[:, :], ALU.min, tag=f"dq_min{t}")
            ts(C, ta[:, :], ta[:, :], 8, ALU.bitwise_and, 12, ALU.logical_shift_left,
               tag=f"dq_s{t}")
            tt(C, td[:, :], td[:, :], ta[:, :], ALU.bitwise_or, tag=f"dq_or{t}")
            cp(C, u16[:, :], td[:, :], tag=f"dq_u{t}")
            cp(C, wcode.ap().rearrange("p (n two) -> p n two", two=2)[:, :, half],
               u16[:].bitcast(BF16), tag=f"dq_il{t}")
        cp(C, scex.ap().rearrange("p (b e) -> p b e", e=32),
           sc[:].rearrange("p (b o) -> p b o", o=1).broadcast_to([128, 64, 32]),
           tag=f"dq_sx{t}")
        if half_scale:
            ts(C, scex[:, :], scex[:, :], 0.5, ALU.mult, tag=f"dq_hs{t}")
        tt(C, wcode[:, :], wcode[:, :], scex[:, :], ALU.mult, tag=f"dq_sm{t}")
        for j in range(KT):
            C.P.emit("tensor",
                     lambda e, o=cbps[0:128, 0:128], i=wcode[:, j * 128:(j + 1) * 128],
                            idn=C.ident_bf[:, :]: e.transpose(o, i, idn),
                     reads=[wcode, C.ident_bf], writes=[pb[7]], tag=f"dq_tr{t}_{j}")
            cp(C, wt[j % 2][:, :], cbps[0:128, 0:128], tag=f"dq_te{t}_{j}")
            dma(C, "sync", cc_in[j * 128:(j + 1) * 128, t * 128:(t + 1) * 128],
                wt[j % 2][:, :], tag=f"dq_st{t}_{j}")
    C.pop_scratch()
    return cc_in


def emit_l1_ffn(C, io, pb):
    nc = C.nc
    groups = [[0, 1, 2, 3, 4, 5, 6, 7]]
    from concourse.replica_groups import maybe_share_collective_output_space
    space = maybe_share_collective_output_space("AllGather", groups)
    gcc = {}
    for name, hs in (("g1", False), ("u1", False), ("d1", True)):
        cin = emit_l1_dequant(C, io, pb, name, half_scale=hs)
        cout = nc.dram_tensor(f"cc_{name}_out", [8 * 2048, 2048], BF16,
                              kind="Internal", addr_space=space).ap()
        C.P.emit("gpsimd",
                 lambda e, ci=cin, co=cout: e.collective_compute(
                     "AllGather", ALU.bypass, replica_groups=groups,
                     ins=[ci.opt()], outs=[co.opt()]),
                 reads=[cin], writes=[cout], cc=True, tag=f"ag_{name}")
        gcc[name] = cout

    C.push_scratch()
    mpool = C.scratch("mpool3", [1, 2 * TOK], F32)
    emit_ln(C, [(C.xown, 0, 0, TOK)], C.xnh, mpool[0:1, 0:TOK],
            mpool[0:1, TOK:2 * TOK], pb, tag="ln3")
    C.pop_scratch()
    C.P.fence()
    xn = C.xnh

    C.push_scratch()
    ibuf = [C.scratch(f"ib{i}", [128, TOK], BF16) for i in range(8)]
    wgb = [C.scratch(f"f_wg{i}", [128, KT * 128], BF16) for i in range(2)]
    wub = [C.scratch(f"f_wu{i}", [128, KT * 128], BF16) for i in range(2)]
    wdb = [C.scratch(f"f_wd{i}", [128, 8 * 128], BF16) for i in range(2)]
    tmp1 = C.scratch("f_t1", [128, 512], F32)
    tmp2 = C.scratch("f_t2", [128, 512], F32)

    def load_gu(buf, cout, rank, ncol, tg):
        return dma(C, "sync",
                   buf.ap().rearrange("p (kt n) -> p kt n", kt=KT),
                   cout[rank * 2048:(rank + 1) * 2048, ncol * 128:(ncol + 1) * 128]
                       .rearrange("(kt p) n -> p kt n", p=128), tag=tg)

    for g in range(16):
        rank = g // 2
        for nch in range(8):
            ncol = (g % 2) * 8 + nch
            load_gu(wgb[nch % 2], gcc["g1"], rank, ncol, f"lg{g}_{nch}")
            load_gu(wub[nch % 2], gcc["u1"], rank, ncol, f"lu{g}_{nch}")
            for c in range(2):
                gps = pb[0 + (nch % 2)][:, 0:512]
                ups = pb[2 + (nch % 2)][:, 0:512]
                for k in range(KT):
                    mm(C, gps, wgb[nch % 2].ap().rearrange("p (kt n) -> p kt n", kt=KT)[:, k, :],
                       xn[k][:, c * 512:(c + 1) * 512], k == 0, k == KT - 1, tag=f"fg{g}")
                for k in range(KT):
                    mm(C, ups, wub[nch % 2].ap().rearrange("p (kt n) -> p kt n", kt=KT)[:, k, :],
                       xn[k][:, c * 512:(c + 1) * 512], k == 0, k == KT - 1, tag=f"fu{g}")
                sl = ibuf[nch][:, c * 512:(c + 1) * 512]
                act(C, tmp1[:, :], gps, AF.Square, tag=f"fsq{g}")
                ts(C, tmp1[:, :], tmp1[:, :], GELU_C * GELU_B, ALU.mult, GELU_C,
                   ALU.add, tag=f"fu1{g}")
                tt(C, tmp2[:, :], tmp1[:, :], gps, ALU.mult, tag=f"fu2{g}")
                act(C, tmp2[:, :], tmp2[:, :], AF.Tanh, tag=f"fth{g}")
                ts(C, tmp2[:, :], tmp2[:, :], 1.0, ALU.add, tag=f"fv1{g}")
                tt(C, tmp2[:, :], tmp2[:, :], gps, ALU.mult, tag=f"fv2{g}")
                tt(C, sl, tmp2[:, :], ups, ALU.mult, tag=f"fiv{g}")
        for oc in range(KT):
            dma(C, "sync",
                wdb[oc % 2].ap().rearrange("p (kt n) -> p kt n", kt=8),
                gcc["d1"][g * 1024:(g + 1) * 1024, oc * 128:(oc + 1) * 128]
                    .rearrange("(kt p) n -> p kt n", p=128), tag=f"ld{g}_{oc}")
            for c in range(2):
                ps = pb[4 + (oc % 2)][:, 0:512]
                for nch in range(8):
                    mm(C, ps, wdb[oc % 2].ap().rearrange("p (kt n) -> p kt n", kt=8)[:, nch, :],
                       ibuf[nch][:, c * 512:(c + 1) * 512], nch == 0, nch == 7,
                       tag=f"fd{g}")
                tt(C, C.xown[oc][:, c * 512:(c + 1) * 512],
                   C.xown[oc][:, c * 512:(c + 1) * 512], ps, ALU.add, tag=f"fr{g}_{oc}")
    C.pop_scratch()
    C.P.fence()


def build_l1ffn_program():
    nc = bass.Bass("TRN2", target_bir_lowering=False, num_devices=8)
    io = {}
    def inp(name, shape, dt=F32):
        io[name] = nc.dram_tensor(name, shape, dt, kind="ExternalInput").ap()
    inp("xres", [D, TOK])
    for n in ("g1", "u1", "d1"):
        inp(f"p{n}", [2048, 1024], I32)
        inp(f"p{n}_sc", [2048, 64])
    io["xoutT"] = nc.dram_tensor("xoutT", [D, TOK], F32, kind="ExternalOutput").ap()
    prog = Prog()
    C = Ctx(nc, prog, scratch_bytes=95 * 1024)
    C.xown = [C.sb(f"xown{k}", [128, TOK], F32) for k in range(KT)]
    C.xnh = [C.sb(f"xnh{k}", [128, TH], BF16) for k in range(KT)]
    pb = [nc.alloc_psum_tensor(f"pb{i}", [128, 512], F32) for i in range(8)]
    pb = [t.ap() for t in pb]
    build_consts(C, ident=True)
    C.finish_persistent()
    for k in range(KT):
        dma(C, "sync", C.xown[k][:, :], io["xres"][k * 128:(k + 1) * 128, :],
            tag=f"ldx{k}")
    emit_l1_ffn(C, io, pb)
    fins = []
    for k in range(KT):
        fins.append(dma(C, "sync", io["xoutT"][k * 128:(k + 1) * 128, :],
                        C.xown[k][:, :], tag=f"sto{k}"))
    prog.emit("sync", lambda e: e.nop(), deps=fins, tag="fin")
    prog.run(nc)
    return nc, io


def stage_l1ffn(inp, x3):
    maps = []
    for c in range(8):
        b, hf = c // 2, c % 2
        xs = x3[b, hf * TOK:(hf + 1) * TOK, :]
        maps.append({
            "xres": np.ascontiguousarray(xs.T),
            "pg1": np.ascontiguousarray(inp["L1_ffn_gate_W_packed"][c * 2048:(c + 1) * 2048]),
            "pg1_sc": np.ascontiguousarray(inp["L1_ffn_gate_scales"][c * 2048:(c + 1) * 2048]),
            "pu1": np.ascontiguousarray(inp["L1_ffn_up_W_packed"][c * 2048:(c + 1) * 2048]),
            "pu1_sc": np.ascontiguousarray(inp["L1_ffn_up_scales"][c * 2048:(c + 1) * 2048]),
            "pd1": np.ascontiguousarray(inp["L1_ffn_down_W_packed"][:, c * 1024:(c + 1) * 1024]),
            "pd1_sc": np.ascontiguousarray(inp["L1_ffn_down_scales"][:, c * 64:(c + 1) * 64]),
        })
    return maps





_CACHE = {}


def _programs():
    if "p" not in _CACHE:
        _CACHE["p"] = (build_l0attn_program()[0], build_moe_program()[0],
                       build_l1attn_program()[0], build_l1ffn_program()[0])
    return _CACHE["p"]


def _run(nc, maps):
    res = run_bass_kernel_spmd(nc, maps, core_ids=list(range(8)))
    return [r["xoutT"] for r in res.results]


def kernel(**inputs):
    inp = {k: np.asarray(v) for k, v in inputs.items()}
    ncA, ncB, ncC, ncD = _programs()

    mapsA = stage_l0attn(inp)
    xs = _run(ncA, mapsA)

    g = inp["L0_ffn_gate_w"]
    shg = np.ascontiguousarray(inp["L0_ffn_shgate_w"].reshape(KT, 128).T).astype(np.float32)
    wg = np.ascontiguousarray(inp["L0_ffn_w_gate"].reshape(8 * 2048, MH))
    wu = np.ascontiguousarray(inp["L0_ffn_w_up"].reshape(8 * 2048, MH))
    wd = np.ascontiguousarray(inp["L0_ffn_w_down"].reshape(8 * 1408, 2048))
    swg = inp["L0_ffn_shw_gate"]; swu = inp["L0_ffn_shw_up"]; swd = inp["L0_ffn_shw_down"]
    gwT = np.ascontiguousarray(g.T)
    sel8 = np.zeros((8, 8 * 128), np.float32)
    for e in range(8):
        sel8[e, e * 128:(e + 1) * 128] = 1.0
    mapsB = []
    for c in range(8):
        mapsB.append({
            "xres": xs[c], "sel8": sel8,
            "gate_wT": gwT, "shg": shg,
            "sh_wg": np.ascontiguousarray(wg[c * 2048:(c + 1) * 2048]),
            "sh_wu": np.ascontiguousarray(wu[c * 2048:(c + 1) * 2048]),
            "sh_wd": np.ascontiguousarray(wd[c * 1408:(c + 1) * 1408]),
            "sh_swg": np.ascontiguousarray(swg[c * 256:(c + 1) * 256]),
            "sh_swu": np.ascontiguousarray(swu[c * 256:(c + 1) * 256]),
            "sh_swd": np.ascontiguousarray(swd[c * 352:(c + 1) * 352]),
        })
    xs = _run(ncB, mapsB)

    mapsC = []
    for c in range(8):
        b, hf = c // 2, c % 2
        lo = hf * TOK - HALO
        cos, sinm = make_rope_tables(np.arange(lo, lo + TH))
        cm = np.zeros((2, KT, DH, 512), np.float32)
        for qb in range(2):
            for kc in range(KT):
                kg = kc * 128 + np.arange(128)[:, None]
                qg = hf * TOK + qb * 512 + np.arange(512)[None, :]
                cm[qb, kc] = np.where(qg >= kg, 0.0, NEG)
        mapsC.append({
            "xres": xs[c],
            "cosT": cos, "sinmT": sinm,
            "cmask": cm.reshape(32, DH, 512),
            "wq1": inp["L1_attn_Wq"], "wk1": inp["L1_attn_Wk"],
            "wv1": inp["L1_attn_Wv"], "wo1": inp["L1_attn_Wo"],
        })
    xs = _run(ncC, mapsC)

    mapsD = []
    for c in range(8):
        mapsD.append({
            "xres": xs[c],
            "pg1": np.ascontiguousarray(inp["L1_ffn_gate_W_packed"][c * 2048:(c + 1) * 2048]),
            "pg1_sc": np.ascontiguousarray(inp["L1_ffn_gate_scales"][c * 2048:(c + 1) * 2048]),
            "pu1": np.ascontiguousarray(inp["L1_ffn_up_W_packed"][c * 2048:(c + 1) * 2048]),
            "pu1_sc": np.ascontiguousarray(inp["L1_ffn_up_scales"][c * 2048:(c + 1) * 2048]),
            "pd1": np.ascontiguousarray(inp["L1_ffn_down_W_packed"][:, c * 1024:(c + 1) * 1024]),
            "pd1_sc": np.ascontiguousarray(inp["L1_ffn_down_scales"][:, c * 64:(c + 1) * 64]),
        })
    xs = _run(ncD, mapsD)

    out = np.zeros((4, 2048, 2048), np.float32)
    for c in range(8):
        b, hf = c // 2, c % 2
        out[b, hf * TOK:(hf + 1) * TOK, :] = np.asarray(xs[c]).T
    return out


# version 04
# speedup vs baseline: 1.1271x; 1.1271x over previous
import sys
sys.path.insert(0, "/opt/trn_rl_repo")
import numpy as np
import ml_dtypes
import concourse.bass as bass
import concourse.mybir as mybir
from concourse.bass_utils import run_bass_kernel_spmd

F32 = mybir.dt.float32
BF16 = mybir.dt.bfloat16
I32 = mybir.dt.int32
U16 = mybir.dt.uint16
AF = mybir.ActivationFunctionType
ALU = mybir.AluOpType
AX = mybir.AxisListType

DEBUG = False
ENGINES = ("sync", "tensor", "vector", "scalar", "gpsimd")
NEG = -1e30


class Ins:
    __slots__ = ("eng", "build", "deps", "dma", "cc", "sig", "sem", "val",
                 "tag", "gid", "pos")
    def __init__(self, eng, build, dma, cc, tag, gid):
        self.eng = eng; self.build = build; self.dma = dma; self.cc = cc
        self.tag = tag; self.gid = gid
        self.deps = []; self.sig = False; self.sem = None; self.val = None
        self.pos = None

def _bkey(x):
    if hasattr(x, "tensor"):
        return x.tensor.name
    return x.name

class Prog:
    N_DMA_SEMS = 6

    def __init__(self):
        self.instrs = []
        self.streams = {e: [] for e in ENGINES}
        self.last_write = {}
        self.readers = {}
        self.fence_deps = []
        self.relax_same = {"tensor"}

    def emit(self, eng, build, reads=(), writes=(), dma=False, cc=False,
             tag="", deps=()):
        ins = Ins(eng, build, dma, cc, tag, len(self.instrs))
        dd = {}
        for b in reads:
            w = self.last_write.get(_bkey(b))
            if w is not None:
                dd[w.gid] = w
        for b in writes:
            k = _bkey(b)
            w = self.last_write.get(k)
            if w is not None:
                dd[w.gid] = w
            for r in self.readers.get(k, ()):
                dd[r.gid] = r
        for d in deps:
            dd[d.gid] = d
        for d in self.fence_deps:
            dd[d.gid] = d
        ins.deps = [d for d in dd.values() if d is not ins]
        for b in reads:
            self.readers.setdefault(_bkey(b), []).append(ins)
        for b in writes:
            k = _bkey(b)
            self.last_write[k] = ins
            self.readers[k] = []
        self.instrs.append(ins)
        self.streams[eng].append(ins)
        ins.pos = len(self.streams[eng]) - 1
        return ins

    def fence(self):
        self.fence_deps = [s[-1] for s in self.streams.values() if s]
        for d in self.fence_deps:
            d.sig = True

    def _need_sem(self, p, d):
        if p.dma or p.cc or d.dma or d.cc:
            return True
        if p.eng != d.eng:
            return True
        return p.eng not in self.relax_same

    def run(self, nc):
        for ins in self.instrs:
            for p in ins.deps:
                if self._need_sem(p, ins):
                    p.sig = True
        csem = {e: nc.alloc_semaphore(f"c_{e}") for e in ENGINES}
        dsem = {e: [nc.alloc_semaphore(f"d_{e}{i}") for i in range(self.N_DMA_SEMS)]
                for e in ("sync", "scalar", "gpsimd", "vector", "tensor")}
        duse = {e: 0 for e in dsem}
        ccount = {e: 0 for e in ENGINES}
        for e in ENGINES:
            for ins in self.streams[e]:
                if ins.cc:
                    ins.sem = nc.alloc_semaphore(f"cc{ins.gid}")
                    ins.val = 1
                elif ins.dma:
                    i = duse[e] % self.N_DMA_SEMS
                    ins.sem = dsem[e][i]
                    ins.val = 16 * (duse[e] // self.N_DMA_SEMS + 1)
                    duse[e] += 1
                elif ins.sig:
                    ccount[e] += 1
                    ins.sem = csem[e]
                    ins.val = ccount[e]
        prog = self

        def mk_body(e):
            def body(eng):
                water = {}
                for ins in prog.streams[e]:
                    for p in ins.deps:
                        if not prog._need_sem(p, ins):
                            continue
                        assert p.sem is not None, (p.tag, p.eng, ins.tag)
                        k = id(p.sem)
                        if water.get(k, 0) >= p.val:
                            continue
                        eng.wait_ge(p.sem, p.val)
                        water[k] = p.val
                    bi = ins.build(eng)
                    if ins.sem is not None:
                        if ins.dma:
                            bi.then_inc(ins.sem, 16)
                        else:
                            bi.then_inc(ins.sem, 1)
            return body

        with nc.Block() as block:
            block.sync(mk_body("sync"))
            block.tensor(mk_body("tensor"))
            block.vector(mk_body("vector"))
            block.scalar(mk_body("scalar"))
            block.gpsimd(mk_body("gpsimd"))
        return



class Ctx:
    def __init__(self, nc, prog, scratch_bytes=0):
        self.nc = nc
        self.P = prog
        self.tiles = {}
        self.uid = 0
        part_sz = nc.SBUF_PARTITION_SIZE_BYTES
        self.scratch_base = part_sz - scratch_bytes
        self.scratch_off = self.scratch_base
        self._scratch_stack = []

    def sb(self, name, shape, dtype):
        h = self.nc.alloc_sbuf_tensor(name, shape, dtype)
        return h

    def finish_persistent(self):
        r = self.nc.bump_sbuf(32)
        assert r is not None, "sbuf exhausted"
        start = r[0] if isinstance(r, tuple) else r
        assert start <= self.scratch_base, \
            f"persistent region ({start}) collides with scratch base ({self.scratch_base})"

    def push_scratch(self):
        self._scratch_stack.append(self.scratch_off)

    def pop_scratch(self):
        self.scratch_off = self._scratch_stack.pop()

    def scratch(self, name, shape, dtype):
        left = self.nc.bytes_allocated(None) if False else None
        nbytes = int(np.prod(shape[1:])) * _dtb(dtype)
        off = (self.scratch_off + 31) & ~31
        assert off + nbytes <= self.nc.SBUF_PARTITION_SIZE_BYTES, \
            f"scratch overflow {name}: {off + nbytes}"
        h = self.nc.alloc_sbuf_tensor_at(f"{name}_{self.uid}", shape, dtype, offset=off)
        self.uid += 1
        self.scratch_off = off + nbytes
        return h

def _dtb(dt):
    return {F32: 4, I32: 4, BF16: 2, U16: 2}[dt]



def _isbuf(x):
    return hasattr(x, "tensor") or hasattr(x, "name")

def dma(C, eng, out, in_, tag="dma", deps=()):
    return C.P.emit(eng, lambda e, o=out, i=in_: e.dma_start(out=o, in_=i),
                    reads=[in_], writes=[out], dma=True, tag=tag, deps=deps)

def mm(C, out, lhsT, rhs, start, stop, tag="mm", deps=(), extra_reads=()):
    reads = [lhsT, rhs] + list(extra_reads)
    return C.P.emit("tensor",
                    lambda e, o=out, l=lhsT, r=rhs, st=start, sp=stop:
                        e.matmul(o, l, r, start=st, stop=sp),
                    reads=reads, writes=[out], tag=tag, deps=deps)

def act(C, out, in_, func, bias=0.0, scale=1.0, tag="act", deps=(), extra_reads=()):
    reads = [in_] + [x for x in (bias, scale) if _isbuf(x)] + list(extra_reads)
    return C.P.emit("scalar",
                    lambda e, o=out, i=in_, f=func, b=bias, s=scale:
                        e.activation(out=o, in_=i, func=f, bias=b, scale=s),
                    reads=reads, writes=[out], tag=tag, deps=deps)

def ve(C, name, *aps, eng="vector", tag="", deps=(), writes=1, **kw):
    outs = list(aps[:writes]); ins = list(aps[writes:])
    def build(e, n=name, o=outs, i=ins, kw=kw):
        return getattr(e, n)(*o, *i, **kw)
    return C.P.emit(eng, build, reads=[x for x in ins if _isbuf(x)],
                    writes=outs, tag=tag or name, deps=deps)

def tt(C, out, a, b, op, eng="vector", tag="tt", deps=()):
    return C.P.emit(eng, lambda e, o=out, x=a, y=b, p=op: e.tensor_tensor(out=o, in0=x, in1=y, op=p),
                    reads=[a, b], writes=[out], tag=tag, deps=deps)

def ts(C, out, in_, s1, op0, s2=None, op1=None, eng="vector", tag="ts", deps=(), extra_reads=()):
    def build(e, o=out, i=in_, s1=s1, s2=s2, op0=op0, op1=op1):
        if op1 is None:
            return e.tensor_scalar(out=o, in0=i, scalar1=s1, scalar2=None, op0=op0)
        return e.tensor_scalar(out=o, in0=i, scalar1=s1, scalar2=s2, op0=op0, op1=op1)
    reads = [in_] + [x for x in (s1, s2) if hasattr(x, "tensor")] + list(extra_reads)
    return C.P.emit(eng, build, reads=reads, writes=[out], tag=tag, deps=deps)

def cp(C, out, in_, eng="vector", tag="cp", deps=()):
    return C.P.emit(eng, lambda e, o=out, i=in_: e.tensor_copy(out=o, in_=i),
                    reads=[in_], writes=[out], tag=tag, deps=deps)

def memset(C, ap, v, eng="vector", tag="memset", deps=()):
    return C.P.emit(eng, lambda e, a=ap, vv=v: e.memset(a, vv),
                    reads=[], writes=[ap], tag=tag, deps=deps)


D = 2048
KT = 16
TOK = 1024
HALO = 256
TH = TOK + HALO
H0, HKV0, DH = 16, 4, 128
GRP = H0 // HKV0
WIN = 256


def declare_io(nc, upto):
    io = {}
    def inp(name, shape, dt=F32):
        io[name] = nc.dram_tensor(name, shape, dt, kind="ExternalInput").ap()
    inp("xinT", [D, TH])
    inp("cosT", [DH, TH]); inp("sinmT", [DH, TH])
    inp("bmask", [6, DH, 512])
    inp("temp", [1, 16])
    inp("hvalid", [DH, 2])
    inp("wqT", [D, D]); inp("wkvT", [D, 1024]); inp("woT", [D, D])
    io["xoutT"] = nc.dram_tensor("xoutT", [D, TOK], F32, kind="ExternalOutput").ap()
    if DEBUG:
        io["dbg_xnh"] = nc.dram_tensor("dbg_xnh", [D, TH], BF16, kind="ExternalOutput").ap()
        io["dbg_khat"] = nc.dram_tensor("dbg_khat", [DH, TH], BF16, kind="ExternalOutput").ap()
        io["dbg_vtok"] = nc.dram_tensor("dbg_vtok", [128, 512], BF16, kind="ExternalOutput").ap()
        io["dbg_ao"] = nc.dram_tensor("dbg_ao", [DH, TOK], BF16, kind="ExternalOutput").ap()
        io["dbg_q"] = nc.dram_tensor("dbg_q", [2 * DH, TOK], BF16, kind="ExternalOutput").ap()
        io["dbg_w"] = nc.dram_tensor("dbg_w", [DH, KT * 128], BF16, kind="ExternalOutput").ap()
        io["dbg_x2"] = nc.dram_tensor("dbg_x2", [2 * 128, TH], BF16, kind="ExternalOutput").ap()
        io["dbg_z"] = nc.dram_tensor("dbg_z", [2, 512], F32, kind="ExternalOutput").ap()
        io["dbg_p"] = nc.dram_tensor("dbg_p", [2 * DH, 512], BF16, kind="ExternalOutput").ap()
    return io


def build_consts(C, ident=False):
    nc = C.nc
    C.ones_bf_row = C.sb("ones_bf_row", [1, 128], BF16)
    C.ones_f_row = C.sb("ones_f_row", [1, 128], F32)
    C.ones_bf_col = C.sb("ones_bf_col", [128, 1], BF16)
    memset(C, C.ones_bf_row[:], 1.0)
    memset(C, C.ones_f_row[:], 1.0)
    memset(C, C.ones_bf_col[:], 1.0)
    if ident:
        C.ident_bf = C.sb("ident_bf", [128, 128], BF16)
        memset(C, C.ident_bf[:], 1.0)
        C.P.emit("gpsimd",
                 lambda e, o=C.ident_bf[:, :]: e.affine_select(
                     out=o, in_=o, pattern=[[-1, 128]], compare_op=ALU.is_equal,
                     fill=0.0, base=0, channel_multiplier=1),
                 reads=[C.ident_bf], writes=[C.ident_bf], tag="identsel")


def bcast_row(C, psum_ap, row_ap, n, bf=False, tag="bcast"):
    ones = C.ones_bf_row if bf else C.ones_f_row
    return mm(C, psum_ap, ones[:], row_ap, True, True, tag=tag)


def emit_ln(C, blocks, xnh, mrow0, rrow0, pb, tag="ln"):
    nc = C.nc
    tot = sum(w for (_, _, _, w) in blocks)
    C.push_scratch()
    xb = C.scratch(f"{tag}_xb", [128, 512], BF16)
    sq = C.scratch(f"{tag}_sq", [128, 512], BF16)
    mrow = mrow0[0:1, 0:tot]
    rrow = rrow0[0:1, 0:tot]
    mb = C.scratch(f"{tag}_mb", [128, 512], F32)
    rb = C.scratch(f"{tag}_rb", [128, 512], F32)

    chunks = []
    for (tiles, sc, dc, w) in blocks:
        off = 0
        while off < w:
            cw = min(512, w - off)
            chunks.append((tiles, sc + off, dc + off, cw))
            off += cw

    dst_pos = 0
    for (tiles, sc, dc, w) in chunks:
        psA = pb[0][:, 0:w]
        psB = pb[1][:, 0:w]
        for k in range(KT):
            xs = tiles[k][:, sc:sc + w]
            cp(C, xb[:, 0:w], xs, tag=f"{tag}_cast")
            tt(C, sq[:, 0:w], xb[:, 0:w], xb[:, 0:w], ALU.mult, tag=f"{tag}_sq")
            mm(C, psA[0:1, :], C.ones_bf_col[:], xb[:, 0:w], k == 0, k == KT - 1, tag=f"{tag}_sx")
            mm(C, psB[0:1, :], C.ones_bf_col[:], sq[:, 0:w], k == 0, k == KT - 1, tag=f"{tag}_sxx")
        m = mrow[0:1, dst_pos:dst_pos + w]
        r = rrow[0:1, dst_pos:dst_pos + w]
        ts(C, m, psA[0:1, :], 1.0 / D, ALU.mult, tag=f"{tag}_mean")
        ve(C, "scalar_tensor_tensor", r, psA[0:1, :], 1.0 / D, m, tag=f"{tag}_m2",
           op0=ALU.mult, op1=ALU.mult)
        ve(C, "scalar_tensor_tensor", r, psB[0:1, :], 1.0 / D, r, tag=f"{tag}_var",
           op0=ALU.mult, op1=ALU.subtract)
        ts(C, r, r, 1e-5, ALU.add, tag=f"{tag}_eps")
        act(C, r, r, AF.Sqrt, tag=f"{tag}_sqrt")
        ve(C, "reciprocal", r, r, tag=f"{tag}_recip")
        bcast_row(C, pb[2][:, 0:w], m, w, tag=f"{tag}_bm")
        bcast_row(C, pb[3][:, 0:w], r, w, tag=f"{tag}_br")
        cp(C, mb[:, 0:w], pb[2][:, 0:w], tag=f"{tag}_mbe")
        cp(C, rb[:, 0:w], pb[3][:, 0:w], tag=f"{tag}_rbe")
        for k in range(KT):
            xs = tiles[k][:, sc:sc + w]
            dst = xnh[k][:, dc:dc + w]
            tt(C, dst, xs, mb[:, 0:w], ALU.subtract, tag=f"{tag}_sub")
            tt(C, dst, dst, rb[:, 0:w], ALU.mult, tag=f"{tag}_mul")
        dst_pos += w
    C.pop_scratch()


def rope_norm(C, psums, out_bf, coss, sinm, ccols, rs_row, pbb, temp_col,
              pbsq, rp, sqb, tag="rn"):
    for ps, c0 in psums:
        w = ps.shape[1]
        cc = coss[:, ccols + c0: ccols + c0 + w]
        ss = sinm[:, ccols + c0: ccols + c0 + w]
        r = rp[:, 0:w]
        tt(C, r, ps, cc, ALU.mult, tag=f"{tag}_cos")
        ve(C, "scalar_tensor_tensor", sqb[0:64, 0:w], ps[64:128, :], 1.0,
           ss[0:64, :], op0=ALU.mult, op1=ALU.mult, tag=f"{tag}_rot1")
        ve(C, "scalar_tensor_tensor", sqb[64:128, 0:w], ps[0:64, :], 1.0,
           ss[64:128, :], op0=ALU.mult, op1=ALU.mult, tag=f"{tag}_rot2")
        tt(C, r, r, sqb[:, 0:w], ALU.add, tag=f"{tag}_add")
        ob = out_bf[:, c0:c0 + w]
        cp(C, ob, r, tag=f"{tag}_cast")
        tt(C, sqb[:, 0:w], ob, ob, ALU.mult, tag=f"{tag}_sq")
        mm(C, pbsq[0:1, 0:w], C.ones_bf_col[:], sqb[:, 0:w], True, True,
           tag=f"{tag}_ssq")
        rs = rs_row[0:1, 0:w]
        ts(C, rs, pbsq[0:1, 0:w], 1e-24, ALU.add, tag=f"{tag}_eps")
        act(C, rs, rs, AF.Sqrt, tag=f"{tag}_sqrt")
        ve(C, "reciprocal", rs, rs, tag=f"{tag}_recip")
        if temp_col is not None:
            tt(C, rs, rs, temp_col.broadcast_to([1, w]), ALU.mult, tag=f"{tag}_temp")
        bcast_row(C, pbb[:, 0:w], rs, w, tag=f"{tag}_bc")
        tt(C, ob, ob, pbb[:, 0:w], ALU.mult, tag=f"{tag}_nrm")


def load_xin(C, io):
    for k in range(KT):
        dma(C, "sync", C.xown[k][:, :], io["xinT"][k * 128:(k + 1) * 128, HALO:TH],
            tag=f"ldxo{k}")
        dma(C, "gpsimd", C.xhalo[k][:, :], io["xinT"][k * 128:(k + 1) * 128, 0:HALO],
            tag=f"ldxh{k}")


def emit_l0_attn(C, io, pb):
    nc = C.nc
    C.push_scratch()
    obuf = [C.scratch(f"ao{k}", [128, TOK], BF16) for k in range(H0)]
    coss = C.scratch("coss", [DH, TH], BF16)
    sinm = C.scratch("sinm", [DH, TH], BF16)
    bmask = C.scratch("bmask", [DH, 6 * 512], BF16)
    khat = [C.scratch(f"khat{g}", [DH, TH], BF16) for g in range(HKV0)]
    vtok = [C.scratch(f"vtok{t}", [128, 512], BF16) for t in range(TH // 128)]
    tmprow = C.scratch("tmprow", [1, 16], F32)
    hval = C.scratch("hval", [DH, 2], F32)
    rnrp = C.scratch("rnrp", [128, 512], F32)
    rnsq = C.scratch("rnsq", [128, 512], BF16)
    rowpool = C.scratch("rowpool", [1, TH + 512], F32)
    rs_row = rowpool[0:1, 0:TH]
    zrow = rowpool[0:1, TH:TH + 512]
    rzb = C.scratch("rzb", [128, 512], F32)

    dma(C, "gpsimd", coss[:, :], io["cosT"], tag="ldcos")
    dma(C, "gpsimd", sinm[:, :], io["sinmT"], tag="ldsin")
    dma(C, "gpsimd", bmask[:, :],
        io["bmask"].rearrange("m p n -> p m n"), tag="ldbm")
    dma(C, "sync", tmprow[:, :], io["temp"], tag="ldtemp")
    dma(C, "sync", hval[:, :], io["hvalid"], tag="ldhv")

    C.push_scratch()
    mpool0 = C.scratch("mpool0", [1, 2 * TH], F32)
    emit_ln(C, [(C.xhalo, 0, 0, HALO), (C.xown, 0, HALO, TOK)], C.xnh,
            mpool0[0:1, 0:TH], mpool0[0:1, TH:2 * TH], pb, tag="ln0")
    C.pop_scratch()
    C.P.fence()

    C.push_scratch()
    wb = [C.scratch(f"wb{i}", [128, KT * 128], BF16) for i in range(2)]
    def load_wchunk(j, src, c0, tg):
        return dma(C, "gpsimd",
                   wb[j].ap().rearrange("p (kt n) -> p kt n", kt=KT),
                   src.rearrange("(kt p) n -> p kt n", p=128)[:, :, c0:c0 + 128],
                   tag=tg)

    cblk = ((0, 256), (256, 512), (768, 512))
    kbanks = [(0, 1, 2), (3, 6, 7)]
    load_wchunk(0, io["wkvT"], 0, "ldwk0")
    for g in range(HKV0):
        if g + 1 < HKV0:
            load_wchunk((g + 1) % 2, io["wkvT"], (g + 1) * 128, f"ldwk{g+1}")
        pc = []
        for bi, (c0, w) in enumerate(cblk):
            ps = pb[kbanks[g % 2][bi]][:, 0:w]
            for k in range(KT):
                lhsT = wb[g % 2].ap().rearrange("p (kt n) -> p kt n", kt=KT)[:, k, :]
                mm(C, ps, lhsT, C.xnh[k][:, c0:c0 + w], k == 0, k == KT - 1,
                   tag=f"kp{g}")
            pc.append((ps, c0))
        rope_norm(C, pc, khat[g][:, :], coss, sinm, 0, rs_row, pb[4], None,
                  pb[5], rnrp, rnsq, tag=f"kn{g}")
    for half in range(2):
        for j in range(2):
            load_wchunk(j, io["wkvT"], 512 + (2 * half + j) * 128, f"ldwv{half}{j}")
        for tc in range(TH // 128):
            ps = pb[2 + (tc % 2)][:, 0:256]
            for j in range(2):
                for k in range(KT):
                    mm(C, ps[:, j * 128:j * 128 + 128],
                       C.xnh[k][:, tc * 128:(tc + 1) * 128],
                       wb[j].ap().rearrange("p (kt n) -> p kt n", kt=KT)[:, k, :],
                       k == 0, k == KT - 1, tag=f"vp{tc}")
            cp(C, vtok[tc][:, half * 256:half * 256 + 256], ps, tag=f"ve{tc}")
    C.pop_scratch()

    C.P.fence()
    C.push_scratch()
    qh = [C.scratch(f"qh{i}", [DH, TOK], BF16) for i in range(2)]
    prb = [C.scratch(f"prb{i}", [128, 512], BF16) for i in range(2)]
    wq = [C.scratch(f"wq{i}", [128, KT * 128], BF16) for i in range(2)]

    def load_wq(h):
        return dma(C, "gpsimd",
                   wq[h % 2].ap().rearrange("p (kt n) -> p kt n", kt=KT),
                   io["wqT"].rearrange("(kt p) n -> p kt n", p=128)[:, :, h * 128:(h + 1) * 128],
                   tag=f"ldwq{h}")

    def emit_qproj(h):
        pc = []
        for ci, (c0, w) in enumerate(((0, 512), (512, 512))):
            ps = pb[2 * (h % 2) + ci][:, 0:w]
            for k in range(KT):
                lhsT = wq[h % 2].ap().rearrange("p (kt n) -> p kt n", kt=KT)[:, k, :]
                mm(C, ps, lhsT, C.xnh[k][:, HALO + c0:HALO + c0 + w],
                   k == 0, k == KT - 1, tag=f"qp{h}")
            pc.append((ps, c0))
        return pc

    load_wq(0)
    pcs = {0: emit_qproj(0)}
    for h in range(H0):
        if h + 1 < H0:
            load_wq(h + 1)
        rope_norm(C, pcs.pop(h), qh[h % 2][:, :], coss, sinm, HALO, rs_row, pb[6],
                  tmprow[0:1, h:h + 1], pb[7], rnrp, rnsq, tag=f"qn{h}")
        if h + 1 < H0:
            pcs[h + 1] = emit_qproj(h + 1)
        g = h // GRP
        for qb in range(2):
            q_ap = qh[h % 2][:, qb * 512:(qb + 1) * 512]
            ops = pb[4][:, 0:512]
            zps = pb[5][0:1, 0:512]
            kcs = [4 * qb + o for o in range(-2, 4)]
            for i, kc in enumerate(kcs):
                kcol = kc * 128 + HALO
                first, last = i == 0, i == len(kcs) - 1
                sps = (pb[6] if i % 2 == 0 else pb[7])[:, 0:512]
                mm(C, sps, khat[g][:, kcol:kcol + 128], q_ap, True, True,
                   tag=f"sc{h}_{qb}")
                act(C, prb[i % 2][:, :], sps, AF.Exp, tag=f"exp{h}")
                mi = kc - 4 * qb + 2
                tt(C, prb[i % 2][:, :], prb[i % 2][:, :],
                   bmask[:, mi * 512:(mi + 1) * 512], ALU.mult, tag=f"msk{h}")
                if kc < 0:
                    ts(C, prb[i % 2][:, :], prb[i % 2][:, :], hval[:, kc + 2:kc + 3],
                       ALU.mult, tag=f"hv{h}")
                mm(C, zps, C.ones_bf_col[:], prb[i % 2][:, :], first, last,
                   tag=f"z{h}")
                mm(C, ops, vtok[kc + 2][:, g * 128:(g + 1) * 128],
                   prb[i % 2][:, :], first, last, tag=f"av{h}")
            ve(C, "reciprocal", zrow, zps, tag=f"rz{h}")
            bcast_row(C, pb[6][:, 0:512], zrow, 512, tag=f"bz{h}")
            cp(C, rzb[:, :], pb[6][:, 0:512], tag=f"rzb{h}")
            tt(C, obuf[h][:, qb * 512:(qb + 1) * 512], ops, rzb[:, :], ALU.mult,
               tag=f"oev{h}")
    C.pop_scratch()
    C.P.fence()

    C.push_scratch()
    wo = [C.scratch(f"wo{i}", [128, KT * 128], BF16) for i in range(2)]
    def load_wo(j):
        return dma(C, "gpsimd",
                   wo[j % 2].ap().rearrange("p (kt n) -> p kt n", kt=KT),
                   io["woT"].rearrange("(kt p) n -> p kt n", p=128)[:, :, j * 128:(j + 1) * 128],
                   tag=f"ldwo{j}")
    load_wo(0)
    for oc in range(KT):
        if oc + 1 < KT:
            load_wo(oc + 1)
        for c in range(2):
            ps = pb[(oc * 2 + c) % 4][:, 0:512]
            for i in range(H0):
                lhsT = wo[oc % 2].ap().rearrange("p (kt n) -> p kt n", kt=KT)[:, i, :]
                mm(C, ps, lhsT, obuf[i][:, c * 512:(c + 1) * 512],
                   i == 0, i == H0 - 1, tag=f"wo{oc}")
            tt(C, C.xown[oc][:, c * 512:(c + 1) * 512],
               C.xown[oc][:, c * 512:(c + 1) * 512], ps, ALU.add,
               tag=f"res{oc}")
    C.pop_scratch()
    C.pop_scratch()
    C.P.fence()



def build_l0attn_program():
    nc = bass.Bass("TRN2", target_bir_lowering=False, num_devices=8)
    io = declare_io(nc, "x1")
    prog = Prog()
    C = Ctx(nc, prog, scratch_bytes=95 * 1024)
    C.xown = [C.sb(f"xown{k}", [128, TOK], F32) for k in range(KT)]
    C.xnh = [C.sb(f"xnh{k}", [128, TH], BF16) for k in range(KT)]
    C.xhalo = [C.sb(f"xhalo{k}", [128, HALO], BF16) for k in range(KT)]
    pb = [nc.alloc_psum_tensor(f"pb{i}", [128, 512], F32) for i in range(8)]
    pb = [t.ap() for t in pb]
    build_consts(C)
    C.finish_persistent()
    load_xin(C, io)
    emit_l0_attn(C, io, pb)
    fins = []
    for k in range(KT):
        fins.append(dma(C, "sync", io["xoutT"][k * 128:(k + 1) * 128, :],
                        C.xown[k][:, :], tag=f"sto{k}"))
    prog.emit("sync", lambda e: e.nop(), deps=fins, tag="fin")
    prog.run(nc)
    return nc, io



def make_rope_tables(positions):
    dh = DH
    inv = 1.0 / (10000.0 ** (np.arange(0, dh, 2, dtype=np.float64) / dh))
    ang = positions.astype(np.float64)[None, :] * inv[:, None]
    cos = np.concatenate([np.cos(ang), np.cos(ang)], axis=0)
    sin = np.concatenate([np.sin(ang), np.sin(ang)], axis=0)
    sinm = sin.copy()
    sinm[0:64] *= -1.0
    return cos.astype(np.float32), sinm.astype(np.float32)


def make_band_masks():
    m = np.zeros((6, 128, 512), np.float32)
    ki = np.arange(128)[:, None]
    qi = np.arange(512)[None, :]
    for o in range(-2, 4):
        d = qi - ki - 128 * o
        ok = (d >= 0) & (d < WIN)
        m[o + 2] = np.where(ok, 1.0, 0.0)
    return m


def stage_l0attn(inp):
    x = inp["L0_attn_x"]
    wq = inp["L0_attn_Wq"]; wkv = inp["L0_attn_Wkv"]; wo = inp["L0_attn_Wo"]
    temp = inp["L0_attn_qk_temp"].reshape(1, 16)
    wqT = np.ascontiguousarray(wq.T)
    wkvT = np.ascontiguousarray(wkv.T)
    woT = np.ascontiguousarray(wo.T)
    bm = make_band_masks()
    maps = []
    for c in range(8):
        b, hf = c // 2, c % 2
        lo = hf * TOK - HALO
        xs = np.zeros((TH, D), np.float32)
        s0 = max(lo, 0)
        xs[s0 - lo:, :] = x[b, s0:hf * TOK + TOK, :]
        cos, sinm = make_rope_tables(np.arange(lo, lo + TH))
        hv = np.full((128, 2), 0.0 if hf == 0 else 1.0, np.float32)
        maps.append({
            "hvalid": hv,
            "xinT": np.ascontiguousarray(xs.T),
            "cosT": cos, "sinmT": sinm,
            "bmask": bm, "temp": temp.astype(np.float32),
            "wqT": wqT, "wkvT": wkvT, "woT": woT,
        })
    return maps





E0 = 8
MH = 1408
MHS = 2816


def emit_moe_gather(C, io):
    nc = C.nc
    groups = [[0, 1, 2, 3, 4, 5, 6, 7]]
    from concourse.replica_groups import maybe_share_collective_output_space
    space = maybe_share_collective_output_space("AllGather", groups)
    out = {}
    specs = [
        ("wg", [2048, MH], [8 * 2048, MH]),
        ("wu", [2048, MH], [8 * 2048, MH]),
        ("wd", [1408, 2048], [8 * 1408, 2048]),
        ("swg", [256, MHS], [2048, MHS]),
        ("swu", [256, MHS], [2048, MHS]),
        ("swd", [352, 2048], [2816, 2048]),
    ]
    for name, sshape, fshape in specs:
        cin = nc.dram_tensor(f"cc_{name}_in", sshape, F32, kind="Internal").ap()
        cout = nc.dram_tensor(f"cc_{name}_out", fshape, F32, kind="Internal",
                              addr_space=space).ap()
        d = dma(C, "gpsimd", cin, io[f"sh_{name}"], tag=f"cp_{name}")
        C.P.emit("gpsimd",
                 lambda e, ci=cin, co=cout, g=groups:
                     e.collective_compute("AllGather", ALU.bypass,
                                          replica_groups=g,
                                          ins=[ci.opt()], outs=[co.opt()]),
                 reads=[cin], writes=[cout], cc=True, tag=f"ag_{name}")
        out[name] = cout
    return out


def emit_l0_moe(C, io, pb, gath):
    nc = C.nc
    C.push_scratch()
    hbuf = [C.scratch(f"hb{i}", [128, TOK], BF16) for i in range(22)]
    cbb2 = [C.scratch(f"cbb{e}", [128, TOK], BF16) for e in range(2)]
    sgb = C.scratch("sgb", [128, TOK], BF16)
    gwT = C.scratch("gwT", [128, KT * 8], BF16)
    shgT = C.scratch("shgT", [128, KT * 1], BF16)
    iota8 = C.scratch("iota8", [128, 8], F32)
    cbT = C.scratch("cbT", [8, TOK], BF16)
    gsc = [C.scratch(f"gsc{i}", [128, 8], F32) for i in range(6)]
    gnm = C.scratch("gnm", [128, 1], F32)
    cb16 = C.scratch("cb16", [128, 8], BF16)

    C.push_scratch()
    mpool = C.scratch("mpool", [1, 2 * TOK], F32)
    emit_ln(C, [(C.xown, 0, 0, TOK)], C.xnh, mpool[0:1, 0:TOK],
            mpool[0:1, TOK:2 * TOK], pb, tag="ln1")
    C.pop_scratch()
    C.P.fence()
    rowg = C.scratch("rowg", [1, TOK], F32)
    wba = [C.scratch(f"mwa{i}", [128, KT * 128], BF16) for i in range(2)]
    wbb = [C.scratch(f"mwb{i}", [128, KT * 128], BF16) for i in range(2)]
    wbd = [C.scratch("mwd0", [128, 22 * 128], BF16)]

    xn = C.xnh

    dma(C, "gpsimd", gwT.ap().rearrange("p (kt n) -> p kt n", kt=KT),
        io["gate_wT"].rearrange("(kt p) n -> p kt n", p=128), tag="ldgw")
    dma(C, "gpsimd", shgT.ap()[:, 0:KT], io["shg"], tag="ldshg")
    C.P.emit("gpsimd", lambda e, o=iota8[:, :]: e.iota(o, pattern=[[1, 8]],
                                                       channel_multiplier=0,
                                                       allow_small_or_imprecise_dtypes=True),
             writes=[iota8], tag="iota8")

    for tc in range(8):
        lps = pb[0][:, 0:8]
        for k in range(KT):
            mm(C, lps, xn[k][:, tc * 128:(tc + 1) * 128],
               gwT.ap().rearrange("p (kt n) -> p kt n", kt=KT)[:, k, :],
               k == 0, k == KT - 1, tag=f"glog{tc}")
        lg, eL, t1, msk, e2, ew = gsc
        cp(C, lg[:, :], lps, tag=f"gcp{tc}")
        ve(C, "reduce_max", gnm[:, 0:1], lg[:, :], axis=AX.X, negate=True,
           tag=f"gm1{tc}")
        act(C, eL[:, :], lg[:, :], AF.Exp, bias=gnm[:, 0:1], tag=f"gexp{tc}")
        ts(C, msk[:, :], eL[:, :], 1.0, ALU.is_ge, tag=f"gc1{tc}")
        ve(C, "scalar_tensor_tensor", t1[:, :], msk[:, :], -1000.0, iota8[:, :],
           op0=ALU.mult, op1=ALU.add, tag=f"gt1{tc}")
        ve(C, "tensor_reduce", t1[:, 1:2], t1[:, :], axis=AX.X, op=ALU.min,
           tag=f"gi1{tc}")
        ts(C, t1[:, 1:2], t1[:, 1:2], 1000.0, ALU.add, tag=f"gi1b{tc}")
        ts(C, msk[:, :], iota8[:, :], t1[:, 1:2], ALU.is_equal, tag=f"gm1k{tc}")
        tt(C, e2[:, :], eL[:, :], msk[:, :], ALU.mult, tag=f"ge2a{tc}")
        tt(C, e2[:, :], eL[:, :], e2[:, :], ALU.subtract, tag=f"ge2{tc}")
        ve(C, "reduce_max", t1[:, 2:3], e2[:, :], axis=AX.X, tag=f"gm2{tc}")
        ts(C, ew[:, :], e2[:, :], t1[:, 2:3], ALU.is_ge, tag=f"gc2{tc}")
        ve(C, "scalar_tensor_tensor", ew[:, :],
           ew[:, :], -1000.0, iota8[:, :], op0=ALU.mult, op1=ALU.add, tag=f"gt2{tc}")
        ve(C, "tensor_reduce", t1[:, 3:4], ew[:, :], axis=AX.X, op=ALU.min,
           tag=f"gi2{tc}")
        ts(C, t1[:, 3:4], t1[:, 3:4], 1000.0, ALU.add, tag=f"gi2b{tc}")
        ts(C, ew[:, :], iota8[:, :], t1[:, 3:4], ALU.is_equal, tag=f"gm2k{tc}")
        tt(C, msk[:, :], msk[:, :], ew[:, :], ALU.add, tag=f"gms{tc}")
        tt(C, ew[:, :], msk[:, :], eL[:, :], ALU.mult, tag=f"gew{tc}")
        ve(C, "reduce_sum", t1[:, 4:5], ew[:, :], axis=AX.X, tag=f"gs12{tc}")
        ve(C, "reciprocal", t1[:, 5:6], t1[:, 4:5], tag=f"gr12{tc}")
        ts(C, e2[:, :], ew[:, :], t1[:, 5:6], ALU.mult, tag=f"gcmb{tc}")
        if DEBUG and tc == 1:
            dma(C, "sync", io["dbg_g"][:, 0:8], eL[:, :], tag="dbgeL")
            dma(C, "sync", io["dbg_g"][:, 8:16], msk[:, :], tag="dbgmsk")
            dma(C, "sync", io["dbg_g"][:, 16:24], ew[:, :], tag="dbgew")
            dma(C, "sync", io["dbg_g"][:, 24:32], t1[:, :], tag="dbgt1")
            dma(C, "sync", io["dbg_g"][:, 32:40], e2[:, :], tag="dbge2")
            dma(C, "sync", io["dbg_g"][:, 40:48], lg[:, :], tag="dbglg")
        cp(C, cb16[:, :], e2[:, :], tag=f"gcb{tc}")
        cbps = pb[7].bitcast(BF16)
        C.P.emit("tensor",
                 lambda e, o=cbps[0:8, 0:128], i=cb16[:, :],
                        idn=C.ident_bf[:, :]: e.transpose(o, i, idn),
                 reads=[cb16, C.ident_bf], writes=[pb[7]], tag=f"gtr{tc}")
        cp(C, cbT[0:8, tc * 128:(tc + 1) * 128], cbps[0:8, 0:128],
           tag=f"gcbT{tc}")
    sel8 = C.scratch("sel8", [8, 8 * 128], BF16)
    dma(C, "gpsimd", sel8[:, :], io["sel8"], tag="ldsel8")

    for c in range(2):
        sps = pb[3][0:1, 0:512]
        for k in range(KT):
            mm(C, sps, shgT.ap()[:, k:k + 1],
               xn[k][:, c * 512:(c + 1) * 512], k == 0, k == KT - 1,
               tag=f"sg{c}")
        act(C, rowg[0:1, c * 512:(c + 1) * 512], sps, AF.Sigmoid, tag=f"sgs{c}")
        bcast_row(C, pb[3][:, 0:512], rowg[0:1, c * 512:(c + 1) * 512], 512,
                  tag=f"sgb{c}")
        cp(C, sgb[:, c * 512:(c + 1) * 512], pb[3][:, 0:512], tag=f"sgbe{c}")

    def stream_w(buf, src_rows, src_c0, w, tag, nkt=KT):
        return dma(C, "gpsimd",
                   buf.ap()[:, 0:nkt * w].rearrange("p (kt n) -> p kt n", kt=nkt),
                   src_rows.rearrange("(kt p) n -> p kt n", p=128)[:, :, src_c0:src_c0 + w],
                   tag=tag)

    if DEBUG:
        dma(C, "sync", io["dbg_cb"], cbT[:, :], tag="dbgcb")
        dma(C, "sync", io["dbg_sg"], rowg[0:1, :], tag="dbgsg")
    nsh = MHS // 128
    for hc in range(nsh):
        stream_w(wba[hc % 2], gath["swg"], hc * 128, 128, f"lswg{hc}")
        stream_w(wbb[hc % 2], gath["swu"], hc * 128, 128, f"lswu{hc}")
        for c in range(2):
            gps = pb[0 + (hc % 2)][:, 0:512]
            ups = pb[2 + (hc % 2)][:, 0:512]
            for k in range(KT):
                mm(C, gps, wba[hc % 2].ap().rearrange("p (kt n) -> p kt n", kt=KT)[:, k, :],
                   xn[k][:, c * 512:(c + 1) * 512], k == 0, k == KT - 1, tag=f"shg{hc}")
            for k in range(KT):
                mm(C, ups, wbb[hc % 2].ap().rearrange("p (kt n) -> p kt n", kt=KT)[:, k, :],
                   xn[k][:, c * 512:(c + 1) * 512], k == 0, k == KT - 1, tag=f"shu{hc}")
            sl = hbuf[hc][:, c * 512:(c + 1) * 512]
            act(C, sl, gps, AF.Silu, tag=f"shsl{hc}")
            tt(C, sl, sl, ups, ALU.mult, tag=f"shmu{hc}")
            tt(C, sl, sl, sgb[:, c * 512:(c + 1) * 512], ALU.mult, tag=f"shsg{hc}")
    for oc in range(KT):
        stream_w(wbd[0], gath["swd"], oc * 128, 128, f"lswd{oc}", nkt=nsh)
        for c in range(2):
            ps = pb[4 + (oc * 2 + c) % 2][:, 0:512]
            for hc in range(nsh):
                mm(C, ps, wbd[0].ap()[:, 0:nsh * 128].rearrange("p (kt n) -> p kt n", kt=nsh)[:, hc, :],
                   hbuf[hc][:, c * 512:(c + 1) * 512], hc == 0, hc == nsh - 1,
                   tag=f"shd{oc}")
            tt(C, C.xown[oc][:, c * 512:(c + 1) * 512],
               C.xown[oc][:, c * 512:(c + 1) * 512], ps, ALU.add, tag=f"shres{oc}")

    if DEBUG:
        dma(C, "sync", io["dbg_h0"], hbuf[0][:, :], tag="dbgh0")
    nh = MH // 128
    wbd2 = [wbd[0], C.scratch("mwd1", [128, 22 * 128], BF16)]

    def emit_gu(e):
        hb = hbuf[(e % 2) * nh:(e % 2) * nh + nh]
        for c in range(2):
            mm(C, pb[6][:, 0:512], sel8[:, e * 128:(e + 1) * 128],
               cbT[0:8, c * 512:(c + 1) * 512], True, True, tag=f"cbbm{e}")
            cp(C, cbb2[e % 2][:, c * 512:(c + 1) * 512], pb[6][:, 0:512],
               tag=f"cbbe{e}")
        for hc in range(nh):
            stream_w(wba[hc % 2], gath["wg"][e * 2048:(e + 1) * 2048, :], hc * 128,
                     128, f"lwg{e}_{hc}")
            stream_w(wbb[hc % 2], gath["wu"][e * 2048:(e + 1) * 2048, :], hc * 128,
                     128, f"lwu{e}_{hc}")
            for c in range(2):
                gps = pb[0 + (hc % 2)][:, 0:512]
                ups = pb[2 + (hc % 2)][:, 0:512]
                for k in range(KT):
                    mm(C, gps, wba[hc % 2].ap().rearrange("p (kt n) -> p kt n", kt=KT)[:, k, :],
                       xn[k][:, c * 512:(c + 1) * 512], k == 0, k == KT - 1, tag=f"eg{e}")
                for k in range(KT):
                    mm(C, ups, wbb[hc % 2].ap().rearrange("p (kt n) -> p kt n", kt=KT)[:, k, :],
                       xn[k][:, c * 512:(c + 1) * 512], k == 0, k == KT - 1, tag=f"eu{e}")
                sl = hb[hc][:, c * 512:(c + 1) * 512]
                act(C, sl, gps, AF.Silu, tag=f"esl{e}")
                tt(C, sl, sl, ups, ALU.mult, tag=f"emu{e}")
                tt(C, sl, sl, cbb2[e % 2][:, c * 512:(c + 1) * 512], ALU.mult,
                   tag=f"ecb{e}")

    def emit_down(e):
        hb = hbuf[(e % 2) * nh:(e % 2) * nh + nh]
        for oc in range(KT):
            stream_w(wbd2[oc % 2], gath["wd"][e * 1408:(e + 1) * 1408, :], oc * 128,
                     128, f"lwd{e}_{oc}", nkt=nh)
            for c in range(2):
                ps = pb[4 + (oc * 2 + c) % 2][:, 0:512]
                for hc in range(nh):
                    mm(C, ps, wbd2[oc % 2].ap()[:, 0:nh * 128].rearrange("p (kt n) -> p kt n", kt=nh)[:, hc, :],
                       hb[hc][:, c * 512:(c + 1) * 512], hc == 0, hc == nh - 1,
                       tag=f"ed{e}")
                tt(C, C.xown[oc][:, c * 512:(c + 1) * 512],
                   C.xown[oc][:, c * 512:(c + 1) * 512], ps, ALU.add,
                   tag=f"eres{e}_{oc}")

    emit_gu(0)
    for e in range(E0):
        if e + 1 < E0:
            emit_gu(e + 1)
        emit_down(e)
    C.pop_scratch()
    C.P.fence()


def build_moe_program():
    nc = bass.Bass("TRN2", target_bir_lowering=False, num_devices=8)
    io = {}
    def inp(name, shape, dt=F32):
        io[name] = nc.dram_tensor(name, shape, dt, kind="ExternalInput").ap()
    inp("xres", [D, TOK])
    inp("gate_wT", [D, 8])
    inp("sel8", [8, 8 * 128])
    inp("shg", [128, KT])
    inp("sh_wg", [2048, MH]); inp("sh_wu", [2048, MH]); inp("sh_wd", [1408, 2048])
    inp("sh_swg", [256, MHS]); inp("sh_swu", [256, MHS]); inp("sh_swd", [352, 2048])
    io["xoutT"] = nc.dram_tensor("xoutT", [D, TOK], F32, kind="ExternalOutput").ap()
    if DEBUG:
        io["dbg_cb"] = nc.dram_tensor("dbg_cb", [8, TOK], BF16, kind="ExternalOutput").ap()
        io["dbg_sg"] = nc.dram_tensor("dbg_sg", [1, TOK], F32, kind="ExternalOutput").ap()
        io["dbg_h0"] = nc.dram_tensor("dbg_h0", [128, TOK], BF16, kind="ExternalOutput").ap()
        io["dbg_g"] = nc.dram_tensor("dbg_g", [128, 48], F32, kind="ExternalOutput").ap()
    prog = Prog()
    C = Ctx(nc, prog, scratch_bytes=95 * 1024)
    C.xown = [C.sb(f"xown{k}", [128, TOK], F32) for k in range(KT)]
    C.xnh = [C.sb(f"xnh{k}", [128, TH], BF16) for k in range(KT)]
    C.xhalo = None
    pb = [nc.alloc_psum_tensor(f"pb{i}", [128, 512], F32) for i in range(8)]
    pb = [t.ap() for t in pb]
    build_consts(C, ident=True)
    C.finish_persistent()
    for k in range(KT):
        dma(C, "sync", C.xown[k][:, :], io["xres"][k * 128:(k + 1) * 128, :],
            tag=f"ldx{k}")
    gath = emit_moe_gather(C, io)
    emit_l0_moe(C, io, pb, gath)
    fins = []
    for k in range(KT):
        fins.append(dma(C, "sync", io["xoutT"][k * 128:(k + 1) * 128, :],
                        C.xown[k][:, :], tag=f"sto{k}"))
    prog.emit("sync", lambda e: e.nop(), deps=fins, tag="fin")
    prog.run(nc)
    return nc, io


def stage_moe(inp, x1):
    g = inp["L0_ffn_gate_w"]
    shg = np.ascontiguousarray(inp["L0_ffn_shgate_w"].reshape(KT, 128).T)
    wg = np.ascontiguousarray(inp["L0_ffn_w_gate"].reshape(8 * 2048, MH))
    wu = np.ascontiguousarray(inp["L0_ffn_w_up"].reshape(8 * 2048, MH))
    wd = np.ascontiguousarray(inp["L0_ffn_w_down"].reshape(8 * 1408, 2048))
    swg = inp["L0_ffn_shw_gate"]; swu = inp["L0_ffn_shw_up"]; swd = inp["L0_ffn_shw_down"]
    gwT = np.ascontiguousarray(g.T)
    maps = []
    for c in range(8):
        b, hf = c // 2, c % 2
        xs = x1[b, hf * TOK:(hf + 1) * TOK, :]
        sel8 = np.zeros((8, 8 * 128), np.float32)
        for e in range(8):
            sel8[e, e * 128:(e + 1) * 128] = 1.0
        maps.append({
            "xres": np.ascontiguousarray(xs.T), "sel8": sel8,
            "gate_wT": gwT, "shg": shg.astype(np.float32),
            "sh_wg": np.ascontiguousarray(wg[c * 2048:(c + 1) * 2048]),
            "sh_wu": np.ascontiguousarray(wu[c * 2048:(c + 1) * 2048]),
            "sh_wd": np.ascontiguousarray(wd[c * 1408:(c + 1) * 1408]),
            "sh_swg": np.ascontiguousarray(swg[c * 256:(c + 1) * 256]),
            "sh_swu": np.ascontiguousarray(swu[c * 256:(c + 1) * 256]),
            "sh_swd": np.ascontiguousarray(swd[c * 352:(c + 1) * 352]),
        })
    return maps





def rope_apply(C, ps, out_bf, coss, sinm, c0, w, rp, sqb, tag):
    cc = coss[:, c0:c0 + w]
    ss = sinm[:, c0:c0 + w]
    r = rp[:, 0:w]
    tt(C, r, ps, cc, ALU.mult, tag=f"{tag}_cos")
    ve(C, "scalar_tensor_tensor", sqb[0:64, 0:w], ps[64:128, :], 1.0,
       ss[0:64, :], op0=ALU.mult, op1=ALU.mult, tag=f"{tag}_rot1")
    ve(C, "scalar_tensor_tensor", sqb[64:128, 0:w], ps[0:64, :], 1.0,
       ss[64:128, :], op0=ALU.mult, op1=ALU.mult, tag=f"{tag}_rot2")
    tt(C, r, r, sqb[:, 0:w], ALU.add, tag=f"{tag}_add")
    cp(C, out_bf, r, tag=f"{tag}_cast")


def emit_l1_attn(C, io, pb):
    nc = C.nc
    INV_SQ_DH = 1.0 / float(np.sqrt(128.0))
    C.push_scratch()
    coss = C.scratch("coss1", [DH, TOK], BF16)
    sinm = C.scratch("sinm1", [DH, TOK], BF16)
    khat = [C.scratch(f"k1_{g}", [DH, 2 * TOK], BF16) for g in range(HKV0)]
    vtok = [C.scratch(f"v1_{t}", [128, 512], BF16) for t in range(2 * TOK // 128)]
    cmask = C.scratch("cmask1", [128, KT * 512], BF16)
    rnrp = C.scratch("rnrp1", [128, 512], F32)
    rnsq = C.scratch("rnsq1", [128, 512], BF16)
    rowq = C.scratch("rowq1", [1, 1024], F32)
    zrow = rowq[0:1, 512:1024]
    rzb = C.scratch("rzb1", [128, 512], F32)
    prb = [C.scratch(f"prb1_{i}", [128, 512], BF16) for i in range(2)]
    wq = [C.scratch(f"wq1_{i}", [128, KT * 128], BF16) for i in range(2)]

    dma(C, "gpsimd", coss[:, :], io["cosT"][:, HALO:TH], tag="ldcos1")
    dma(C, "gpsimd", sinm[:, :], io["sinmT"][:, HALO:TH], tag="ldsin1")

    C.push_scratch()
    mpool = C.scratch("mpool1", [1, 2 * TOK], F32)
    emit_ln(C, [(C.xown, 0, 0, TOK)], C.xnh, mpool[0:1, 0:TOK],
            mpool[0:1, TOK:2 * TOK], pb, tag="ln2")
    C.pop_scratch()
    C.P.fence()
    xn = C.xnh

    C.push_scratch()
    kown = [C.scratch(f"ko{g}", [128, TOK], BF16) for g in range(HKV0)]
    def load_w1(buf, name, c0, tg):
        return dma(C, "gpsimd",
                   buf.ap().rearrange("p (kt n) -> p kt n", kt=KT),
                   io[name].rearrange("(kt p) n -> p kt n", p=128)[:, :, c0:c0 + 128],
                   tag=tg)
    kb = [(0, 1), (2, 3)]
    load_w1(wq[0], "wk1", 0, "ldk1_0")
    for g in range(HKV0):
        if g + 1 < HKV0:
            load_w1(wq[(g + 1) % 2], "wk1", (g + 1) * 128, f"ldk1_{g+1}")
        for c in range(2):
            ps = pb[kb[g % 2][c]][:, 0:512]
            for k in range(KT):
                mm(C, ps, wq[g % 2].ap().rearrange("p (kt n) -> p kt n", kt=KT)[:, k, :],
                   xn[k][:, c * 512:(c + 1) * 512], k == 0, k == KT - 1, tag=f"k1p{g}")
            rope_apply(C, ps, kown[g][:, c * 512:(c + 1) * 512], coss, sinm,
                       c * 512, 512, rnrp, rnsq, tag=f"k1r{g}")
    for half in range(2):
        for j in range(2):
            load_w1(wq[j], "wv1", (2 * half + j) * 128, f"ldv1_{half}{j}")
        for tc in range(TOK // 128):
            ps = pb[4 + (tc % 2)][:, 0:256]
            for j in range(2):
                for k in range(KT):
                    mm(C, ps[:, j * 128:j * 128 + 128],
                       xn[k][:, tc * 128:(tc + 1) * 128],
                       wq[j].ap().rearrange("p (kt n) -> p kt n", kt=KT)[:, k, :],
                       k == 0, k == KT - 1, tag=f"v1p{tc}")
            cp(C, vtok[tc][:, half * 256:half * 256 + 256], ps, tag=f"v1e{tc}")

    groups = [[0, 1], [2, 3], [4, 5], [6, 7]]
    cc_kown = nc.dram_tensor("cc_kown", [512, TOK], BF16, kind="Internal").ap()
    cc_kfull = nc.dram_tensor("cc_kfull", [1024, TOK], BF16, kind="Internal").ap()
    cc_vown = nc.dram_tensor("cc_vown", [TOK, 512], BF16, kind="Internal").ap()
    cc_vfull = nc.dram_tensor("cc_vfull", [2 * TOK, 512], BF16, kind="Internal").ap()
    for g in range(HKV0):
        dma(C, "sync", cc_kown[g * 128:(g + 1) * 128, :], kown[g][:, :], tag=f"kx{g}")
    for tc in range(TOK // 128):
        dma(C, "sync", cc_vown[tc * 128:(tc + 1) * 128, :], vtok[tc][:, :], tag=f"vx{tc}")
    C.pop_scratch()
    aoh = [C.scratch(f"aoh{i}", [128, 512], BF16) for i in range(H0)]
    q1b = [C.scratch(f"q1buf{i}", [128, 512], BF16) for i in range(2)]
    C.P.emit("gpsimd",
             lambda e, ci=cc_kown, co=cc_kfull: e.collective_compute(
                 "AllGather", ALU.bypass, replica_groups=groups,
                 ins=[ci.opt()], outs=[co.opt()]),
             reads=[cc_kown], writes=[cc_kfull], cc=True, tag="agk")
    C.P.emit("gpsimd",
             lambda e, ci=cc_vown, co=cc_vfull: e.collective_compute(
                 "AllGather", ALU.bypass, replica_groups=groups,
                 ins=[ci.opt()], outs=[co.opt()]),
             reads=[cc_vown], writes=[cc_vfull], cc=True, tag="agv")
    for g in range(HKV0):
        for r in range(2):
            dma(C, "sync", khat[g][:, r * TOK:(r + 1) * TOK],
                cc_kfull[r * 512 + g * 128: r * 512 + (g + 1) * 128, :], tag=f"kl{g}")
    for tc in range(2 * TOK // 128):
        dma(C, "sync", vtok[tc][:, :], cc_vfull[tc * 128:(tc + 1) * 128, :],
            tag=f"vl{tc}")

    def emit_q1(qb, h):
        ps = pb[h % 2][:, 0:512]
        for k in range(KT):
            mm(C, ps, wq[h % 2].ap().rearrange("p (kt n) -> p kt n", kt=KT)[:, k, :],
               xn[k][:, qb * 512:(qb + 1) * 512], k == 0, k == KT - 1,
               tag=f"q1p{h}")
        return ps

    load_w1(wq[0], "wq1", 0, "ldq1_0")
    qbh = [(qb, h) for qb in range(2) for h in range(H0)]
    pend = {}
    for idx, (qb, h) in enumerate(qbh):
        if idx == 0:
            pend[idx] = emit_q1(qb, h)
        if idx + 1 < len(qbh):
            nqb, nh = qbh[idx + 1]
            load_w1(wq[nh % 2], "wq1", nh * 128, f"ldq1_{nqb}_{nh}")
        if qb == 0 and h == 0:
            dma(C, "gpsimd", cmask[:, :],
                io["cmask"].rearrange("m p n -> p m n")[:, 0:KT, :], tag="ldcm0")
        g = h // GRP
        qps = pend.pop(idx)
        q1 = q1b[idx % 2]
        rope_apply(C, qps, q1[:, 0:512], coss, sinm, qb * 512, 512,
                   rnrp, rnsq, tag=f"q1r{h}")
        if idx + 1 < len(qbh):
            pend[idx + 1] = emit_q1(*qbh[idx + 1])
        zps = pb[3][0:1, 0:512]
        ops = pb[2][:, 0:512]
        for kc in range(KT):
            sps = (pb[6] if kc % 2 == 0 else pb[7])[:, 0:512]
            mm(C, sps, khat[g][:, kc * 128:(kc + 1) * 128], q1[:, 0:512],
               True, True, tag=f"s1{h}_{qb}")
            act(C, prb[kc % 2][:, :], sps, AF.Exp, scale=INV_SQ_DH, tag=f"s1e{h}")
            tt(C, prb[kc % 2][:, :], prb[kc % 2][:, :],
               cmask[:, kc * 512:(kc + 1) * 512], ALU.mult, tag=f"s1m{h}")
            mm(C, zps, C.ones_bf_col[:], prb[kc % 2][:, :], kc == 0, kc == KT - 1,
               tag=f"z1{h}")
            mm(C, ops, vtok[kc][:, g * 128:(g + 1) * 128], prb[kc % 2][:, :],
               kc == 0, kc == KT - 1, tag=f"av1{h}")
        ve(C, "reciprocal", zrow, zps, tag=f"rz1{h}")
        bcast_row(C, pb[4][:, 0:512], zrow, 512, tag=f"bz1{h}")
        cp(C, rzb[:, :], pb[4][:, 0:512], tag=f"rzb1{h}")
        tt(C, aoh[h][:, :], ops, rzb[:, :], ALU.mult, tag=f"oe1{h}")
        if h == H0 - 1:
            for oc in range(KT):
                load_w1(wq[oc % 2], "wo1", oc * 128, f"ldo1_{qb}_{oc}")
                ps = pb[4 + (oc % 2)][:, 0:512]
                for i in range(H0):
                    mm(C, ps, wq[oc % 2].ap().rearrange("p (kt n) -> p kt n", kt=KT)[:, i, :],
                       aoh[i][:, :], i == 0, i == H0 - 1, tag=f"wo1{oc}")
                tt(C, C.xown[oc][:, qb * 512:(qb + 1) * 512],
                   C.xown[oc][:, qb * 512:(qb + 1) * 512], ps, ALU.add, tag=f"r1{oc}")
            if qb == 0:
                dma(C, "gpsimd", cmask[:, :],
                    io["cmask"].rearrange("m p n -> p m n")[:, KT:2 * KT, :],
                    tag="ldcm1")
    C.pop_scratch()
    C.P.fence()


def build_l1attn_program():
    nc = bass.Bass("TRN2", target_bir_lowering=False, num_devices=8)
    io = {}
    def inp(name, shape, dt=F32):
        io[name] = nc.dram_tensor(name, shape, dt, kind="ExternalInput").ap()
    inp("xres", [D, TOK])
    inp("cosT", [DH, TH]); inp("sinmT", [DH, TH])
    inp("cmask", [32, DH, 512])
    inp("wq1", [D, D]); inp("wk1", [D, 512]); inp("wv1", [D, 512]); inp("wo1", [D, D])
    io["xoutT"] = nc.dram_tensor("xoutT", [D, TOK], F32, kind="ExternalOutput").ap()
    prog = Prog()
    C = Ctx(nc, prog, scratch_bytes=95 * 1024)
    C.xown = [C.sb(f"xown{k}", [128, TOK], F32) for k in range(KT)]
    C.xnh = [C.sb(f"xnh{k}", [128, TH], BF16) for k in range(KT)]
    pb = [nc.alloc_psum_tensor(f"pb{i}", [128, 512], F32) for i in range(8)]
    pb = [t.ap() for t in pb]
    build_consts(C)
    C.finish_persistent()
    for k in range(KT):
        dma(C, "sync", C.xown[k][:, :], io["xres"][k * 128:(k + 1) * 128, :],
            tag=f"ldx{k}")
    emit_l1_attn(C, io, pb)
    fins = []
    for k in range(KT):
        fins.append(dma(C, "sync", io["xoutT"][k * 128:(k + 1) * 128, :],
                        C.xown[k][:, :], tag=f"sto{k}"))
    prog.emit("sync", lambda e: e.nop(), deps=fins, tag="fin")
    prog.run(nc)
    return nc, io


def stage_l1attn(inp, x2):
    maps = []
    for c in range(8):
        b, hf = c // 2, c % 2
        xs = x2[b, hf * TOK:(hf + 1) * TOK, :]
        lo = hf * TOK - HALO
        cos, sinm = make_rope_tables(np.arange(lo, lo + TH))
        cm = np.zeros((2, KT, DH, 512), np.float32)
        for qb in range(2):
            for kc in range(KT):
                kg = kc * 128 + np.arange(128)[:, None]
                qg = hf * TOK + qb * 512 + np.arange(512)[None, :]
                cm[qb, kc] = np.where(qg >= kg, 1.0, 0.0)
        maps.append({
            "xres": np.ascontiguousarray(xs.T),
            "cosT": cos, "sinmT": sinm,
            "cmask": cm.reshape(32, DH, 512),
            "wq1": inp["L1_attn_Wq"], "wk1": inp["L1_attn_Wk"],
            "wv1": inp["L1_attn_Wv"], "wo1": inp["L1_attn_Wo"],
        })
    return maps





N1 = 16384
GELU_C = 0.7978845608028654
GELU_B = 0.044715


def emit_l1_dequant(C, io, pb, name, half_scale=False):
    nc = C.nc
    cc_in = nc.dram_tensor(f"cc_{name}_in", [2048, 2048], BF16, kind="Internal").ap()
    C.push_scratch()
    pk = C.scratch(f"dq_{name}_pk", [128, 1024], I32)
    sc = C.scratch(f"dq_{name}_sc", [128, 64], F32)
    ta = C.scratch(f"dq_{name}_ta", [128, 1024], I32)
    tb = C.scratch(f"dq_{name}_tb", [128, 1024], I32)
    td = C.scratch(f"dq_{name}_td", [128, 1024], I32)
    u16 = C.scratch(f"dq_{name}_u16", [128, 1024], U16)
    wcode = C.scratch(f"dq_{name}_wc", [128, 2048], BF16)
    scex = C.scratch(f"dq_{name}_se", [128, 2048], F32)
    wt = [C.scratch(f"dq_{name}_wt{i}", [128, 128], BF16) for i in range(2)]
    cbps = pb[7].bitcast(BF16)
    for t in range(KT):
        dma(C, "sync", pk[:, :], io[f"p{name}"][t * 128:(t + 1) * 128, :], tag=f"dqp{t}")
        dma(C, "sync", sc[:, :], io[f"p{name}_sc"][t * 128:(t + 1) * 128, :], tag=f"dqs{t}")
        for half in range(2):
            if half == 0:
                ts(C, ta[:, :], pk[:, :], 15, ALU.bitwise_and, tag=f"dq_c{t}")
            else:
                ts(C, ta[:, :], pk[:, :], 4, ALU.logical_shift_right, 15,
                   ALU.bitwise_and, tag=f"dq_c{t}")
            ts(C, tb[:, :], ta[:, :], 7, ALU.bitwise_and, tag=f"dq_m{t}")
            ts(C, td[:, :], tb[:, :], 0x40, ALU.mult, 0x3F00, ALU.add, tag=f"dq_t1{t}")
            ts(C, tb[:, :], tb[:, :], 0x3F00, ALU.mult, tag=f"dq_t2{t}")
            tt(C, td[:, :], td[:, :], tb[:, :], ALU.min, tag=f"dq_min{t}")
            ts(C, ta[:, :], ta[:, :], 8, ALU.bitwise_and, 12, ALU.logical_shift_left,
               tag=f"dq_s{t}")
            tt(C, td[:, :], td[:, :], ta[:, :], ALU.bitwise_or, tag=f"dq_or{t}")
            cp(C, u16[:, :], td[:, :], tag=f"dq_u{t}")
            cp(C, wcode.ap().rearrange("p (n two) -> p n two", two=2)[:, :, half],
               u16[:].bitcast(BF16), tag=f"dq_il{t}")
        cp(C, scex.ap().rearrange("p (b e) -> p b e", e=32),
           sc[:].rearrange("p (b o) -> p b o", o=1).broadcast_to([128, 64, 32]),
           tag=f"dq_sx{t}")
        if half_scale:
            ts(C, scex[:, :], scex[:, :], 0.5, ALU.mult, tag=f"dq_hs{t}")
        tt(C, wcode[:, :], wcode[:, :], scex[:, :], ALU.mult, tag=f"dq_sm{t}")
        for j in range(KT):
            C.P.emit("tensor",
                     lambda e, o=cbps[0:128, 0:128], i=wcode[:, j * 128:(j + 1) * 128],
                            idn=C.ident_bf[:, :]: e.transpose(o, i, idn),
                     reads=[wcode, C.ident_bf], writes=[pb[7]], tag=f"dq_tr{t}_{j}")
            cp(C, wt[j % 2][:, :], cbps[0:128, 0:128], tag=f"dq_te{t}_{j}")
            dma(C, "sync", cc_in[j * 128:(j + 1) * 128, t * 128:(t + 1) * 128],
                wt[j % 2][:, :], tag=f"dq_st{t}_{j}")
    C.pop_scratch()
    return cc_in


def emit_l1_ffn(C, io, pb):
    nc = C.nc
    groups = [[0, 1, 2, 3, 4, 5, 6, 7]]
    from concourse.replica_groups import maybe_share_collective_output_space
    space = maybe_share_collective_output_space("AllGather", groups)
    gcc = {}
    for name, hs in (("g1", False), ("u1", False), ("d1", True)):
        cin = emit_l1_dequant(C, io, pb, name, half_scale=hs)
        cout = nc.dram_tensor(f"cc_{name}_out", [8 * 2048, 2048], BF16,
                              kind="Internal", addr_space=space).ap()
        C.P.emit("gpsimd",
                 lambda e, ci=cin, co=cout: e.collective_compute(
                     "AllGather", ALU.bypass, replica_groups=groups,
                     ins=[ci.opt()], outs=[co.opt()]),
                 reads=[cin], writes=[cout], cc=True, tag=f"ag_{name}")
        gcc[name] = cout

    C.push_scratch()
    mpool = C.scratch("mpool3", [1, 2 * TOK], F32)
    emit_ln(C, [(C.xown, 0, 0, TOK)], C.xnh, mpool[0:1, 0:TOK],
            mpool[0:1, TOK:2 * TOK], pb, tag="ln3")
    C.pop_scratch()
    C.P.fence()
    xn = C.xnh

    C.push_scratch()
    ibuf = [C.scratch(f"ib{i}", [128, TOK], BF16) for i in range(16)]
    wgb = [C.scratch(f"f_wg{i}", [128, KT * 128], BF16) for i in range(2)]
    wub = [C.scratch(f"f_wu{i}", [128, KT * 128], BF16) for i in range(2)]
    wdb = [C.scratch(f"f_wd{i}", [128, 8 * 128], BF16) for i in range(2)]
    tmp1 = C.scratch("f_t1", [128, 512], F32)
    tmp2 = C.scratch("f_t2", [128, 512], F32)

    def load_gu(buf, cout, rank, ncol, tg):
        return dma(C, "sync",
                   buf.ap().rearrange("p (kt n) -> p kt n", kt=KT),
                   cout[rank * 2048:(rank + 1) * 2048, ncol * 128:(ncol + 1) * 128]
                       .rearrange("(kt p) n -> p kt n", p=128), tag=tg)

    def f_gu(g):
        rank = g // 2
        ib = ibuf[(g % 2) * 8:(g % 2) * 8 + 8]
        for nch in range(8):
            ncol = (g % 2) * 8 + nch
            load_gu(wgb[nch % 2], gcc["g1"], rank, ncol, f"lg{g}_{nch}")
            load_gu(wub[nch % 2], gcc["u1"], rank, ncol, f"lu{g}_{nch}")
            for c in range(2):
                gps = pb[0 + (nch % 2)][:, 0:512]
                ups = pb[2 + (nch % 2)][:, 0:512]
                for k in range(KT):
                    mm(C, gps, wgb[nch % 2].ap().rearrange("p (kt n) -> p kt n", kt=KT)[:, k, :],
                       xn[k][:, c * 512:(c + 1) * 512], k == 0, k == KT - 1, tag=f"fg{g}")
                for k in range(KT):
                    mm(C, ups, wub[nch % 2].ap().rearrange("p (kt n) -> p kt n", kt=KT)[:, k, :],
                       xn[k][:, c * 512:(c + 1) * 512], k == 0, k == KT - 1, tag=f"fu{g}")
                sl = ib[nch][:, c * 512:(c + 1) * 512]
                act(C, tmp1[:, :], gps, AF.Square, tag=f"fsq{g}")
                ts(C, tmp1[:, :], tmp1[:, :], GELU_C * GELU_B, ALU.mult, GELU_C,
                   ALU.add, tag=f"fu1{g}")
                tt(C, tmp2[:, :], tmp1[:, :], gps, ALU.mult, tag=f"fu2{g}")
                act(C, tmp2[:, :], tmp2[:, :], AF.Tanh, tag=f"fth{g}")
                ts(C, tmp2[:, :], tmp2[:, :], 1.0, ALU.add, tag=f"fv1{g}")
                tt(C, tmp2[:, :], tmp2[:, :], gps, ALU.mult, tag=f"fv2{g}")
                tt(C, sl, tmp2[:, :], ups, ALU.mult, tag=f"fiv{g}")

    def f_down(g):
        ib = ibuf[(g % 2) * 8:(g % 2) * 8 + 8]
        for oc in range(KT):
            dma(C, "sync",
                wdb[oc % 2].ap().rearrange("p (kt n) -> p kt n", kt=8),
                gcc["d1"][g * 1024:(g + 1) * 1024, oc * 128:(oc + 1) * 128]
                    .rearrange("(kt p) n -> p kt n", p=128), tag=f"ld{g}_{oc}")
            for c in range(2):
                ps = pb[4 + (oc % 2)][:, 0:512]
                for nch in range(8):
                    mm(C, ps, wdb[oc % 2].ap().rearrange("p (kt n) -> p kt n", kt=8)[:, nch, :],
                       ib[nch][:, c * 512:(c + 1) * 512], nch == 0, nch == 7,
                       tag=f"fd{g}")
                tt(C, C.xown[oc][:, c * 512:(c + 1) * 512],
                   C.xown[oc][:, c * 512:(c + 1) * 512], ps, ALU.add, tag=f"fr{g}_{oc}")

    f_gu(0)
    for g in range(16):
        if g + 1 < 16:
            f_gu(g + 1)
        f_down(g)
    C.pop_scratch()
    C.P.fence()


def build_l1ffn_program():
    nc = bass.Bass("TRN2", target_bir_lowering=False, num_devices=8)
    io = {}
    def inp(name, shape, dt=F32):
        io[name] = nc.dram_tensor(name, shape, dt, kind="ExternalInput").ap()
    inp("xres", [D, TOK])
    for n in ("g1", "u1", "d1"):
        inp(f"p{n}", [2048, 1024], I32)
        inp(f"p{n}_sc", [2048, 64])
    io["xoutT"] = nc.dram_tensor("xoutT", [D, TOK], F32, kind="ExternalOutput").ap()
    prog = Prog()
    C = Ctx(nc, prog, scratch_bytes=95 * 1024)
    C.xown = [C.sb(f"xown{k}", [128, TOK], F32) for k in range(KT)]
    C.xnh = [C.sb(f"xnh{k}", [128, TH], BF16) for k in range(KT)]
    pb = [nc.alloc_psum_tensor(f"pb{i}", [128, 512], F32) for i in range(8)]
    pb = [t.ap() for t in pb]
    build_consts(C, ident=True)
    C.finish_persistent()
    for k in range(KT):
        dma(C, "sync", C.xown[k][:, :], io["xres"][k * 128:(k + 1) * 128, :],
            tag=f"ldx{k}")
    emit_l1_ffn(C, io, pb)
    fins = []
    for k in range(KT):
        fins.append(dma(C, "sync", io["xoutT"][k * 128:(k + 1) * 128, :],
                        C.xown[k][:, :], tag=f"sto{k}"))
    prog.emit("sync", lambda e: e.nop(), deps=fins, tag="fin")
    prog.run(nc)
    return nc, io


def stage_l1ffn(inp, x3):
    maps = []
    for c in range(8):
        b, hf = c // 2, c % 2
        xs = x3[b, hf * TOK:(hf + 1) * TOK, :]
        maps.append({
            "xres": np.ascontiguousarray(xs.T),
            "pg1": np.ascontiguousarray(inp["L1_ffn_gate_W_packed"][c * 2048:(c + 1) * 2048]),
            "pg1_sc": np.ascontiguousarray(inp["L1_ffn_gate_scales"][c * 2048:(c + 1) * 2048]),
            "pu1": np.ascontiguousarray(inp["L1_ffn_up_W_packed"][c * 2048:(c + 1) * 2048]),
            "pu1_sc": np.ascontiguousarray(inp["L1_ffn_up_scales"][c * 2048:(c + 1) * 2048]),
            "pd1": np.ascontiguousarray(inp["L1_ffn_down_W_packed"][:, c * 1024:(c + 1) * 1024]),
            "pd1_sc": np.ascontiguousarray(inp["L1_ffn_down_scales"][:, c * 64:(c + 1) * 64]),
        })
    return maps





def build_fused_program():
    nc = bass.Bass("TRN2", target_bir_lowering=False, num_devices=8)
    io = {}
    def inp(name, shape, dt=F32):
        io[name] = nc.dram_tensor(name, shape, dt, kind="ExternalInput").ap()
    inp("xinT", [D, TH])
    inp("cosT", [DH, TH]); inp("sinmT", [DH, TH])
    inp("bmask", [6, DH, 512])
    inp("temp", [1, 16]); inp("hvalid", [DH, 2])
    inp("wqT", [D, D]); inp("wkvT", [D, 1024]); inp("woT", [D, D])
    inp("gate_wT", [D, 8]); inp("sel8", [8, 8 * 128]); inp("shg", [128, KT])
    inp("sh_wg", [2048, MH]); inp("sh_wu", [2048, MH]); inp("sh_wd", [1408, 2048])
    inp("sh_swg", [256, MHS]); inp("sh_swu", [256, MHS]); inp("sh_swd", [352, 2048])
    inp("cmask", [32, DH, 512])
    inp("wq1", [D, D]); inp("wk1", [D, 512]); inp("wv1", [D, 512]); inp("wo1", [D, D])
    for n in ("g1", "u1", "d1"):
        inp(f"p{n}", [2048, 1024], I32)
        inp(f"p{n}_sc", [2048, 64])
    io["xoutT"] = nc.dram_tensor("xoutT", [D, TOK], F32, kind="ExternalOutput").ap()

    prog = Prog()
    C = Ctx(nc, prog, scratch_bytes=93 * 1024)
    C.xown = [C.sb(f"xown{k}", [128, TOK], F32) for k in range(KT)]
    C.xnh = [C.sb(f"xnh{k}", [128, TH], BF16) for k in range(KT)]
    C.xhalo = [C.sb(f"xhalo{k}", [128, HALO], BF16) for k in range(KT)]
    pb = [nc.alloc_psum_tensor(f"pb{i}", [128, 512], F32) for i in range(8)]
    pb = [t.ap() for t in pb]
    build_consts(C, ident=True)
    C.finish_persistent()
    load_xin(C, io)
    gath = emit_moe_gather(C, io)
    emit_l0_attn(C, io, pb)
    emit_l0_moe(C, io, pb, gath)
    emit_l1_attn(C, io, pb)
    emit_l1_ffn(C, io, pb)
    fins = []
    for k in range(KT):
        fins.append(dma(C, "sync", io["xoutT"][k * 128:(k + 1) * 128, :],
                        C.xown[k][:, :], tag=f"sto{k}"))
    prog.emit("sync", lambda e: e.nop(), deps=fins, tag="fin")
    prog.run(nc)
    return nc, io


def stage_fused(inp):
    mapsA = stage_l0attn(inp)
    g = inp["L0_ffn_gate_w"]
    shg = np.ascontiguousarray(inp["L0_ffn_shgate_w"].reshape(KT, 128).T).astype(np.float32)
    wg = np.ascontiguousarray(inp["L0_ffn_w_gate"].reshape(8 * 2048, MH))
    wu = np.ascontiguousarray(inp["L0_ffn_w_up"].reshape(8 * 2048, MH))
    wd = np.ascontiguousarray(inp["L0_ffn_w_down"].reshape(8 * 1408, 2048))
    swg = inp["L0_ffn_shw_gate"]; swu = inp["L0_ffn_shw_up"]; swd = inp["L0_ffn_shw_down"]
    gwT = np.ascontiguousarray(g.T)
    sel8 = np.zeros((8, 8 * 128), np.float32)
    for e in range(8):
        sel8[e, e * 128:(e + 1) * 128] = 1.0
    maps = []
    for c in range(8):
        b, hf = c // 2, c % 2
        m = dict(mapsA[c])
        m.update({
            "sel8": sel8, "gate_wT": gwT, "shg": shg,
            "sh_wg": np.ascontiguousarray(wg[c * 2048:(c + 1) * 2048]),
            "sh_wu": np.ascontiguousarray(wu[c * 2048:(c + 1) * 2048]),
            "sh_wd": np.ascontiguousarray(wd[c * 1408:(c + 1) * 1408]),
            "sh_swg": np.ascontiguousarray(swg[c * 256:(c + 1) * 256]),
            "sh_swu": np.ascontiguousarray(swu[c * 256:(c + 1) * 256]),
            "sh_swd": np.ascontiguousarray(swd[c * 352:(c + 1) * 352]),
            "wq1": inp["L1_attn_Wq"], "wk1": inp["L1_attn_Wk"],
            "wv1": inp["L1_attn_Wv"], "wo1": inp["L1_attn_Wo"],
            "pg1": np.ascontiguousarray(inp["L1_ffn_gate_W_packed"][c * 2048:(c + 1) * 2048]),
            "pg1_sc": np.ascontiguousarray(inp["L1_ffn_gate_scales"][c * 2048:(c + 1) * 2048]),
            "pu1": np.ascontiguousarray(inp["L1_ffn_up_W_packed"][c * 2048:(c + 1) * 2048]),
            "pu1_sc": np.ascontiguousarray(inp["L1_ffn_up_scales"][c * 2048:(c + 1) * 2048]),
            "pd1": np.ascontiguousarray(inp["L1_ffn_down_W_packed"][:, c * 1024:(c + 1) * 1024]),
            "pd1_sc": np.ascontiguousarray(inp["L1_ffn_down_scales"][:, c * 64:(c + 1) * 64]),
        })
        cm = np.zeros((2, KT, DH, 512), np.float32)
        for qb in range(2):
            for kc in range(KT):
                kg = kc * 128 + np.arange(128)[:, None]
                qg = hf * TOK + qb * 512 + np.arange(512)[None, :]
                cm[qb, kc] = np.where(qg >= kg, 1.0, 0.0)
        m["cmask"] = cm.reshape(32, DH, 512)
        maps.append(m)
    return maps





_CACHE = {}


def kernel(**inputs):
    inp = {k: np.asarray(v) for k, v in inputs.items()}
    if "p" not in _CACHE:
        _CACHE["p"] = build_fused_program()[0]
    nc = _CACHE["p"]
    maps = stage_fused(inp)
    res = run_bass_kernel_spmd(nc, maps, core_ids=list(range(8)))
    out = np.zeros((4, 2048, 2048), np.float32)
    for c in range(8):
        b, hf = c // 2, c % 2
        out[b, hf * TOK:(hf + 1) * TOK, :] = np.asarray(res.results[c]["xoutT"]).T
    return out


# version 08
# speedup vs baseline: 1.1758x; 1.0431x over previous
import sys
sys.path.insert(0, "/opt/trn_rl_repo")
import numpy as np
import ml_dtypes
import concourse.bass as bass
import concourse.mybir as mybir
from concourse.bass_utils import run_bass_kernel_spmd

F32 = mybir.dt.float32
BF16 = mybir.dt.bfloat16
I32 = mybir.dt.int32
U16 = mybir.dt.uint16
AF = mybir.ActivationFunctionType
ALU = mybir.AluOpType
AX = mybir.AxisListType

DEBUG = False
ENGINES = ("sync", "tensor", "vector", "scalar", "gpsimd")
NEG = -1e30


class Ins:
    __slots__ = ("eng", "build", "deps", "dma", "cc", "sig", "sem", "val",
                 "tag", "gid", "pos")
    def __init__(self, eng, build, dma, cc, tag, gid):
        self.eng = eng; self.build = build; self.dma = dma; self.cc = cc
        self.tag = tag; self.gid = gid
        self.deps = []; self.sig = False; self.sem = None; self.val = None
        self.pos = None

def _bkey(x):
    if hasattr(x, "tensor"):
        return x.tensor.name
    return x.name

class Prog:
    N_DMA_SEMS = 12

    def __init__(self):
        self.instrs = []
        self.streams = {e: [] for e in ENGINES}
        self.last_write = {}
        self.readers = {}
        self.fence_deps = []
        self.relax_same = {"tensor"}

    def emit(self, eng, build, reads=(), writes=(), dma=False, cc=False,
             tag="", deps=(), untracked_write=False):
        ins = Ins(eng, build, dma, cc, tag, len(self.instrs))
        dd = {}
        for b in reads:
            w = self.last_write.get(_bkey(b))
            if w is not None:
                dd[w.gid] = w
        if not untracked_write:
            for b in writes:
                k = _bkey(b)
                w = self.last_write.get(k)
                if w is not None:
                    dd[w.gid] = w
                for r in self.readers.get(k, ()):
                    dd[r.gid] = r
        for d in deps:
            dd[d.gid] = d
        for d in self.fence_deps:
            dd[d.gid] = d
        ins.deps = [d for d in dd.values() if d is not ins]
        for b in reads:
            self.readers.setdefault(_bkey(b), []).append(ins)
        if not untracked_write:
            for b in writes:
                k = _bkey(b)
                self.last_write[k] = ins
                self.readers[k] = []
        self.instrs.append(ins)
        self.streams[eng].append(ins)
        ins.pos = len(self.streams[eng]) - 1
        return ins

    def fence(self):
        self.fence_deps = [s[-1] for s in self.streams.values() if s]
        for d in self.fence_deps:
            d.sig = True

    def _need_sem(self, p, d):
        if p.dma or p.cc or d.dma or d.cc:
            return True
        if p.eng != d.eng:
            return True
        return p.eng not in self.relax_same

    def run(self, nc):
        for ins in self.instrs:
            for p in ins.deps:
                if self._need_sem(p, ins):
                    p.sig = True
        csem = {e: nc.alloc_semaphore(f"c_{e}") for e in ENGINES}
        dsem = {e: [nc.alloc_semaphore(f"d_{e}{i}") for i in range(self.N_DMA_SEMS)]
                for e in ("sync", "scalar", "gpsimd", "vector", "tensor")}
        duse = {e: 0 for e in dsem}
        ccount = {e: 0 for e in ENGINES}
        for e in ENGINES:
            for ins in self.streams[e]:
                if ins.cc:
                    ins.sem = nc.alloc_semaphore(f"cc{ins.gid}")
                    ins.val = 1
                elif ins.dma:
                    i = duse[e] % self.N_DMA_SEMS
                    ins.sem = dsem[e][i]
                    ins.val = 16 * (duse[e] // self.N_DMA_SEMS + 1)
                    duse[e] += 1
                elif ins.sig:
                    ccount[e] += 1
                    ins.sem = csem[e]
                    ins.val = ccount[e]
        prog = self

        def mk_body(e):
            def body(eng):
                water = {}
                for ins in prog.streams[e]:
                    for p in ins.deps:
                        if not prog._need_sem(p, ins):
                            continue
                        assert p.sem is not None, (p.tag, p.eng, ins.tag)
                        k = id(p.sem)
                        if water.get(k, 0) >= p.val:
                            continue
                        eng.wait_ge(p.sem, p.val)
                        water[k] = p.val
                    bi = ins.build(eng)
                    if ins.sem is not None:
                        if ins.dma:
                            bi.then_inc(ins.sem, 16)
                        else:
                            bi.then_inc(ins.sem, 1)
            return body

        with nc.Block() as block:
            block.sync(mk_body("sync"))
            block.tensor(mk_body("tensor"))
            block.vector(mk_body("vector"))
            block.scalar(mk_body("scalar"))
            block.gpsimd(mk_body("gpsimd"))
        return



class Ctx:
    def __init__(self, nc, prog, scratch_bytes=0):
        self.nc = nc
        self.P = prog
        self.cc_last = None
        self.tiles = {}
        self.uid = 0
        part_sz = nc.SBUF_PARTITION_SIZE_BYTES
        self.scratch_base = part_sz - scratch_bytes
        self.scratch_off = self.scratch_base
        self._scratch_stack = []

    def sb(self, name, shape, dtype):
        h = self.nc.alloc_sbuf_tensor(name, shape, dtype)
        return h

    def finish_persistent(self):
        r = self.nc.bump_sbuf(32)
        assert r is not None, "sbuf exhausted"
        start = r[0] if isinstance(r, tuple) else r
        assert start <= self.scratch_base, \
            f"persistent region ({start}) collides with scratch base ({self.scratch_base})"

    def push_scratch(self):
        self._scratch_stack.append(self.scratch_off)

    def pop_scratch(self):
        self.scratch_off = self._scratch_stack.pop()

    def scratch(self, name, shape, dtype):
        left = self.nc.bytes_allocated(None) if False else None
        nbytes = int(np.prod(shape[1:])) * _dtb(dtype)
        off = (self.scratch_off + 31) & ~31
        assert off + nbytes <= self.nc.SBUF_PARTITION_SIZE_BYTES, \
            f"scratch overflow {name}: {off + nbytes}"
        h = self.nc.alloc_sbuf_tensor_at(f"{name}_{self.uid}", shape, dtype, offset=off)
        self.uid += 1
        self.scratch_off = off + nbytes
        return h

def _dtb(dt):
    return {F32: 4, I32: 4, BF16: 2, U16: 2}[dt]



def _isbuf(x):
    return hasattr(x, "tensor") or hasattr(x, "name")

def dma(C, eng, out, in_, tag="dma", deps=(), untracked=False):
    return C.P.emit(eng, lambda e, o=out, i=in_: e.dma_start(out=o, in_=i),
                    reads=[in_], writes=[out], dma=True, tag=tag, deps=deps,
                    untracked_write=untracked)

def mm(C, out, lhsT, rhs, start, stop, tag="mm", deps=(), extra_reads=()):
    reads = [lhsT, rhs] + list(extra_reads)
    return C.P.emit("tensor",
                    lambda e, o=out, l=lhsT, r=rhs, st=start, sp=stop:
                        e.matmul(o, l, r, start=st, stop=sp),
                    reads=reads, writes=[out], tag=tag, deps=deps)

def act(C, out, in_, func, bias=0.0, scale=1.0, tag="act", deps=(), extra_reads=()):
    reads = [in_] + [x for x in (bias, scale) if _isbuf(x)] + list(extra_reads)
    return C.P.emit("scalar",
                    lambda e, o=out, i=in_, f=func, b=bias, s=scale:
                        e.activation(out=o, in_=i, func=f, bias=b, scale=s),
                    reads=reads, writes=[out], tag=tag, deps=deps)

def ve(C, name, *aps, eng="vector", tag="", deps=(), writes=1, **kw):
    outs = list(aps[:writes]); ins = list(aps[writes:])
    def build(e, n=name, o=outs, i=ins, kw=kw):
        return getattr(e, n)(*o, *i, **kw)
    return C.P.emit(eng, build, reads=[x for x in ins if _isbuf(x)],
                    writes=outs, tag=tag or name, deps=deps)

def tt(C, out, a, b, op, eng="vector", tag="tt", deps=()):
    return C.P.emit(eng, lambda e, o=out, x=a, y=b, p=op: e.tensor_tensor(out=o, in0=x, in1=y, op=p),
                    reads=[a, b], writes=[out], tag=tag, deps=deps)

def ts(C, out, in_, s1, op0, s2=None, op1=None, eng="vector", tag="ts", deps=(), extra_reads=()):
    def build(e, o=out, i=in_, s1=s1, s2=s2, op0=op0, op1=op1):
        if op1 is None:
            return e.tensor_scalar(out=o, in0=i, scalar1=s1, scalar2=None, op0=op0)
        return e.tensor_scalar(out=o, in0=i, scalar1=s1, scalar2=s2, op0=op0, op1=op1)
    reads = [in_] + [x for x in (s1, s2) if hasattr(x, "tensor")] + list(extra_reads)
    return C.P.emit(eng, build, reads=reads, writes=[out], tag=tag, deps=deps)

def cp(C, out, in_, eng="vector", tag="cp", deps=()):
    return C.P.emit(eng, lambda e, o=out, i=in_: e.tensor_copy(out=o, in_=i),
                    reads=[in_], writes=[out], tag=tag, deps=deps)

def memset(C, ap, v, eng="vector", tag="memset", deps=()):
    return C.P.emit(eng, lambda e, a=ap, vv=v: e.memset(a, vv),
                    reads=[], writes=[ap], tag=tag, deps=deps)


D = 2048
KT = 16
TOK = 1024
HALO = 256
TH = TOK + HALO
H0, HKV0, DH = 16, 4, 128
GRP = H0 // HKV0
WIN = 256


def declare_io(nc, upto):
    io = {}
    def inp(name, shape, dt=F32):
        io[name] = nc.dram_tensor(name, shape, dt, kind="ExternalInput").ap()
    inp("xinT", [D, TH])
    inp("cosT", [DH, TH]); inp("sinmT", [DH, TH])
    inp("bmask", [6, DH, 512])
    inp("temp", [128, 16])
    inp("hvalid", [DH, 2])
    inp("wqT", [D, D]); inp("wkvT", [D, 1024]); inp("woT", [D, D])
    io["xoutT"] = nc.dram_tensor("xoutT", [D, TOK], F32, kind="ExternalOutput").ap()
    if DEBUG:
        io["dbg_xnh"] = nc.dram_tensor("dbg_xnh", [D, TH], BF16, kind="ExternalOutput").ap()
        io["dbg_khat"] = nc.dram_tensor("dbg_khat", [DH, TH], BF16, kind="ExternalOutput").ap()
        io["dbg_vtok"] = nc.dram_tensor("dbg_vtok", [128, 512], BF16, kind="ExternalOutput").ap()
        io["dbg_ao"] = nc.dram_tensor("dbg_ao", [DH, TOK], BF16, kind="ExternalOutput").ap()
        io["dbg_q"] = nc.dram_tensor("dbg_q", [2 * DH, TOK], BF16, kind="ExternalOutput").ap()
        io["dbg_w"] = nc.dram_tensor("dbg_w", [DH, KT * 128], BF16, kind="ExternalOutput").ap()
        io["dbg_x2"] = nc.dram_tensor("dbg_x2", [2 * 128, TH], BF16, kind="ExternalOutput").ap()
        io["dbg_z"] = nc.dram_tensor("dbg_z", [2, 512], F32, kind="ExternalOutput").ap()
        io["dbg_p"] = nc.dram_tensor("dbg_p", [2 * DH, 512], BF16, kind="ExternalOutput").ap()
    return io


def build_consts(C, ident=False):
    nc = C.nc
    C.ones_bf_row = C.sb("ones_bf_row", [1, 128], BF16)
    C.ones_f_row = C.sb("ones_f_row", [1, 128], F32)
    C.ones_bf_col = C.sb("ones_bf_col", [128, 1], BF16)
    memset(C, C.ones_bf_row[:], 1.0)
    memset(C, C.ones_f_row[:], 1.0)
    memset(C, C.ones_bf_col[:], 1.0)
    if ident:
        C.ident_bf = C.sb("ident_bf", [128, 128], BF16)
        memset(C, C.ident_bf[:], 1.0)
        C.P.emit("gpsimd",
                 lambda e, o=C.ident_bf[:, :]: e.affine_select(
                     out=o, in_=o, pattern=[[-1, 128]], compare_op=ALU.is_equal,
                     fill=0.0, base=0, channel_multiplier=1),
                 reads=[C.ident_bf], writes=[C.ident_bf], tag="identsel")


def bcast_row(C, psum_ap, row_ap, n, bf=False, tag="bcast"):
    ones = C.ones_bf_row if bf else C.ones_f_row
    return mm(C, psum_ap, ones[:], row_ap, True, True, tag=tag)


def emit_ln(C, blocks, xnh, mrow0, rrow0, pb, tag="ln"):
    nc = C.nc
    tot = sum(w for (_, _, _, w) in blocks)
    C.push_scratch()
    xb = C.scratch(f"{tag}_xb", [128, 512], BF16)
    sq = C.scratch(f"{tag}_sq", [128, 512], BF16)
    mrow = mrow0[0:1, 0:tot]
    rrow = rrow0[0:1, 0:tot]
    mb = C.scratch(f"{tag}_mb", [128, 512], F32)
    rb = C.scratch(f"{tag}_rb", [128, 512], F32)

    chunks = []
    for (tiles, sc, dc, w) in blocks:
        off = 0
        while off < w:
            cw = min(512, w - off)
            chunks.append((tiles, sc + off, dc + off, cw))
            off += cw

    dst_pos = 0
    for (tiles, sc, dc, w) in chunks:
        psA = pb[0][:, 0:w]
        psB = pb[1][:, 0:w]
        for k in range(KT):
            xs = tiles[k][:, sc:sc + w]
            cp(C, xb[:, 0:w], xs, tag=f"{tag}_cast")
            tt(C, sq[:, 0:w], xb[:, 0:w], xb[:, 0:w], ALU.mult, tag=f"{tag}_sq")
            mm(C, psA[0:1, :], C.ones_bf_col[:], xb[:, 0:w], k == 0, k == KT - 1, tag=f"{tag}_sx")
            mm(C, psB[0:1, :], C.ones_bf_col[:], sq[:, 0:w], k == 0, k == KT - 1, tag=f"{tag}_sxx")
        m = mrow[0:1, dst_pos:dst_pos + w]
        r = rrow[0:1, dst_pos:dst_pos + w]
        ts(C, m, psA[0:1, :], 1.0 / D, ALU.mult, tag=f"{tag}_mean")
        ve(C, "scalar_tensor_tensor", r, psA[0:1, :], 1.0 / D, m, tag=f"{tag}_m2",
           op0=ALU.mult, op1=ALU.mult)
        ve(C, "scalar_tensor_tensor", r, psB[0:1, :], 1.0 / D, r, tag=f"{tag}_var",
           op0=ALU.mult, op1=ALU.subtract)
        ts(C, r, r, 1e-5, ALU.add, tag=f"{tag}_eps")
        act(C, r, r, AF.Sqrt, tag=f"{tag}_sqrt")
        ve(C, "reciprocal", r, r, tag=f"{tag}_recip")
        bcast_row(C, pb[2][:, 0:w], m, w, tag=f"{tag}_bm")
        bcast_row(C, pb[3][:, 0:w], r, w, tag=f"{tag}_br")
        cp(C, mb[:, 0:w], pb[2][:, 0:w], tag=f"{tag}_mbe")
        cp(C, rb[:, 0:w], pb[3][:, 0:w], tag=f"{tag}_rbe")
        for k in range(KT):
            xs = tiles[k][:, sc:sc + w]
            dst = xnh[k][:, dc:dc + w]
            tt(C, dst, xs, mb[:, 0:w], ALU.subtract, tag=f"{tag}_sub")
            tt(C, dst, dst, rb[:, 0:w], ALU.mult, tag=f"{tag}_mul")
        dst_pos += w
    C.pop_scratch()


def rope_norm(C, psums, out_bf, coss, sinm, ccols, rs_row, pbb, temp_col,
              pbsq, rp, sqb, tag="rn"):
    for ps, c0 in psums:
        w = ps.shape[1]
        cc = coss[:, ccols + c0: ccols + c0 + w]
        ss = sinm[:, ccols + c0: ccols + c0 + w]
        r = rp[:, 0:w]
        tt(C, r, ps, cc, ALU.mult, tag=f"{tag}_cos")
        ve(C, "scalar_tensor_tensor", sqb[0:64, 0:w], ps[64:128, :], 1.0,
           ss[0:64, :], op0=ALU.mult, op1=ALU.mult, tag=f"{tag}_rot1")
        ve(C, "scalar_tensor_tensor", sqb[64:128, 0:w], ps[0:64, :], 1.0,
           ss[64:128, :], op0=ALU.mult, op1=ALU.mult, tag=f"{tag}_rot2")
        tt(C, r, r, sqb[:, 0:w], ALU.add, tag=f"{tag}_add")
        ob = out_bf[:, c0:c0 + w]
        cp(C, ob, r, tag=f"{tag}_cast")
        tt(C, sqb[:, 0:w], ob, ob, ALU.mult, tag=f"{tag}_sq")
        mm(C, pbsq[0:1, 0:w], C.ones_bf_col[:], sqb[:, 0:w], True, True,
           tag=f"{tag}_ssq")
        rs = rs_row[0:1, 0:w]
        ts(C, rs, pbsq[0:1, 0:w], 1e-24, ALU.add, tag=f"{tag}_eps")
        act(C, rs, rs, AF.Sqrt, tag=f"{tag}_sqrt")
        ve(C, "reciprocal", rs, rs, tag=f"{tag}_recip")
        if temp_col is not None:
            tt(C, rs, rs, temp_col.broadcast_to([1, w]), ALU.mult, tag=f"{tag}_temp")
        bcast_row(C, pbb[:, 0:w], rs, w, tag=f"{tag}_bc")
        tt(C, ob, ob, pbb[:, 0:w], ALU.mult, tag=f"{tag}_nrm")


def load_xin(C, io):
    for k in range(KT):
        dma(C, "sync", C.xown[k][:, :], io["xinT"][k * 128:(k + 1) * 128, HALO:TH],
            tag=f"ldxo{k}")


def emit_l0_attn(C, io, pb):
    nc = C.nc
    C.push_scratch()
    obuf = [C.scratch(f"ao{k}", [128, TOK], BF16) for k in range(H0)]
    coss = C.scratch("coss", [DH, TH], BF16)
    sinm = C.scratch("sinm", [DH, TH], BF16)
    bmask = C.scratch("bmask", [DH, 6 * 512], BF16)
    khat = [C.scratch(f"khat{g}", [DH, TH], BF16) for g in range(HKV0)]
    vtok = [C.scratch(f"vtok{t}", [128, 512], BF16) for t in range(TH // 128)]
    tmprow = C.scratch("tmprow", [128, 16], F32)
    hval = C.scratch("hval", [DH, 2], F32)
    rnrp = C.scratch("rnrp", [128, 512], F32)
    rnsq = C.scratch("rnsq", [128, 512], BF16)
    rowpool = C.scratch("rowpool", [1, TH + 512], F32)
    rs_row = rowpool[0:1, 0:TH]
    zrow = rowpool[0:1, TH:TH + 512]
    rzb = C.scratch("rzb", [128, 512], F32)
    C.xhalo = [C.scratch(f"xhalo{k}", [128, HALO], BF16) for k in range(KT)]
    for k in range(KT):
        dma(C, "gpsimd", C.xhalo[k][:, :], io["xinT"][k * 128:(k + 1) * 128, 0:HALO],
            tag=f"ldxh{k}")

    dma(C, "gpsimd", coss[:, :], io["cosT"], tag="ldcos")
    dma(C, "gpsimd", sinm[:, :], io["sinmT"], tag="ldsin")
    dma(C, "gpsimd", bmask[:, :],
        io["bmask"].rearrange("m p n -> p m n"), tag="ldbm")
    dma(C, "sync", tmprow[:, :], io["temp"], tag="ldtemp")
    dma(C, "sync", hval[:, :], io["hvalid"], tag="ldhv")

    C.push_scratch()
    mpool0 = C.scratch("mpool0", [1, 2 * TH], F32)
    emit_ln(C, [(C.xhalo, 0, 0, HALO), (C.xown, 0, HALO, TOK)], C.xnh,
            mpool0[0:1, 0:TH], mpool0[0:1, TH:2 * TH], pb, tag="ln0")
    C.pop_scratch()
    C.P.fence()

    C.push_scratch()
    wb = [C.scratch(f"wb{i}", [128, KT * 128], BF16) for i in range(2)]
    def load_wchunk(j, src, c0, tg):
        return dma(C, "gpsimd",
                   wb[j].ap().rearrange("p (kt n) -> p kt n", kt=KT),
                   src.rearrange("(kt p) n -> p kt n", p=128)[:, :, c0:c0 + 128],
                   tag=tg)

    cblk = ((0, 256), (256, 512), (768, 512))
    kbanks = [(0, 1, 2), (3, 6, 7)]
    load_wchunk(0, io["wkvT"], 0, "ldwk0")
    for g in range(HKV0):
        if g + 1 < HKV0:
            load_wchunk((g + 1) % 2, io["wkvT"], (g + 1) * 128, f"ldwk{g+1}")
        pc = []
        for bi, (c0, w) in enumerate(cblk):
            ps = pb[kbanks[g % 2][bi]][:, 0:w]
            for k in range(KT):
                lhsT = wb[g % 2].ap().rearrange("p (kt n) -> p kt n", kt=KT)[:, k, :]
                mm(C, ps, lhsT, C.xnh[k][:, c0:c0 + w], k == 0, k == KT - 1,
                   tag=f"kp{g}")
            pc.append((ps, c0))
        rope_norm(C, pc, khat[g][:, :], coss, sinm, 0, rs_row, pb[4], None,
                  pb[5], rnrp, rnsq, tag=f"kn{g}")
    for half in range(2):
        for j in range(2):
            load_wchunk(j, io["wkvT"], 512 + (2 * half + j) * 128, f"ldwv{half}{j}")
        for tc in range(TH // 128):
            ps = pb[2 + (tc % 2)][:, 0:256]
            for j in range(2):
                for k in range(KT):
                    mm(C, ps[:, j * 128:j * 128 + 128],
                       C.xnh[k][:, tc * 128:(tc + 1) * 128],
                       wb[j].ap().rearrange("p (kt n) -> p kt n", kt=KT)[:, k, :],
                       k == 0, k == KT - 1, tag=f"vp{tc}")
            cp(C, vtok[tc][:, half * 256:half * 256 + 256], ps, tag=f"ve{tc}")
    C.pop_scratch()

    C.P.fence()
    C.push_scratch()
    qh = [C.scratch(f"qh{i}", [DH, TOK], BF16) for i in range(2)]
    prb = [C.scratch(f"prb{i}", [128, 512], BF16) for i in range(2)]
    wq = [C.scratch(f"wq{i}", [128, KT * 128], BF16) for i in range(2)]

    def load_wq(h):
        return dma(C, "gpsimd",
                   wq[h % 2].ap().rearrange("p (kt n) -> p kt n", kt=KT),
                   io["wqT"].rearrange("(kt p) n -> p kt n", p=128)[:, :, h * 128:(h + 1) * 128],
                   tag=f"ldwq{h}")

    def emit_qproj(h):
        pc = []
        for ci, (c0, w) in enumerate(((0, 512), (512, 512))):
            ps = pb[2 * (h % 2) + ci][:, 0:w]
            for k in range(KT):
                lhsT = wq[h % 2].ap().rearrange("p (kt n) -> p kt n", kt=KT)[:, k, :]
                mm(C, ps, lhsT, C.xnh[k][:, HALO + c0:HALO + c0 + w],
                   k == 0, k == KT - 1, tag=f"qp{h}")
            pc.append((ps, c0))
        return pc

    load_wq(0)
    pcs = {0: emit_qproj(0)}
    for h in range(H0):
        if h + 1 < H0:
            load_wq(h + 1)
        rope_norm(C, pcs.pop(h), qh[h % 2][:, :], coss, sinm, HALO, rs_row, pb[6],
                  tmprow[0:1, h:h + 1], pb[7], rnrp, rnsq, tag=f"qn{h}")
        if h + 1 < H0:
            pcs[h + 1] = emit_qproj(h + 1)
        g = h // GRP
        for qb in range(2):
            q_ap = qh[h % 2][:, qb * 512:(qb + 1) * 512]
            ops = pb[4][:, 0:512]
            zps = pb[5][0:1, 0:512]
            kcs = [4 * qb + o for o in range(-2, 4)]
            for i, kc in enumerate(kcs):
                kcol = kc * 128 + HALO
                first, last = i == 0, i == len(kcs) - 1
                sps = (pb[6] if i % 2 == 0 else pb[7])[:, 0:512]
                mm(C, sps, khat[g][:, kcol:kcol + 128], q_ap, True, True,
                   tag=f"sc{h}_{qb}")
                act(C, prb[i % 2][:, :], sps, AF.Exp, tag=f"exp{h}")
                mi = kc - 4 * qb + 2
                tt(C, prb[i % 2][:, :], prb[i % 2][:, :],
                   bmask[:, mi * 512:(mi + 1) * 512], ALU.mult, tag=f"msk{h}")
                if kc < 0:
                    ts(C, prb[i % 2][:, :], prb[i % 2][:, :], hval[:, kc + 2:kc + 3],
                       ALU.mult, tag=f"hv{h}")
                mm(C, zps, C.ones_bf_col[:], prb[i % 2][:, :], first, last,
                   tag=f"z{h}")
                mm(C, ops, vtok[kc + 2][:, g * 128:(g + 1) * 128],
                   prb[i % 2][:, :], first, last, tag=f"av{h}")
            ve(C, "reciprocal", zrow, zps, tag=f"rz{h}")
            bcast_row(C, pb[6][:, 0:512], zrow, 512, tag=f"bz{h}")
            cp(C, rzb[:, :], pb[6][:, 0:512], tag=f"rzb{h}")
            tt(C, obuf[h][:, qb * 512:(qb + 1) * 512], ops, rzb[:, :], ALU.mult,
               tag=f"oev{h}")
    C.pop_scratch()
    C.P.fence()

    C.push_scratch()
    wo = [C.scratch(f"wo{i}", [128, KT * 128], BF16) for i in range(2)]
    def load_wo(j):
        return dma(C, "gpsimd",
                   wo[j % 2].ap().rearrange("p (kt n) -> p kt n", kt=KT),
                   io["woT"].rearrange("(kt p) n -> p kt n", p=128)[:, :, j * 128:(j + 1) * 128],
                   tag=f"ldwo{j}")
    load_wo(0)
    for oc in range(KT):
        if oc + 1 < KT:
            load_wo(oc + 1)
        for c in range(2):
            ps = pb[(oc * 2 + c) % 4][:, 0:512]
            for i in range(H0):
                lhsT = wo[oc % 2].ap().rearrange("p (kt n) -> p kt n", kt=KT)[:, i, :]
                mm(C, ps, lhsT, obuf[i][:, c * 512:(c + 1) * 512],
                   i == 0, i == H0 - 1, tag=f"wo{oc}")
            tt(C, C.xown[oc][:, c * 512:(c + 1) * 512],
               C.xown[oc][:, c * 512:(c + 1) * 512], ps, ALU.add,
               tag=f"res{oc}")
    C.pop_scratch()
    C.pop_scratch()
    C.P.fence()



def build_l0attn_program():
    nc = bass.Bass("TRN2", target_bir_lowering=False, num_devices=8)
    io = declare_io(nc, "x1")
    prog = Prog()
    C = Ctx(nc, prog, scratch_bytes=95 * 1024)
    C.xown = [C.sb(f"xown{k}", [128, TOK], F32) for k in range(KT)]
    C.xnh = [C.sb(f"xnh{k}", [128, TH], BF16) for k in range(KT)]
    pb = [nc.alloc_psum_tensor(f"pb{i}", [128, 512], F32) for i in range(8)]
    pb = [t.ap() for t in pb]
    build_consts(C)
    C.finish_persistent()
    load_xin(C, io)
    emit_l0_attn(C, io, pb)
    fins = []
    for k in range(KT):
        fins.append(dma(C, "sync", io["xoutT"][k * 128:(k + 1) * 128, :],
                        C.xown[k][:, :], tag=f"sto{k}"))
    prog.emit("sync", lambda e: e.nop(), deps=fins, tag="fin")
    prog.run(nc)
    return nc, io



def make_rope_tables(positions):
    dh = DH
    inv = 1.0 / (10000.0 ** (np.arange(0, dh, 2, dtype=np.float64) / dh))
    ang = positions.astype(np.float64)[None, :] * inv[:, None]
    cos = np.concatenate([np.cos(ang), np.cos(ang)], axis=0)
    sin = np.concatenate([np.sin(ang), np.sin(ang)], axis=0)
    sinm = sin.copy()
    sinm[0:64] *= -1.0
    return cos.astype(np.float32), sinm.astype(np.float32)


def make_band_masks():
    m = np.zeros((6, 128, 512), np.float32)
    ki = np.arange(128)[:, None]
    qi = np.arange(512)[None, :]
    for o in range(-2, 4):
        d = qi - ki - 128 * o
        ok = (d >= 0) & (d < WIN)
        m[o + 2] = np.where(ok, 1.0, 0.0)
    return m


def stage_l0attn(inp):
    x = inp["L0_attn_x"]
    wq = inp["L0_attn_Wq"]; wkv = inp["L0_attn_Wkv"]; wo = inp["L0_attn_Wo"]
    temp = np.broadcast_to(inp["L0_attn_qk_temp"].reshape(1, 16), (128, 16)).copy()
    wqT = np.ascontiguousarray(wq.T)
    wkvT = np.ascontiguousarray(wkv.T)
    woT = np.ascontiguousarray(wo.T)
    bm = make_band_masks()
    maps = []
    for c in range(8):
        b, hf = c // 2, c % 2
        lo = hf * TOK - HALO
        xs = np.zeros((TH, D), np.float32)
        s0 = max(lo, 0)
        xs[s0 - lo:, :] = x[b, s0:hf * TOK + TOK, :]
        cos, sinm = make_rope_tables(np.arange(lo, lo + TH))
        hv = np.full((128, 2), 0.0 if hf == 0 else 1.0, np.float32)
        maps.append({
            "hvalid": hv,
            "xinT": np.ascontiguousarray(xs.T),
            "cosT": cos, "sinmT": sinm,
            "bmask": bm, "temp": temp.astype(np.float32),
            "wqT": wqT, "wkvT": wkvT, "woT": woT,
        })
    return maps



E0 = 8
MH = 1408
MHS = 2816


def emit_moe_gather(C, io):
    nc = C.nc
    groups = [[0, 1, 2, 3, 4, 5, 6, 7]]
    from concourse.replica_groups import maybe_share_collective_output_space
    space = maybe_share_collective_output_space("AllGather", groups, prefer_local=True)
    out = {}
    specs = [
        ("wg", [2048, MH], [8 * 2048, MH]),
        ("wu", [2048, MH], [8 * 2048, MH]),
        ("wd", [1408, 2048], [8 * 1408, 2048]),
        ("swg", [256, MHS], [2048, MHS]),
        ("swu", [256, MHS], [2048, MHS]),
        ("swd", [352, 2048], [2816, 2048]),
    ]
    for name, sshape, fshape in specs:
        cin = nc.dram_tensor(f"cc_{name}_in", sshape, BF16, kind="Internal").ap()
        cout = nc.dram_tensor(f"cc_{name}_out", fshape, BF16, kind="Internal",
                              addr_space=space).ap()
        d = dma(C, "gpsimd", cin, io[f"sh_{name}"], tag=f"cp_{name}")
        ccdeps = [d] + ([C.cc_last] if C.cc_last is not None else [])
        C.cc_last = C.P.emit("gpsimd",
                 lambda e, ci=cin, co=cout, g=groups:
                     e.collective_compute("AllGather", ALU.bypass,
                                          replica_groups=g,
                                          ins=[ci.opt()], outs=[co.opt()]),
                 reads=[cin], writes=[cout], cc=True, tag=f"ag_{name}",
                 deps=ccdeps)
        out[name] = cout
    return out


def emit_l0_moe(C, io, pb, gath, bg=None):
    nc = C.nc
    def BG():
        if bg is not None:
            bg()
    C.push_scratch()
    hbuf = [C.scratch(f"hb{i}", [128, TOK], BF16) for i in range(22)]
    cbb2 = [C.scratch(f"cbb{e}", [128, TOK], BF16) for e in range(2)]
    sgb = C.scratch("sgb", [128, TOK], BF16)
    gwT = C.scratch("gwT", [128, KT * 8], BF16)
    shgT = C.scratch("shgT", [128, KT * 1], BF16)
    iota8 = C.scratch("iota8", [128, 8], F32)
    cbT = C.scratch("cbT", [8, TOK], BF16)
    gsc = [C.scratch(f"gsc{i}", [128, 8], F32) for i in range(6)]
    gnm = C.scratch("gnm", [128, 1], F32)
    cb16 = C.scratch("cb16", [128, 8], BF16)

    C.push_scratch()
    mpool = C.scratch("mpool", [1, 2 * TOK], F32)
    emit_ln(C, [(C.xown, 0, 0, TOK)], C.xnh, mpool[0:1, 0:TOK],
            mpool[0:1, TOK:2 * TOK], pb, tag="ln1")
    C.pop_scratch()
    C.P.fence()
    rowg = C.scratch("rowg", [1, TOK], F32)
    wba = [C.scratch(f"mwa{i}", [128, KT * 128], BF16) for i in range(2)]
    wbb = [C.scratch(f"mwb{i}", [128, KT * 128], BF16) for i in range(2)]
    wbd = [C.scratch("mwd0", [128, 22 * 128], BF16)]

    xn = C.xnh

    dma(C, "gpsimd", gwT.ap().rearrange("p (kt n) -> p kt n", kt=KT),
        io["gate_wT"].rearrange("(kt p) n -> p kt n", p=128), tag="ldgw")
    dma(C, "gpsimd", shgT.ap()[:, 0:KT], io["shg"], tag="ldshg")
    C.P.emit("gpsimd", lambda e, o=iota8[:, :]: e.iota(o, pattern=[[1, 8]],
                                                       channel_multiplier=0,
                                                       allow_small_or_imprecise_dtypes=True),
             writes=[iota8], tag="iota8")

    for tc in range(8):
        lps = pb[0][:, 0:8]
        for k in range(KT):
            mm(C, lps, xn[k][:, tc * 128:(tc + 1) * 128],
               gwT.ap().rearrange("p (kt n) -> p kt n", kt=KT)[:, k, :],
               k == 0, k == KT - 1, tag=f"glog{tc}")
        lg, eL, t1, msk, e2, ew = gsc
        cp(C, lg[:, :], lps, tag=f"gcp{tc}")
        ve(C, "reduce_max", gnm[:, 0:1], lg[:, :], axis=AX.X, negate=True,
           tag=f"gm1{tc}")
        act(C, eL[:, :], lg[:, :], AF.Exp, bias=gnm[:, 0:1], tag=f"gexp{tc}")
        ts(C, msk[:, :], eL[:, :], 1.0, ALU.is_ge, tag=f"gc1{tc}")
        ve(C, "scalar_tensor_tensor", t1[:, :], msk[:, :], -1000.0, iota8[:, :],
           op0=ALU.mult, op1=ALU.add, tag=f"gt1{tc}")
        ve(C, "tensor_reduce", t1[:, 1:2], t1[:, :], axis=AX.X, op=ALU.min,
           tag=f"gi1{tc}")
        ts(C, t1[:, 1:2], t1[:, 1:2], 1000.0, ALU.add, tag=f"gi1b{tc}")
        ts(C, msk[:, :], iota8[:, :], t1[:, 1:2], ALU.is_equal, tag=f"gm1k{tc}")
        tt(C, e2[:, :], eL[:, :], msk[:, :], ALU.mult, tag=f"ge2a{tc}")
        tt(C, e2[:, :], eL[:, :], e2[:, :], ALU.subtract, tag=f"ge2{tc}")
        ve(C, "reduce_max", t1[:, 2:3], e2[:, :], axis=AX.X, tag=f"gm2{tc}")
        ts(C, ew[:, :], e2[:, :], t1[:, 2:3], ALU.is_ge, tag=f"gc2{tc}")
        ve(C, "scalar_tensor_tensor", ew[:, :],
           ew[:, :], -1000.0, iota8[:, :], op0=ALU.mult, op1=ALU.add, tag=f"gt2{tc}")
        ve(C, "tensor_reduce", t1[:, 3:4], ew[:, :], axis=AX.X, op=ALU.min,
           tag=f"gi2{tc}")
        ts(C, t1[:, 3:4], t1[:, 3:4], 1000.0, ALU.add, tag=f"gi2b{tc}")
        ts(C, ew[:, :], iota8[:, :], t1[:, 3:4], ALU.is_equal, tag=f"gm2k{tc}")
        tt(C, msk[:, :], msk[:, :], ew[:, :], ALU.add, tag=f"gms{tc}")
        tt(C, ew[:, :], msk[:, :], eL[:, :], ALU.mult, tag=f"gew{tc}")
        ve(C, "reduce_sum", t1[:, 4:5], ew[:, :], axis=AX.X, tag=f"gs12{tc}")
        ve(C, "reciprocal", t1[:, 5:6], t1[:, 4:5], tag=f"gr12{tc}")
        ts(C, e2[:, :], ew[:, :], t1[:, 5:6], ALU.mult, tag=f"gcmb{tc}")
        if DEBUG and tc == 1:
            dma(C, "sync", io["dbg_g"][:, 0:8], eL[:, :], tag="dbgeL")
            dma(C, "sync", io["dbg_g"][:, 8:16], msk[:, :], tag="dbgmsk")
            dma(C, "sync", io["dbg_g"][:, 16:24], ew[:, :], tag="dbgew")
            dma(C, "sync", io["dbg_g"][:, 24:32], t1[:, :], tag="dbgt1")
            dma(C, "sync", io["dbg_g"][:, 32:40], e2[:, :], tag="dbge2")
            dma(C, "sync", io["dbg_g"][:, 40:48], lg[:, :], tag="dbglg")
        cp(C, cb16[:, :], e2[:, :], tag=f"gcb{tc}")
        cbps = pb[7].bitcast(BF16)
        C.P.emit("tensor",
                 lambda e, o=cbps[0:8, 0:128], i=cb16[:, :],
                        idn=C.ident_bf[:, :]: e.transpose(o, i, idn),
                 reads=[cb16, C.ident_bf], writes=[pb[7]], tag=f"gtr{tc}")
        cp(C, cbT[0:8, tc * 128:(tc + 1) * 128], cbps[0:8, 0:128],
           tag=f"gcbT{tc}")
    sel8 = C.scratch("sel8", [8, 8 * 128], BF16)
    dma(C, "gpsimd", sel8[:, :], io["sel8"], tag="ldsel8")

    for c in range(2):
        sps = pb[3][0:1, 0:512]
        for k in range(KT):
            mm(C, sps, shgT.ap()[:, k:k + 1],
               xn[k][:, c * 512:(c + 1) * 512], k == 0, k == KT - 1,
               tag=f"sg{c}")
        act(C, rowg[0:1, c * 512:(c + 1) * 512], sps, AF.Sigmoid, tag=f"sgs{c}")
        bcast_row(C, pb[3][:, 0:512], rowg[0:1, c * 512:(c + 1) * 512], 512,
                  tag=f"sgb{c}")
        cp(C, sgb[:, c * 512:(c + 1) * 512], pb[3][:, 0:512], tag=f"sgbe{c}")

    _wq = [0]
    def stream_w(buf, src_rows, src_c0, w, tag, nkt=KT):
        _wq[0] ^= 1
        return dma(C, "sync" if _wq[0] else "scalar",
                   buf.ap()[:, 0:nkt * w].rearrange("p (kt n) -> p kt n", kt=nkt),
                   src_rows.rearrange("(kt p) n -> p kt n", p=128)[:, :, src_c0:src_c0 + w],
                   tag=tag)

    if DEBUG:
        dma(C, "sync", io["dbg_cb"], cbT[:, :], tag="dbgcb")
        dma(C, "sync", io["dbg_sg"], rowg[0:1, :], tag="dbgsg")
    nsh = MHS // 128
    for hc in range(nsh):
        stream_w(wba[hc % 2], gath["swg"], hc * 128, 128, f"lswg{hc}")
        stream_w(wbb[hc % 2], gath["swu"], hc * 128, 128, f"lswu{hc}")
        for c in range(2):
            gps = pb[0 + (hc % 2)][:, 0:512]
            ups = pb[2 + (hc % 2)][:, 0:512]
            for k in range(KT):
                mm(C, gps, wba[hc % 2].ap().rearrange("p (kt n) -> p kt n", kt=KT)[:, k, :],
                   xn[k][:, c * 512:(c + 1) * 512], k == 0, k == KT - 1, tag=f"shg{hc}")
            for k in range(KT):
                mm(C, ups, wbb[hc % 2].ap().rearrange("p (kt n) -> p kt n", kt=KT)[:, k, :],
                   xn[k][:, c * 512:(c + 1) * 512], k == 0, k == KT - 1, tag=f"shu{hc}")
            sl = hbuf[hc][:, c * 512:(c + 1) * 512]
            act(C, sl, gps, AF.Silu, tag=f"shsl{hc}")
            tt(C, sl, sl, ups, ALU.mult, tag=f"shmu{hc}")
            tt(C, sl, sl, sgb[:, c * 512:(c + 1) * 512], ALU.mult, tag=f"shsg{hc}")
        BG()
    for oc in range(KT):
        stream_w(wbd[0], gath["swd"], oc * 128, 128, f"lswd{oc}", nkt=nsh)
        for c in range(2):
            ps = pb[4 + (oc * 2 + c) % 2][:, 0:512]
            for hc in range(nsh):
                mm(C, ps, wbd[0].ap()[:, 0:nsh * 128].rearrange("p (kt n) -> p kt n", kt=nsh)[:, hc, :],
                   hbuf[hc][:, c * 512:(c + 1) * 512], hc == 0, hc == nsh - 1,
                   tag=f"shd{oc}")
            tt(C, C.xown[oc][:, c * 512:(c + 1) * 512],
               C.xown[oc][:, c * 512:(c + 1) * 512], ps, ALU.add, tag=f"shres{oc}")
        BG()

    if DEBUG:
        dma(C, "sync", io["dbg_h0"], hbuf[0][:, :], tag="dbgh0")
    nh = MH // 128
    wbd2 = [wbd[0], C.scratch("mwd1", [128, 22 * 128], BF16)]

    def emit_gu(e):
        hb = hbuf[(e % 2) * nh:(e % 2) * nh + nh]
        for c in range(2):
            mm(C, pb[6][:, 0:512], sel8[:, e * 128:(e + 1) * 128],
               cbT[0:8, c * 512:(c + 1) * 512], True, True, tag=f"cbbm{e}")
            cp(C, cbb2[e % 2][:, c * 512:(c + 1) * 512], pb[6][:, 0:512],
               tag=f"cbbe{e}")
        for hc in range(nh):
            stream_w(wba[hc % 2], gath["wg"][e * 2048:(e + 1) * 2048, :], hc * 128,
                     128, f"lwg{e}_{hc}")
            stream_w(wbb[hc % 2], gath["wu"][e * 2048:(e + 1) * 2048, :], hc * 128,
                     128, f"lwu{e}_{hc}")
            for c in range(2):
                gps = pb[0 + (hc % 2)][:, 0:512]
                ups = pb[2 + (hc % 2)][:, 0:512]
                for k in range(KT):
                    mm(C, gps, wba[hc % 2].ap().rearrange("p (kt n) -> p kt n", kt=KT)[:, k, :],
                       xn[k][:, c * 512:(c + 1) * 512], k == 0, k == KT - 1, tag=f"eg{e}")
                for k in range(KT):
                    mm(C, ups, wbb[hc % 2].ap().rearrange("p (kt n) -> p kt n", kt=KT)[:, k, :],
                       xn[k][:, c * 512:(c + 1) * 512], k == 0, k == KT - 1, tag=f"eu{e}")
                sl = hb[hc][:, c * 512:(c + 1) * 512]
                act(C, sl, gps, AF.Silu, tag=f"esl{e}")
                tt(C, sl, sl, ups, ALU.mult, tag=f"emu{e}")
                tt(C, sl, sl, cbb2[e % 2][:, c * 512:(c + 1) * 512], ALU.mult,
                   tag=f"ecb{e}")
            BG()

    def emit_down(e):
        hb = hbuf[(e % 2) * nh:(e % 2) * nh + nh]
        for oc in range(KT):
            stream_w(wbd2[oc % 2], gath["wd"][e * 1408:(e + 1) * 1408, :], oc * 128,
                     128, f"lwd{e}_{oc}", nkt=nh)
            for c in range(2):
                ps = pb[4 + (oc * 2 + c) % 2][:, 0:512]
                for hc in range(nh):
                    mm(C, ps, wbd2[oc % 2].ap()[:, 0:nh * 128].rearrange("p (kt n) -> p kt n", kt=nh)[:, hc, :],
                       hb[hc][:, c * 512:(c + 1) * 512], hc == 0, hc == nh - 1,
                       tag=f"ed{e}")
                tt(C, C.xown[oc][:, c * 512:(c + 1) * 512],
                   C.xown[oc][:, c * 512:(c + 1) * 512], ps, ALU.add,
                   tag=f"eres{e}_{oc}")
            BG()

    emit_gu(0)
    for e in range(E0):
        if e + 1 < E0:
            emit_gu(e + 1)
        emit_down(e)
    C.pop_scratch()
    C.P.fence()


def build_moe_program():
    nc = bass.Bass("TRN2", target_bir_lowering=False, num_devices=8)
    io = {}
    def inp(name, shape, dt=F32):
        io[name] = nc.dram_tensor(name, shape, dt, kind="ExternalInput").ap()
    inp("xres", [D, TOK])
    inp("gate_wT", [D, 8])
    inp("sel8", [8, 8 * 128])
    inp("shg", [128, KT])
    inp("sh_wg", [2048, MH]); inp("sh_wu", [2048, MH]); inp("sh_wd", [1408, 2048])
    inp("sh_swg", [256, MHS]); inp("sh_swu", [256, MHS]); inp("sh_swd", [352, 2048])
    io["xoutT"] = nc.dram_tensor("xoutT", [D, TOK], F32, kind="ExternalOutput").ap()
    if DEBUG:
        io["dbg_cb"] = nc.dram_tensor("dbg_cb", [8, TOK], BF16, kind="ExternalOutput").ap()
        io["dbg_sg"] = nc.dram_tensor("dbg_sg", [1, TOK], F32, kind="ExternalOutput").ap()
        io["dbg_h0"] = nc.dram_tensor("dbg_h0", [128, TOK], BF16, kind="ExternalOutput").ap()
        io["dbg_g"] = nc.dram_tensor("dbg_g", [128, 48], F32, kind="ExternalOutput").ap()
    prog = Prog()
    C = Ctx(nc, prog, scratch_bytes=95 * 1024)
    C.xown = [C.sb(f"xown{k}", [128, TOK], F32) for k in range(KT)]
    C.xnh = [C.sb(f"xnh{k}", [128, TH], BF16) for k in range(KT)]
    C.xhalo = None
    pb = [nc.alloc_psum_tensor(f"pb{i}", [128, 512], F32) for i in range(8)]
    pb = [t.ap() for t in pb]
    build_consts(C, ident=True)
    C.finish_persistent()
    for k in range(KT):
        dma(C, "sync", C.xown[k][:, :], io["xres"][k * 128:(k + 1) * 128, :],
            tag=f"ldx{k}")
    gath = emit_moe_gather(C, io)
    emit_l0_moe(C, io, pb, gath)
    fins = []
    for k in range(KT):
        fins.append(dma(C, "sync", io["xoutT"][k * 128:(k + 1) * 128, :],
                        C.xown[k][:, :], tag=f"sto{k}"))
    prog.emit("sync", lambda e: e.nop(), deps=fins, tag="fin")
    prog.run(nc)
    return nc, io


def stage_moe(inp, x1):
    g = inp["L0_ffn_gate_w"]
    shg = np.ascontiguousarray(inp["L0_ffn_shgate_w"].reshape(KT, 128).T)
    wg = np.ascontiguousarray(inp["L0_ffn_w_gate"].reshape(8 * 2048, MH))
    wu = np.ascontiguousarray(inp["L0_ffn_w_up"].reshape(8 * 2048, MH))
    wd = np.ascontiguousarray(inp["L0_ffn_w_down"].reshape(8 * 1408, 2048))
    swg = inp["L0_ffn_shw_gate"]; swu = inp["L0_ffn_shw_up"]; swd = inp["L0_ffn_shw_down"]
    gwT = np.ascontiguousarray(g.T)
    maps = []
    for c in range(8):
        b, hf = c // 2, c % 2
        xs = x1[b, hf * TOK:(hf + 1) * TOK, :]
        sel8 = np.zeros((8, 8 * 128), np.float32)
        for e in range(8):
            sel8[e, e * 128:(e + 1) * 128] = 1.0
        maps.append({
            "xres": np.ascontiguousarray(xs.T), "sel8": sel8,
            "gate_wT": gwT, "shg": shg.astype(np.float32),
            "sh_wg": np.ascontiguousarray(wg[c * 2048:(c + 1) * 2048]),
            "sh_wu": np.ascontiguousarray(wu[c * 2048:(c + 1) * 2048]),
            "sh_wd": np.ascontiguousarray(wd[c * 1408:(c + 1) * 1408]),
            "sh_swg": np.ascontiguousarray(swg[c * 256:(c + 1) * 256]),
            "sh_swu": np.ascontiguousarray(swu[c * 256:(c + 1) * 256]),
            "sh_swd": np.ascontiguousarray(swd[c * 352:(c + 1) * 352]),
        })
    return maps



def rope_apply(C, ps, out_bf, coss, sinm, c0, w, rp, sqb, tag):
    cc = coss[:, c0:c0 + w]
    ss = sinm[:, c0:c0 + w]
    r = rp[:, 0:w]
    tt(C, r, ps, cc, ALU.mult, tag=f"{tag}_cos")
    ve(C, "scalar_tensor_tensor", sqb[0:64, 0:w], ps[64:128, :], 1.0,
       ss[0:64, :], op0=ALU.mult, op1=ALU.mult, tag=f"{tag}_rot1")
    ve(C, "scalar_tensor_tensor", sqb[64:128, 0:w], ps[0:64, :], 1.0,
       ss[64:128, :], op0=ALU.mult, op1=ALU.mult, tag=f"{tag}_rot2")
    tt(C, r, r, sqb[:, 0:w], ALU.add, tag=f"{tag}_add")
    cp(C, out_bf, r, tag=f"{tag}_cast")


def emit_l1_attn(C, io, pb):
    nc = C.nc
    INV_SQ_DH = 1.0 / float(np.sqrt(128.0))
    C.push_scratch()
    coss = C.scratch("coss1", [DH, TOK], BF16)
    sinm = C.scratch("sinm1", [DH, TOK], BF16)
    khat = [C.scratch(f"k1_{g}", [DH, 2 * TOK], BF16) for g in range(HKV0)]
    vtok = [C.scratch(f"v1_{t}", [128, 512], BF16) for t in range(2 * TOK // 128)]
    cmask = C.scratch("cmask1", [128, KT * 512], BF16)
    rnrp = C.scratch("rnrp1", [128, 512], F32)
    rnsq = C.scratch("rnsq1", [128, 512], BF16)
    rowq = C.scratch("rowq1", [1, 512], F32)
    zrow = rowq[0:1, 0:512]
    rzb = None
    prb = [C.scratch(f"prb1_{i}", [128, 512], BF16) for i in range(2)]
    wq = [C.scratch(f"wq1_{i}", [128, KT * 128], BF16) for i in range(2)]

    dma(C, "gpsimd", coss[:, :], io["cosT"][:, HALO:TH], tag="ldcos1")
    dma(C, "gpsimd", sinm[:, :], io["sinmT"][:, HALO:TH], tag="ldsin1")

    C.push_scratch()
    mpool = C.scratch("mpool1", [1, 2 * TOK], F32)
    emit_ln(C, [(C.xown, 0, 0, TOK)], C.xnh, mpool[0:1, 0:TOK],
            mpool[0:1, TOK:2 * TOK], pb, tag="ln2")
    C.pop_scratch()
    C.P.fence()
    xn = C.xnh

    C.push_scratch()
    kown = [C.scratch(f"ko{g}", [128, TOK], BF16) for g in range(HKV0)]
    def load_w1(buf, name, c0, tg):
        return dma(C, "gpsimd",
                   buf.ap().rearrange("p (kt n) -> p kt n", kt=KT),
                   io[name].rearrange("(kt p) n -> p kt n", p=128)[:, :, c0:c0 + 128],
                   tag=tg)
    kb = [(0, 1), (2, 3)]
    load_w1(wq[0], "wk1", 0, "ldk1_0")
    for g in range(HKV0):
        if g + 1 < HKV0:
            load_w1(wq[(g + 1) % 2], "wk1", (g + 1) * 128, f"ldk1_{g+1}")
        for c in range(2):
            ps = pb[kb[g % 2][c]][:, 0:512]
            for k in range(KT):
                mm(C, ps, wq[g % 2].ap().rearrange("p (kt n) -> p kt n", kt=KT)[:, k, :],
                   xn[k][:, c * 512:(c + 1) * 512], k == 0, k == KT - 1, tag=f"k1p{g}")
            rope_apply(C, ps, kown[g][:, c * 512:(c + 1) * 512], coss, sinm,
                       c * 512, 512, rnrp, rnsq, tag=f"k1r{g}")
    for half in range(2):
        for j in range(2):
            load_w1(wq[j], "wv1", (2 * half + j) * 128, f"ldv1_{half}{j}")
        for tc in range(TOK // 128):
            ps = pb[4 + (tc % 2)][:, 0:256]
            for j in range(2):
                for k in range(KT):
                    mm(C, ps[:, j * 128:j * 128 + 128],
                       xn[k][:, tc * 128:(tc + 1) * 128],
                       wq[j].ap().rearrange("p (kt n) -> p kt n", kt=KT)[:, k, :],
                       k == 0, k == KT - 1, tag=f"v1p{tc}")
            cp(C, vtok[tc][:, half * 256:half * 256 + 256], ps, tag=f"v1e{tc}")

    groups = [[0, 1], [2, 3], [4, 5], [6, 7]]
    cc_kown = nc.dram_tensor("cc_kown", [512, TOK], BF16, kind="Internal").ap()
    cc_kfull = nc.dram_tensor("cc_kfull", [1024, TOK], BF16, kind="Internal").ap()
    cc_vown = nc.dram_tensor("cc_vown", [TOK, 512], BF16, kind="Internal").ap()
    cc_vfull = nc.dram_tensor("cc_vfull", [2 * TOK, 512], BF16, kind="Internal").ap()
    for g in range(HKV0):
        dma(C, "sync", cc_kown[g * 128:(g + 1) * 128, :], kown[g][:, :], tag=f"kx{g}")
    for tc in range(TOK // 128):
        dma(C, "sync", cc_vown[tc * 128:(tc + 1) * 128, :], vtok[tc][:, :], tag=f"vx{tc}")
    C.pop_scratch()
    aoh = [C.scratch(f"aoh{i}", [128, 512], BF16) for i in range(H0)]
    C.cc_last = C.P.emit("gpsimd",
             lambda e, ci=cc_kown, co=cc_kfull: e.collective_compute(
                 "AllGather", ALU.bypass, replica_groups=groups,
                 ins=[ci.opt()], outs=[co.opt()]),
             reads=[cc_kown], writes=[cc_kfull], cc=True, tag="agk",
             deps=[C.cc_last] if C.cc_last is not None else [])
    C.cc_last = C.P.emit("gpsimd",
             lambda e, ci=cc_vown, co=cc_vfull: e.collective_compute(
                 "AllGather", ALU.bypass, replica_groups=groups,
                 ins=[ci.opt()], outs=[co.opt()]),
             reads=[cc_vown], writes=[cc_vfull], cc=True, tag="agv",
             deps=[C.cc_last])
    C.cc_last = C.P.emit("gpsimd",
             lambda e, ci=cc_kown, co=cc_kfull: e.collective_compute(
                 "AllGather", ALU.bypass, replica_groups=groups,
                 ins=[ci.opt()], outs=[co.opt()]),
             reads=[cc_kown], writes=[cc_kfull], cc=True, tag="agk2",
             deps=[C.cc_last])
    C.cc_last = C.P.emit("gpsimd",
             lambda e, ci=cc_vown, co=cc_vfull: e.collective_compute(
                 "AllGather", ALU.bypass, replica_groups=groups,
                 ins=[ci.opt()], outs=[co.opt()]),
             reads=[cc_vown], writes=[cc_vfull], cc=True, tag="agv2",
             deps=[C.cc_last])
    for g in range(HKV0):
        for r in range(2):
            dma(C, "sync", khat[g][:, r * TOK:(r + 1) * TOK],
                cc_kfull[r * 512 + g * 128: r * 512 + (g + 1) * 128, :], tag=f"kl{g}",
                deps=[C.cc_last])
    for tc in range(2 * TOK // 128):
        dma(C, "sync", vtok[tc][:, :], cc_vfull[tc * 128:(tc + 1) * 128, :],
            tag=f"vl{tc}", deps=[C.cc_last])

    qhat = [C.scratch(f"qh1_{h}", [128, 512], BF16) for h in range(8)]
    seq = [(qb, h) for qb in range(2) for h in range(H0)]

    def emit_q1(idx):
        qb, h = seq[idx]
        ps = pb[h % 2][:, 0:512]
        for k in range(KT):
            mm(C, ps, wq[h % 2].ap().rearrange("p (kt n) -> p kt n", kt=KT)[:, k, :],
               xn[k][:, qb * 512:(qb + 1) * 512], k == 0, k == KT - 1,
               tag=f"q1p{h}")
        rope_apply(C, ps, qhat[idx % 8][:, :], coss, sinm, qb * 512, 512,
                   rnrp, rnsq, tag=f"q1r{qb}_{h}")

    load_w1(wq[0], "wq1", 0, "ldq1_0")
    for idx0 in range(8):
        nh = seq[idx0 + 1][1] if idx0 + 1 < len(seq) else None
        if nh is not None:
            load_w1(wq[nh % 2], "wq1", nh * 128, f"ldq1a_{idx0}")
        emit_q1(idx0)

    for qb in range(2):
        dma(C, "gpsimd", cmask[:, 0:8 * 512],
            io["cmask"].rearrange("m p n -> p m n")[:, qb * KT + 8:qb * KT + 16, :],
            tag=f"ldcmA{qb}")
        dma(C, "gpsimd", cmask[:, 8 * 512:KT * 512],
            io["cmask"].rearrange("m p n -> p m n")[:, qb * KT:qb * KT + 8, :],
            tag=f"ldcmB{qb}")
        for h in range(H0):
            idx = qb * H0 + h
            g = h // GRP
            q1 = qhat[idx % 8]
            zps = pb[3][0:1, 0:512]
            ops = pb[2][:, 0:512]
            for i, kc in enumerate(range(KT - 1, -1, -1)):
                first, last = i == 0, i == KT - 1
                sps = (pb[6] if i % 2 == 0 else pb[7])[:, 0:512]
                mm(C, sps, khat[g][:, kc * 128:(kc + 1) * 128], q1[:, 0:512],
                   True, True, tag=f"s1{h}_{qb}")
                act(C, prb[i % 2][:, :], sps, AF.Exp, scale=INV_SQ_DH, tag=f"s1e{h}")
                mslot = (kc - 8) if kc >= 8 else (8 + kc)
                tt(C, prb[i % 2][:, :], prb[i % 2][:, :],
                   cmask[:, mslot * 512:(mslot + 1) * 512], ALU.mult, tag=f"s1m{h}")
                mm(C, zps, C.ones_bf_col[:], prb[i % 2][:, :], first, last,
                   tag=f"z1{h}")
                mm(C, ops, vtok[kc][:, g * 128:(g + 1) * 128], prb[i % 2][:, :],
                   first, last, tag=f"av1{h}")
            ve(C, "reciprocal", zrow, zps, tag=f"rz1{h}")
            bcast_row(C, pb[4][:, 0:512], zrow, 512, tag=f"bz1{h}")
            cp(C, rnrp[:, 0:512], pb[4][:, 0:512], tag=f"rzb1{h}")
            tt(C, aoh[h][:, :], ops, rnrp[:, 0:512], ALU.mult, tag=f"oe1{h}")
            if idx + 8 < len(seq):
                nqb, nh = seq[idx + 8]
                load_w1(wq[nh % 2], "wq1", nh * 128, f"ldq1c_{idx}")
                emit_q1(idx + 8)
        for oc in range(KT):
            load_w1(wq[oc % 2], "wo1", oc * 128, f"ldo1_{qb}_{oc}")
            ps = pb[4 + (oc % 2)][:, 0:512]
            for i in range(H0):
                mm(C, ps, wq[oc % 2].ap().rearrange("p (kt n) -> p kt n", kt=KT)[:, i, :],
                   aoh[i][:, :], i == 0, i == H0 - 1, tag=f"wo1{oc}")
            tt(C, C.xown[oc][:, qb * 512:(qb + 1) * 512],
               C.xown[oc][:, qb * 512:(qb + 1) * 512], ps, ALU.add, tag=f"r1{oc}")
    C.pop_scratch()
    C.P.fence()


def build_l1attn_program():
    nc = bass.Bass("TRN2", target_bir_lowering=False, num_devices=8)
    io = {}
    def inp(name, shape, dt=F32):
        io[name] = nc.dram_tensor(name, shape, dt, kind="ExternalInput").ap()
    inp("xres", [D, TOK])
    inp("cosT", [DH, TH]); inp("sinmT", [DH, TH])
    inp("cmask", [32, DH, 512])
    inp("wq1", [D, D]); inp("wk1", [D, 512]); inp("wv1", [D, 512]); inp("wo1", [D, D])
    io["xoutT"] = nc.dram_tensor("xoutT", [D, TOK], F32, kind="ExternalOutput").ap()
    prog = Prog()
    C = Ctx(nc, prog, scratch_bytes=95 * 1024)
    C.xown = [C.sb(f"xown{k}", [128, TOK], F32) for k in range(KT)]
    C.xnh = [C.sb(f"xnh{k}", [128, TH], BF16) for k in range(KT)]
    pb = [nc.alloc_psum_tensor(f"pb{i}", [128, 512], F32) for i in range(8)]
    pb = [t.ap() for t in pb]
    build_consts(C)
    C.finish_persistent()
    for k in range(KT):
        dma(C, "sync", C.xown[k][:, :], io["xres"][k * 128:(k + 1) * 128, :],
            tag=f"ldx{k}")
    emit_l1_attn(C, io, pb)
    fins = []
    for k in range(KT):
        fins.append(dma(C, "sync", io["xoutT"][k * 128:(k + 1) * 128, :],
                        C.xown[k][:, :], tag=f"sto{k}"))
    prog.emit("sync", lambda e: e.nop(), deps=fins, tag="fin")
    prog.run(nc)
    return nc, io


def stage_l1attn(inp, x2):
    maps = []
    for c in range(8):
        b, hf = c // 2, c % 2
        xs = x2[b, hf * TOK:(hf + 1) * TOK, :]
        lo = hf * TOK - HALO
        cos, sinm = make_rope_tables(np.arange(lo, lo + TH))
        cm = np.zeros((2, KT, DH, 512), np.float32)
        for qb in range(2):
            for kc in range(KT):
                kg = kc * 128 + np.arange(128)[:, None]
                qg = hf * TOK + qb * 512 + np.arange(512)[None, :]
                cm[qb, kc] = np.where(qg >= kg, 1.0, 0.0)
        maps.append({
            "xres": np.ascontiguousarray(xs.T),
            "cosT": cos, "sinmT": sinm,
            "cmask": cm.reshape(32, DH, 512),
            "wq1": inp["L1_attn_Wq"], "wk1": inp["L1_attn_Wk"],
            "wv1": inp["L1_attn_Wv"], "wo1": inp["L1_attn_Wo"],
        })
    return maps



N1 = 16384
GELU_C = 0.7978845608028654
GELU_B = 0.044715


def alloc_dq_bufs(C):
    return {
        "pk": C.scratch("dq_pk", [128, 256], I32),
        "sc": C.scratch("dq_sc", [128, 64], F32),
        "ta": C.scratch("dq_ta", [128, 256], I32),
        "tb": C.scratch("dq_tb", [128, 256], I32),
        "td": C.scratch("dq_td", [128, 256], I32),
        "u16": C.scratch("dq_u16", [128, 256], U16),
        "wc": [C.scratch(f"dq_wc{i}", [128, 512], BF16) for i in range(2)],
        "se": C.scratch("dq_se", [128, 512], F32),
        "wt": [C.scratch(f"dq_wt{i}", [128, 128], BF16) for i in range(2)],
    }


def make_dequant_steps(C, io, pb, bufs):
    nc = C.nc
    cc_ins = {}
    store_insts = {"g1": [], "u1": [], "d1": []}
    steps = []
    pk, sc = bufs["pk"], bufs["sc"]
    ta, tb, td, u16 = bufs["ta"], bufs["tb"], bufs["td"], bufs["u16"]
    wcp, scex, wt = bufs["wc"], bufs["se"], bufs["wt"]
    cbps = pb[7].bitcast(BF16)
    for name, hs in (("g1", False), ("u1", False), ("d1", True)):
        cc_in = nc.dram_tensor(f"cc_{name}_in", [2048, 2048], BF16, kind="Internal").ap()
        cc_ins[name] = cc_in

        def compute(t, q, idx, name=name, hs=hs):
            wcode = wcp[idx % 2]
            dma(C, "scalar", pk[:, :],
                io[f"p{name}"][t * 128:(t + 1) * 128, q * 256:(q + 1) * 256],
                tag=f"dqp_{name}{t}_{q}")
            if q == 0:
                dma(C, "scalar", sc[:, :],
                    io[f"p{name}_sc"][t * 128:(t + 1) * 128, :], tag=f"dqs_{name}{t}")
            E = "vector"
            E2 = "vector"
            for half in range(2):
                if half == 0:
                    ts(C, ta[:, :], pk[:, :], 15, ALU.bitwise_and, eng=E, tag="dq_c")
                else:
                    ts(C, ta[:, :], pk[:, :], 4, ALU.logical_shift_right, 15,
                       ALU.bitwise_and, eng=E, tag="dq_c")
                ts(C, tb[:, :], ta[:, :], 7, ALU.bitwise_and, eng=E, tag="dq_m")
                ts(C, td[:, :], tb[:, :], 0x40, ALU.mult, 0x3F00, ALU.add, eng=E, tag="dq_t1")
                ts(C, tb[:, :], tb[:, :], 0x3F00, ALU.mult, eng=E, tag="dq_t2")
                tt(C, td[:, :], td[:, :], tb[:, :], ALU.min, eng=E, tag="dq_min")
                ts(C, ta[:, :], ta[:, :], 8, ALU.bitwise_and, 12, ALU.logical_shift_left,
                   eng=E, tag="dq_s")
                tt(C, td[:, :], td[:, :], ta[:, :], ALU.bitwise_or, eng=E, tag="dq_or")
                cp(C, u16[:, :], td[:, :], eng=E, tag="dq_u")
                cp(C, wcode.ap().rearrange("p (n two) -> p n two", two=2)[:, :, half],
                   u16[:].bitcast(BF16), eng=E, tag="dq_il")
            cp(C, scex.ap().rearrange("p (b e) -> p b e", e=32),
               sc[:, q * 16:(q + 1) * 16].rearrange("p (b o) -> p b o", o=1)
                   .broadcast_to([128, 16, 32]), eng=E2, tag="dq_sx")
            if hs:
                ts(C, scex[:, :], scex[:, :], 0.5, ALU.mult, eng=E2, tag="dq_hs")
            tt(C, wcode[:, :], wcode[:, :], scex[:, :], ALU.mult, eng=E2, tag="dq_sm")

        def flush(t, q, idx, name=name, cc_in=cc_in):
            wcode = wcp[idx % 2]
            for j in range(4):
                jj = q * 4 + j
                C.P.emit("tensor",
                         lambda e, o=cbps[0:128, 0:128], i=wcode[:, j * 128:(j + 1) * 128],
                                idn=C.ident_bf[:, :]: e.transpose(o, i, idn),
                         reads=[wcode, C.ident_bf], writes=[pb[7]], tag="dq_tr")
                C.P.emit("scalar",
                         lambda e, o=wt[j % 2][:, :], i=cbps[0:128, 0:128]:
                             e.copy(out=o, in_=i),
                         reads=[pb[7]], writes=[wt[j % 2]], tag="dq_te")
                store_insts[name].append(
                    dma(C, "scalar", cc_in[jj * 128:(jj + 1) * 128, t * 128:(t + 1) * 128],
                        wt[j % 2][:, :], tag="dq_st", untracked=True))

        for t in range(KT):
            for q in range(4):
                idx = len(steps)
                steps.append((
                    lambda t=t, q=q, idx=idx, compute=compute: compute(t, q, idx),
                    lambda t=t, q=q, idx=idx, flush=flush: flush(t, q, idx),
                ))
    return steps, cc_ins, store_insts


def emit_l1_gathers(C, cc_ins, store_insts):
    nc = C.nc
    groups = [[0, 1, 2, 3, 4, 5, 6, 7]]
    from concourse.replica_groups import maybe_share_collective_output_space
    space = maybe_share_collective_output_space("AllGather", groups, prefer_local=True)
    gcc = {}
    for name in ("g1", "u1", "d1"):
        cin = cc_ins[name]
        cout = nc.dram_tensor(f"cc_{name}_out", [8 * 2048, 2048], BF16,
                              kind="Internal", addr_space=space).ap()
        C.cc_last = C.P.emit("gpsimd",
                 lambda e, ci=cin, co=cout: e.collective_compute(
                     "AllGather", ALU.bypass, replica_groups=groups,
                     ins=[ci.opt()], outs=[co.opt()]),
                 reads=[cin], writes=[cout], cc=True, tag=f"ag_{name}",
                 deps=list(store_insts[name]) + ([C.cc_last] if C.cc_last is not None else []))
        gcc[name] = cout
    return gcc


def emit_l1_ffn(C, io, pb, gcc):
    nc = C.nc
    C.push_scratch()
    mpool = C.scratch("mpool3", [1, 2 * TOK], F32)
    emit_ln(C, [(C.xown, 0, 0, TOK)], C.xnh, mpool[0:1, 0:TOK],
            mpool[0:1, TOK:2 * TOK], pb, tag="ln3")
    C.pop_scratch()
    C.P.fence()
    xn = C.xnh

    C.push_scratch()
    ibuf = [C.scratch(f"ib{i}", [128, TOK], BF16) for i in range(16)]
    wgb = [C.scratch(f"f_wg{i}", [128, KT * 128], BF16) for i in range(2)]
    wub = [C.scratch(f"f_wu{i}", [128, KT * 128], BF16) for i in range(2)]
    wdb = [C.scratch(f"f_wd{i}", [128, 8 * 128], BF16) for i in range(2)]
    tmp1 = C.scratch("f_t1", [128, 512], F32)
    tmp2 = C.scratch("f_t2", [128, 512], F32)

    def load_gu(buf, cout, rank, ncol, tg):
        return dma(C, "sync",
                   buf.ap().rearrange("p (kt n) -> p kt n", kt=KT),
                   cout[rank * 2048:(rank + 1) * 2048, ncol * 128:(ncol + 1) * 128]
                       .rearrange("(kt p) n -> p kt n", p=128), tag=tg)

    def f_gu(g):
        rank = g // 2
        ib = ibuf[(g % 2) * 8:(g % 2) * 8 + 8]
        for nch in range(8):
            ncol = (g % 2) * 8 + nch
            load_gu(wgb[nch % 2], gcc["g1"], rank, ncol, f"lg{g}_{nch}")
            load_gu(wub[nch % 2], gcc["u1"], rank, ncol, f"lu{g}_{nch}")
            for c in range(2):
                gps = pb[0 + (nch % 2)][:, 0:512]
                ups = pb[2 + (nch % 2)][:, 0:512]
                for k in range(KT):
                    mm(C, gps, wgb[nch % 2].ap().rearrange("p (kt n) -> p kt n", kt=KT)[:, k, :],
                       xn[k][:, c * 512:(c + 1) * 512], k == 0, k == KT - 1, tag=f"fg{g}")
                for k in range(KT):
                    mm(C, ups, wub[nch % 2].ap().rearrange("p (kt n) -> p kt n", kt=KT)[:, k, :],
                       xn[k][:, c * 512:(c + 1) * 512], k == 0, k == KT - 1, tag=f"fu{g}")
                sl = ib[nch][:, c * 512:(c + 1) * 512]
                act(C, tmp1[:, :], gps, AF.Square, tag=f"fsq{g}")
                ts(C, tmp1[:, :], tmp1[:, :], GELU_C * GELU_B, ALU.mult, GELU_C,
                   ALU.add, tag=f"fu1{g}")
                tt(C, tmp2[:, :], tmp1[:, :], gps, ALU.mult, tag=f"fu2{g}")
                act(C, tmp2[:, :], tmp2[:, :], AF.Tanh, tag=f"fth{g}")
                ts(C, tmp2[:, :], tmp2[:, :], 1.0, ALU.add, tag=f"fv1{g}")
                tt(C, tmp2[:, :], tmp2[:, :], gps, ALU.mult, tag=f"fv2{g}")
                tt(C, sl, tmp2[:, :], ups, ALU.mult, tag=f"fiv{g}")

    def f_down(g):
        ib = ibuf[(g % 2) * 8:(g % 2) * 8 + 8]
        for oc in range(KT):
            dma(C, "sync",
                wdb[oc % 2].ap().rearrange("p (kt n) -> p kt n", kt=8),
                gcc["d1"][g * 1024:(g + 1) * 1024, oc * 128:(oc + 1) * 128]
                    .rearrange("(kt p) n -> p kt n", p=128), tag=f"ld{g}_{oc}")
            for c in range(2):
                ps = pb[4 + (oc % 2)][:, 0:512]
                for nch in range(8):
                    mm(C, ps, wdb[oc % 2].ap().rearrange("p (kt n) -> p kt n", kt=8)[:, nch, :],
                       ib[nch][:, c * 512:(c + 1) * 512], nch == 0, nch == 7,
                       tag=f"fd{g}")
                tt(C, C.xown[oc][:, c * 512:(c + 1) * 512],
                   C.xown[oc][:, c * 512:(c + 1) * 512], ps, ALU.add, tag=f"fr{g}_{oc}")

    f_gu(0)
    for g in range(16):
        if g + 1 < 16:
            f_gu(g + 1)
        f_down(g)
    C.pop_scratch()
    C.P.fence()


def build_l1ffn_program():
    nc = bass.Bass("TRN2", target_bir_lowering=False, num_devices=8)
    io = {}
    def inp(name, shape, dt=F32):
        io[name] = nc.dram_tensor(name, shape, dt, kind="ExternalInput").ap()
    inp("xres", [D, TOK])
    for n in ("g1", "u1", "d1"):
        inp(f"p{n}", [2048, 1024], I32)
        inp(f"p{n}_sc", [2048, 64])
    io["xoutT"] = nc.dram_tensor("xoutT", [D, TOK], F32, kind="ExternalOutput").ap()
    prog = Prog()
    C = Ctx(nc, prog, scratch_bytes=95 * 1024)
    C.xown = [C.sb(f"xown{k}", [128, TOK], F32) for k in range(KT)]
    C.xnh = [C.sb(f"xnh{k}", [128, TH], BF16) for k in range(KT)]
    pb = [nc.alloc_psum_tensor(f"pb{i}", [128, 512], F32) for i in range(8)]
    pb = [t.ap() for t in pb]
    build_consts(C, ident=True)
    C.finish_persistent()
    for k in range(KT):
        dma(C, "sync", C.xown[k][:, :], io["xres"][k * 128:(k + 1) * 128, :],
            tag=f"ldx{k}")
    C.push_scratch()
    dqb = alloc_dq_bufs(C)
    steps, cc_ins, st_ins = make_dequant_steps(C, io, pb, dqb)
    for (cfn, ffn) in steps:
        cfn()
        ffn()
    gcc = emit_l1_gathers(C, cc_ins, st_ins)
    emit_l1_ffn(C, io, pb, gcc)
    C.pop_scratch()
    fins = []
    for k in range(KT):
        fins.append(dma(C, "sync", io["xoutT"][k * 128:(k + 1) * 128, :],
                        C.xown[k][:, :], tag=f"sto{k}"))
    prog.emit("sync", lambda e: e.nop(), deps=fins, tag="fin")
    prog.run(nc)
    return nc, io


def stage_l1ffn(inp, x3):
    maps = []
    for c in range(8):
        b, hf = c // 2, c % 2
        xs = x3[b, hf * TOK:(hf + 1) * TOK, :]
        maps.append({
            "xres": np.ascontiguousarray(xs.T),
            "pg1": np.ascontiguousarray(inp["L1_ffn_gate_W_packed"][c * 2048:(c + 1) * 2048]),
            "pg1_sc": np.ascontiguousarray(inp["L1_ffn_gate_scales"][c * 2048:(c + 1) * 2048]),
            "pu1": np.ascontiguousarray(inp["L1_ffn_up_W_packed"][c * 2048:(c + 1) * 2048]),
            "pu1_sc": np.ascontiguousarray(inp["L1_ffn_up_scales"][c * 2048:(c + 1) * 2048]),
            "pd1": np.ascontiguousarray(inp["L1_ffn_down_W_packed"][:, c * 1024:(c + 1) * 1024]),
            "pd1_sc": np.ascontiguousarray(inp["L1_ffn_down_scales"][:, c * 64:(c + 1) * 64]),
        })
    return maps



def build_fused_program():
    nc = bass.Bass("TRN2", target_bir_lowering=False, num_devices=8)
    io = {}
    def inp(name, shape, dt=F32):
        io[name] = nc.dram_tensor(name, shape, dt, kind="ExternalInput").ap()
    inp("xinT", [D, TH])
    inp("cosT", [DH, TH]); inp("sinmT", [DH, TH])
    inp("bmask", [6, DH, 512])
    inp("temp", [128, 16]); inp("hvalid", [DH, 2])
    inp("wqT", [D, D]); inp("wkvT", [D, 1024]); inp("woT", [D, D])
    inp("gate_wT", [D, 8]); inp("sel8", [8, 8 * 128]); inp("shg", [128, KT])
    inp("sh_wg", [2048, MH]); inp("sh_wu", [2048, MH]); inp("sh_wd", [1408, 2048])
    inp("sh_swg", [256, MHS]); inp("sh_swu", [256, MHS]); inp("sh_swd", [352, 2048])
    inp("cmask", [32, DH, 512])
    inp("wq1", [D, D]); inp("wk1", [D, 512]); inp("wv1", [D, 512]); inp("wo1", [D, D])
    for n in ("g1", "u1", "d1"):
        inp(f"p{n}", [2048, 1024], I32)
        inp(f"p{n}_sc", [2048, 64])
    io["xoutT"] = nc.dram_tensor("xoutT", [D, TOK], F32, kind="ExternalOutput").ap()
    io["fdbg"] = nc.dram_tensor("fdbg", [4 * 128, TOK], F32, kind="ExternalOutput").ap()

    prog = Prog()
    C = Ctx(nc, prog, scratch_bytes=101 * 1024)
    C.xown = [C.sb(f"xown{k}", [128, TOK], F32) for k in range(KT)]
    C.xnh = [C.sb(f"xnh{k}", [128, TH], BF16) for k in range(KT)]
    pb = [nc.alloc_psum_tensor(f"pb{i}", [128, 512], F32) for i in range(8)]
    pb = [t.ap() for t in pb]
    build_consts(C, ident=True)
    C.finish_persistent()
    load_xin(C, io)
    gath = emit_moe_gather(C, io)
    emit_l0_attn(C, io, pb)
    C.push_scratch()
    dqb = alloc_dq_bufs(C)
    steps, cc_ins, st_ins = make_dequant_steps(C, io, pb, dqb)
    sit = iter(steps)
    pend = [None]
    def bg():
        if pend[0] is not None:
            pend[0]()
            pend[0] = None
        nxt = next(sit, None)
        if nxt is not None:
            nxt[0]()
            pend[0] = nxt[1]
    dma(C, "sync", io["fdbg"][0:128, :], C.xown[0][:, :], tag="fdbg0")
    emit_l0_moe(C, io, pb, gath, bg=bg)
    if pend[0] is not None:
        pend[0]()
        pend[0] = None
    for (cfn, ffn) in sit:
        cfn()
        ffn()
    dma(C, "sync", io["fdbg"][128:256, :], C.xown[0][:, :], tag="fdbg1")
    gcc = emit_l1_gathers(C, cc_ins, st_ins)
    dma(C, "gpsimd", io["fdbg"][384:512, :], gcc["g1"][0:128, 0:TOK], tag="fdbg3")
    emit_l1_attn(C, io, pb)
    dma(C, "sync", io["fdbg"][256:384, :], C.xown[0][:, :], tag="fdbg2")
    emit_l1_ffn(C, io, pb, gcc)
    C.pop_scratch()
    fins = []
    for k in range(KT):
        fins.append(dma(C, "sync", io["xoutT"][k * 128:(k + 1) * 128, :],
                        C.xown[k][:, :], tag=f"sto{k}"))
    prog.emit("sync", lambda e: e.nop(), deps=fins, tag="fin")
    prog.run(nc)
    return nc, io


def stage_fused(inp):
    mapsA = stage_l0attn(inp)
    g = inp["L0_ffn_gate_w"]
    shg = np.ascontiguousarray(inp["L0_ffn_shgate_w"].reshape(KT, 128).T).astype(np.float32)
    wg = np.ascontiguousarray(inp["L0_ffn_w_gate"].reshape(8 * 2048, MH))
    wu = np.ascontiguousarray(inp["L0_ffn_w_up"].reshape(8 * 2048, MH))
    wd = np.ascontiguousarray(inp["L0_ffn_w_down"].reshape(8 * 1408, 2048))
    swg = inp["L0_ffn_shw_gate"]; swu = inp["L0_ffn_shw_up"]; swd = inp["L0_ffn_shw_down"]
    gwT = np.ascontiguousarray(g.T)
    sel8 = np.zeros((8, 8 * 128), np.float32)
    for e in range(8):
        sel8[e, e * 128:(e + 1) * 128] = 1.0
    maps = []
    for c in range(8):
        b, hf = c // 2, c % 2
        m = dict(mapsA[c])
        m.update({
            "sel8": sel8, "gate_wT": gwT, "shg": shg,
            "sh_wg": np.ascontiguousarray(wg[c * 2048:(c + 1) * 2048]),
            "sh_wu": np.ascontiguousarray(wu[c * 2048:(c + 1) * 2048]),
            "sh_wd": np.ascontiguousarray(wd[c * 1408:(c + 1) * 1408]),
            "sh_swg": np.ascontiguousarray(swg[c * 256:(c + 1) * 256]),
            "sh_swu": np.ascontiguousarray(swu[c * 256:(c + 1) * 256]),
            "sh_swd": np.ascontiguousarray(swd[c * 352:(c + 1) * 352]),
            "wq1": inp["L1_attn_Wq"], "wk1": inp["L1_attn_Wk"],
            "wv1": inp["L1_attn_Wv"], "wo1": inp["L1_attn_Wo"],
            "pg1": np.ascontiguousarray(inp["L1_ffn_gate_W_packed"][c * 2048:(c + 1) * 2048]),
            "pg1_sc": np.ascontiguousarray(inp["L1_ffn_gate_scales"][c * 2048:(c + 1) * 2048]),
            "pu1": np.ascontiguousarray(inp["L1_ffn_up_W_packed"][c * 2048:(c + 1) * 2048]),
            "pu1_sc": np.ascontiguousarray(inp["L1_ffn_up_scales"][c * 2048:(c + 1) * 2048]),
            "pd1": np.ascontiguousarray(inp["L1_ffn_down_W_packed"][:, c * 1024:(c + 1) * 1024]),
            "pd1_sc": np.ascontiguousarray(inp["L1_ffn_down_scales"][:, c * 64:(c + 1) * 64]),
        })
        cm = np.zeros((2, KT, DH, 512), np.float32)
        for qb in range(2):
            for kc in range(KT):
                kg = kc * 128 + np.arange(128)[:, None]
                qg = hf * TOK + qb * 512 + np.arange(512)[None, :]
                cm[qb, kc] = np.where(qg >= kg, 1.0, 0.0)
        m["cmask"] = cm.reshape(32, DH, 512)
        maps.append(m)
    return maps



_CACHE = {}


def kernel(**inputs):
    inp = {k: np.asarray(v) for k, v in inputs.items()}
    if "p" not in _CACHE:
        _CACHE["p"] = build_fused_program()[0]
    nc = _CACHE["p"]
    maps = stage_fused(inp)
    res = run_bass_kernel_spmd(nc, maps, core_ids=list(range(8)))
    out = np.zeros((4, 2048, 2048), np.float32)
    for c in range(8):
        b, hf = c // 2, c % 2
        out[b, hf * TOK:(hf + 1) * TOK, :] = np.asarray(res.results[c]["xoutT"]).T
    return out
